# version 11
# baseline (speedup 1.0000x reference)
import numpy as np
import ml_dtypes
import concourse.bass as bass
import concourse.mybir as mybir
from contextlib import ExitStack

F32 = mybir.dt.float32
BF16 = mybir.dt.bfloat16
I32 = mybir.dt.int32
U32 = mybir.dt.uint32
AF = mybir.ActivationFunctionType
ALU = mybir.AluOpType
AX = mybir.AxisListType

ENGINES = ("sync", "gpsimd", "vector", "scalar", "tensor")

D = 1024
L = 2048
NLT = L // 128
NDC = D // 128
H_A = 4
DK = 256
MACRO = 128
NMAC = L // MACRO


class Sem:
    __slots__ = ("h", "n")

    def __init__(self, h):
        self.h = h
        self.n = 0

    def bump(self, k):
        self.n += k
        return self.n


class B:

    def __init__(self, nc, stack):
        self.nc = nc
        self.stack = stack
        self.qs = {n: [] for n in ENGINES}
        self._semid = 0

    def sem(self, name=None):
        self._semid += 1
        h = self.stack.enter_context(self.nc.semaphore(name or f"s{self._semid}"))
        return Sem(h)

    def op(self, eng, fn):
        self.qs[eng].append(fn)

    def run(self, eng, meth, *args, sem=None, inc=1, lowprec=False, **kw):
        if sem is not None:
            sem.bump(inc)
        nc = self.nc

        def fn(E, meth=meth, args=args, kw=kw, sem=sem, inc=inc, lowprec=lowprec):
            try:
                if lowprec:
                    with nc.allow_low_precision(reason="bf16 PE transpose"):
                        ins = getattr(E, meth)(*args, **kw)
                else:
                    ins = getattr(E, meth)(*args, **kw)
            except Exception as ex:
                raise RuntimeError(f"emit fail {meth} args={[getattr(a, 'tensor', None) and a.tensor.name for a in args]} kw={list(kw)}: {ex}") from ex
            if sem is not None:
                ins.then_inc(sem.h, inc)
        self.op(eng, fn)

    def dma(self, eng, out, in_, sem, **kw):
        sem.bump(16)

        def fn(E, out=out, in_=in_, kw=kw, sem=sem):
            E.dma_start(out=out, in_=in_, **kw).then_inc(sem.h, 16)
        self.op(eng, fn)

    def wait(self, eng, sem, val=None):
        v = sem.n if val is None else val
        self.op(eng, lambda E, v=v, sem=sem: E.wait_ge(sem.h, v))

    def finalize(self):
        blk = self.stack.enter_context(self.nc.Block())
        for name in ENGINES:
            q = self.qs[name]
            if q:
                getattr(blk, name)(lambda E, q=q: [f(E) for f in q])


def host_consts():
    ii = np.arange(MACRO)
    blk = ii // 32
    c = {}
    c["ident"] = np.eye(128, dtype=np.float32)
    c["neg_ident"] = (-np.eye(128)).astype(np.float32)
    c["ones_row"] = np.ones((1, 128), np.float32)
    c["mask_neg_bsl_bd"] = np.where((blk[:, None] == blk[None, :]) & (ii[:, None] > ii[None, :]), -1.0, 0.0).astype(np.float32)
    c["mask_neg_bsu_bd"] = np.ascontiguousarray(c["mask_neg_bsl_bd"].T)
    c["mask_neg_bsl_blk"] = np.where(blk[:, None] > blk[None, :], -1.0, 0.0).astype(np.float32)
    c["mask_neg_bsu_blk"] = np.ascontiguousarray(c["mask_neg_bsl_blk"].T)
    c["mask_triu_inc"] = (ii[:, None] <= ii[None, :]).astype(np.float32)
    hsel = np.zeros((128, 16), np.float32)
    for h in range(4):
        hsel[:, 4 * h + h] = 1.0
    c["hsel"] = hsel
    return c


def declare_consts(nc):
    t = {}
    for name, shape in [
        ("ident", [128, 128]), ("neg_ident", [128, 128]), ("ones_row", [1, 128]),
        ("mask_neg_bsl_bd", [128, 128]), ("mask_neg_bsu_bd", [128, 128]),
        ("mask_neg_bsl_blk", [128, 128]), ("mask_neg_bsu_blk", [128, 128]),
        ("mask_triu_inc", [128, 128]), ("hsel", [128, 16]),
    ]:
        t[name] = nc.dram_tensor("c_" + name, shape, F32, kind="ExternalInput").ap()
    return t


LQ = 512
NQT = LQ // 128
NQM = LQ // MACRO
NQUART = L // LQ


class AttnACore:

    def __init__(self, nc, b, B_loc):
        self.nc = nc
        self.b = b
        self.B_loc = B_loc
        s = self.s = {}
        e = b.stack.enter_context
        s["poolQK"] = e(nc.sbuf_tensor("poolQK", [128, 2 * NDC * LQ], F32))
        s["qT"] = s["poolQK"][:, 0:NDC * LQ]
        s["knT"] = s["poolQK"][:, NDC * LQ:2 * NDC * LQ]
        s["wo"] = s["poolQK"][:, 0:NDC * D]
        s["xlnT"] = e(nc.sbuf_tensor("xlnT", [128, NDC * LQ], F32))
        s["vpos"] = e(nc.sbuf_tensor("vpos", [128, NQM * D], F32))
        s["knpos"] = e(nc.sbuf_tensor("knpos", [128, NQM * D], F32))
        s["S_l"] = e(nc.sbuf_tensor("S_l", [128, H_A * 512], F32))
        s["S_g"] = e(nc.sbuf_tensor("S_g", [128, H_A * 512], F32))
        s["wring"] = e(nc.sbuf_tensor("wring", [128, 2 * 1024], F32))
        s["wcat"] = e(nc.sbuf_tensor("wcat", [128, NDC * 68], F32))
        s["conv"] = e(nc.sbuf_tensor("conv", [128, 3 * NDC * 4], F32))
        s["gbias"] = e(nc.sbuf_tensor("gbias", [128, 1], F32))
        s["ccar"] = e(nc.sbuf_tensor("ccar", [128, 3 * NDC * 3], F32))
        s["ident_f"] = e(nc.sbuf_tensor("identf", [128, 128], F32))
        s["neg_ident"] = e(nc.sbuf_tensor("nidentf", [128, 128], F32))
        s["ones_row"] = e(nc.sbuf_tensor("onesr", [1, 128], F32))
        s["ones_col"] = e(nc.sbuf_tensor("onesc", [128, 1], F32))
        for m in ("mask_neg_bsl_bd", "mask_neg_bsu_bd", "mask_neg_bsl_blk",
                  "mask_neg_bsu_blk", "mask_triu_inc"):
            s[m] = e(nc.sbuf_tensor(m, [128, 128], F32))
        s["hsel"] = e(nc.sbuf_tensor("hsel", [128, 16], F32))
        s["xrow"] = e(nc.sbuf_tensor("xrow", [128, 2 * 1024], F32))
        s["lnout"] = e(nc.sbuf_tensor("lnout", [128, 2 * 1024], F32))
        s["x1_sb"] = e(nc.sbuf_tensor("x1_sb", [128, 2 * 1024], F32))
        s["xsq"] = e(nc.sbuf_tensor("xsq", [128, 2 * 1024], F32))
        s["stat"] = e(nc.sbuf_tensor("stat", [128, 32], F32))
        s["o3T_t"] = e(nc.sbuf_tensor("o3T_t", [128, NDC * 128], F32))
        s["raw"] = e(nc.sbuf_tensor("raw", [128, 2 * 520], F32))
        s["acc0"] = e(nc.sbuf_tensor("acc0", [128, 512], F32))
        s["acc1"] = e(nc.sbuf_tensor("acc1", [128, 512], F32))
        s["scr512"] = e(nc.sbuf_tensor("scr512", [128, 512], F32))
        s["gates"] = e(nc.sbuf_tensor("gates", [128, LQ], F32))
        s["possrc"] = e(nc.sbuf_tensor("possrc", [128, LQ], F32))
        s["arena_a"] = e(nc.sbuf_tensor("arena_a", [128, LQ], F32))
        s["rows4"] = e(nc.sbuf_tensor("rows4", [64, LQ], F32))
        s["nscr"] = e(nc.sbuf_tensor("nscr", [64, 2 * LQ], F32))
        s["pospan"] = e(nc.sbuf_tensor("pospan", [128, NQM * 96], F32))
        s["eps"] = e(nc.sbuf_tensor("eps", [128, 2], F32))
        for nm in ("Fn", "AnT", "Te_l", "Te_g"):
            s[nm] = e(nc.sbuf_tensor(nm, [128, 128], F32))
        for nm in ("M", "MT", "N1", "N2", "N3",
                   "NT1", "NT2", "NT3", "N4", "Ra", "Rb", "Tsym_nT", "Q1pp",
                   "negLb", "negLbT", "Rt0", "Lb2", "TbTp"):
            for sfx in ("_0", "_1"):
                s[nm + sfx] = e(nc.sbuf_tensor(nm + sfx, [128, 128], F32))
        for sfx in ("_0", "_1"):
            s["Y_sb" + sfx] = e(nc.sbuf_tensor("Y_sb" + sfx, [128, 256], F32))
            s["U_sb" + sfx] = e(nc.sbuf_tensor("U_sb" + sfx, [128, 256], F32))
        s["o2"] = e(nc.sbuf_tensor("o2", [128, 256], F32))
        s["osc"] = e(nc.sbuf_tensor("osc", [128, 260], F32))
        s["normw_row"] = e(nc.sbuf_tensor("normw_row", [1, 256], F32))
        s["normw"] = e(nc.sbuf_tensor("normw", [128, 256], F32))
        self.ek = ExpKit(nc, b.stack, "a", 512)
        p = self.p = {}
        p["proj0"] = e(nc.psum_tensor("ps_proj0", [128, 512], F32))
        p["proj1"] = e(nc.psum_tensor("ps_proj1", [128, 512], F32))
        p["stat"] = e(nc.psum_tensor("ps_stat", [128, 512], F32))
        p["bc"] = e(nc.psum_tensor("ps_bc", [128, 512], F32))
        p["tr"] = p["stat"]
        p["tr2"] = p["bc"]
        self.ptr = e(nc.psum_tensor("ps_ptr", [128, 128], F32))
        self.pyu = e(nc.psum_tensor("ps_yu", [128, 512], F32))
        self.pS = {0: e(nc.psum_tensor("ps_S_l", [128, 512], F32)),
                   1: e(nc.psum_tensor("ps_S_g", [128, 512], F32))}
        self.ldw = b.sem("ldw")
        self.ldwh = b.sem("ldwh")
        self.ldx = b.sem("ldx")
        self.ldx2 = b.sem("ldx2")
        self.st_out = b.sem("st_out")
        self.st_out2 = b.sem("st_out2")
        self.pe = b.sem("pe")
        self.dv = b.sem("dv")
        self.dv2 = b.sem("dv2")
        self.ac = b.sem("ac")
        self.pe_c = {0: b.sem("pe_a"), 1: b.sem("pe_b")}
        self.dv_c = {0: b.sem("dv_a"), 1: b.sem("dv_b")}
        self.ac_c = {0: b.sem("ac_a"), 1: b.sem("ac_b")}
        self.dv_e = self.dv
        self.ac_e = self.ac

    def load_consts(self, cds):
        b, s = self.b, self.s
        b.run("vector", "memset", s["possrc"][:, :], 0.0, sem=self.dv)
        for m in ("mask_neg_bsl_bd", "mask_neg_bsu_bd", "mask_neg_bsl_blk",
                  "mask_neg_bsu_blk", "mask_triu_inc"):
            b.dma("sync", s[m][:, :], cds[m][:, :], self.ldwh)
        b.dma("sync", s["ident_f"][:, :], cds["ident"][:, :], self.ldwh)
        b.dma("sync", s["neg_ident"][:, :], cds["neg_ident"][:, :], self.ldwh)
        b.dma("sync", s["ones_row"][:, :], cds["ones_row"][:, :], self.ldwh)
        b.dma("sync", s["ones_col"][:, :], cds["ones_row"].rearrange("o p -> p o"), self.ldwh)
        b.dma("sync", s["hsel"][:, :], cds["hsel"][:, :], self.ldwh)
        b.run("vector", "memset", s["eps"][:, 0:1], 1e-5)
        b.run("vector", "memset", s["eps"][:, 1:2], 1e-12, sem=self.dv)

    def load_weights(self, w):
        b, s = self.b, self.s
        b.dma("sync", s["wcat"][:, :].rearrange("p (kc c) -> p kc c", kc=NDC),
              w["Wcat"].rearrange("(kc p) c -> p kc c", p=128), self.ldwh)
        b.dma("sync", s["conv"][:, :].rearrange("p (proj dc j) -> p proj dc j", proj=3, dc=NDC),
              w["convcat"].rearrange("(dc p) (proj j) -> p proj dc j", p=128, j=4), self.ldwh)
        b.dma("sync", s["gbias"][:, :], w["gbias"][:, :], self.ldwh)

    def normw_setup(self, w_normw):
        b, s, p = self.b, self.s, self.p
        b.dma("sync", s["normw_row"][:, :], w_normw[:, :], self.ldwh)
        b.wait("tensor", self.ldwh)
        b.run("tensor", "matmul", p["bc"][:, 0:256], s["ones_row"][0:1, :],
              s["normw_row"][0:1, :], start=True, stop=True, sem=self.pe)
        b.wait("vector", self.pe)
        b.run("vector", "tensor_copy", out=s["normw"][:, :], in_=p["bc"][:, 0:256], sem=self.dv)

    def ln_pass(self, x_dram, qt):
        b, s = self.b, self.s
        for lt in range(NQT):
            par = lt % 2
            hb = par * 1024
            o = par * 1024
            st = s["stat"][:, par * 16:(par + 1) * 16]
            b.wait("sync", self.dv)
            b.wait("sync", self.ac)
            ldxp = self.ldx if par == 0 else self.ldx2
            b.dma("sync", s["xrow"][:, hb:hb + 1024],
                  x_dram[qt * LQ + lt * 128: qt * LQ + (lt + 1) * 128, :], ldxp)
            xf = s["xrow"][:, hb:hb + 1024]
            b.wait("vector", ldxp)
            b.run("vector", "tensor_scalar", sem=self.dv,
                  out=s["xsq"][:, o:o + 1024], in0=xf, scalar1=1.0, scalar2=0.0,
                  op0=ALU.mult, op1=ALU.add, accum_out=st[:, 0:1])
            b.wait("scalar", ldxp)
            b.wait("scalar", self.dv)
            b.run("scalar", "activation", sem=self.ac,
                  out=s["xsq"][:, o:o + 1024], in_=xf, func=AF.Square, accum_out=st[:, 1:2])
            b.wait("vector", self.ac)
            b.run("vector", "tensor_scalar", out=st[:, 2:3], in0=st[:, 0:1],
                  scalar1=1.0 / D, scalar2=None, op0=ALU.mult, sem=self.dv)
            b.run("vector", "tensor_scalar", out=st[:, 12:13], in0=st[:, 1:2],
                  scalar1=1.0 / D, scalar2=1e-5, op0=ALU.mult, op1=ALU.add, sem=self.dv)
            b.wait("vector", self.dv)
            b.run("vector", "tensor_tensor", out=st[:, 4:5], in0=st[:, 2:3],
                  in1=st[:, 2:3], op=ALU.mult, sem=self.dv)
            b.wait("vector", self.dv)
            b.run("vector", "tensor_sub", out=st[:, 12:13], in0=st[:, 12:13],
                  in1=st[:, 4:5], sem=self.dv)
            b.wait("scalar", self.dv)
            b.run("scalar", "activation", out=st[:, 6:7], in_=st[:, 12:13],
                  func=AF.Sqrt, sem=self.ac)
            b.wait("vector", self.ac)
            b.run("vector", "reciprocal", out=st[:, 7:8], in_=st[:, 6:7], sem=self.dv)
            b.wait("vector", self.dv)
            emit_newton_rstd(b, self.dv, st[:, 7:8], st[:, 12:13],
                             st[:, 13:14], st[:, 14:15])
            b.run("vector", "tensor_scalar", out=st[:, 15:16], in0=st[:, 2:3],
                  scalar1=-1.0, scalar2=None, op0=ALU.mult, sem=self.dv)
            b.wait("vector", self.dv)
            b.wait("vector", self.pe)
            b.wait("vector", self.st_out)
            b.wait("vector", self.st_out2)
            b.run("vector", "tensor_scalar", sem=self.dv,
                  out=s["lnout"][:, o:o + 1024], in0=xf,
                  scalar1=st[:, 15:16], scalar2=st[:, 7:8],
                  op0=ALU.add, op1=ALU.mult)
            b.wait("tensor", self.dv)
            for g in range(2):
                srcs = [s["lnout"][:, o + (g * 4 + i) * 128: o + (g * 4 + i + 1) * 128]
                        for i in range(4)]
                dst = s["xlnT"][:, :].rearrange("p (kc l) -> p kc l", kc=NDC)[
                    :, g * 4:(g + 1) * 4, lt * 128:(lt + 1) * 128]
                emit_tgroup(self, srcs, dst)

    def projections(self, wdram, qt):
        b, s, p = self.b, self.s, self.p
        b.wait("tensor", self.ldwh)
        b.wait("vector", self.pe)
        rawcopy_ac = {}
        taps_dv = {}
        sig_ac = {}
        nblk = 0
        for pi, (proj, wnm) in enumerate((("q", "Wq"), ("k", "Wk"), ("v", "Wv"))):
            for oc in range(NDC):
                k = nblk
                half = k % 2
                wr = s["wring"][:, half * 1024:(half + 1) * 1024]
                if k >= 2:
                    b.wait("gpsimd", self.ac, rawcopy_ac[k - 2])
                b.wait("gpsimd", self.ldw)
                b.dma("gpsimd", wr.rearrange("p (kc c) -> p kc c", kc=NDC),
                      wdram[wnm][:, oc * 128:(oc + 1) * 128].rearrange("(kc p) c -> p kc c", p=128),
                      self.ldw)
                b.wait("tensor", self.ldw)
                nblk += 1
                ps = p["proj0"] if k % 2 == 0 else p["proj1"]
                if k >= 2:
                    b.wait("tensor", self.ac, rawcopy_ac[k - 2])
                if k == 0:
                    b.wait("tensor", self.dv)
                    b.wait("tensor", self.dv2)
                for kc in range(NDC):
                    b.run("tensor", "matmul", ps[:, :],
                          wr[:, kc * 128:(kc + 1) * 128],
                          s["xlnT"][:, kc * LQ: (kc + 1) * LQ],
                          start=(kc == 0), stop=(kc == NDC - 1),
                          sem=(self.pe if kc == NDC - 1 else None))
                raw = s["raw"][:, (k % 2) * 520:(k % 2) * 520 + 520]
                cslot = s["ccar"][:, (pi * NDC + oc) * 3: (pi * NDC + oc) * 3 + 3]
                b.wait("scalar", self.pe)
                if k >= 2:
                    b.wait("scalar", self.dv, taps_dv[k - 2])
                if qt == 0:
                    b.run("scalar", "memzero", raw[:, 0:4], sem=self.ac)
                else:
                    b.run("scalar", "copy", out=raw[:, 1:4], in_=cslot, sem=self.ac)
                b.wait("scalar", self.ac)
                b.run("scalar", "copy", out=raw[:, 4:516], in_=ps[:, :], sem=self.ac)
                b.wait("scalar", self.ac)
                b.run("scalar", "copy", out=cslot, in_=raw[:, 513:516], sem=self.ac)
                rawcopy_ac[k] = self.ac.n
                b.wait("vector", self.ac, rawcopy_ac[k])
                cc = lambda j: s["conv"][:, (pi * NDC + oc) * 4 + j: (pi * NDC + oc) * 4 + j + 1]
                b.run("vector", "tensor_scalar", out=s["acc0"][:, :],
                      in0=raw[:, 1:513], scalar1=cc(0), scalar2=None, op0=ALU.mult,
                      sem=self.dv)
                b.wait("vector", self.dv)
                b.run("vector", "scalar_tensor_tensor", out=s["acc1"][:, :],
                      in0=raw[:, 2:514], scalar=cc(1), in1=s["acc0"][:, :],
                      op0=ALU.mult, op1=ALU.add, sem=self.dv)
                b.wait("vector", self.dv)
                b.run("vector", "scalar_tensor_tensor", out=s["acc0"][:, :],
                      in0=raw[:, 3:515], scalar=cc(2), in1=s["acc1"][:, :],
                      op0=ALU.mult, op1=ALU.add, sem=self.dv)
                b.wait("vector", self.dv)
                b.run("vector", "scalar_tensor_tensor", out=s["acc1"][:, :],
                      in0=raw[:, 4:516], scalar=cc(3), in1=s["acc0"][:, :],
                      op0=ALU.mult, op1=ALU.add, sem=self.dv)
                taps_dv[k] = self.dv.n
                b.wait("scalar", self.dv, taps_dv[k])
                b.run("scalar", "activation", out=s["scr512"][:, :], in_=s["acc1"][:, :],
                      func=AF.Sigmoid, sem=self.ac)
                sig_ac[k] = self.ac.n
                b.wait("vector", self.ac, sig_ac[k])
                if proj == "q":
                    dst = s["qT"][:, oc * LQ:(oc + 1) * LQ]
                    b.run("vector", "tensor_tensor", out=dst, in0=s["acc1"][:, :],
                          in1=s["scr512"][:, :], op=ALU.mult, sem=self.dv)
                elif proj == "k":
                    dst = s["knT"][:, oc * LQ:(oc + 1) * LQ]
                    b.run("vector", "tensor_tensor", out=dst, in0=s["acc1"][:, :],
                          in1=s["scr512"][:, :], op=ALU.mult, sem=self.dv)
                else:
                    b.run("vector", "tensor_tensor", out=s["scr512"][:, :],
                          in0=s["acc1"][:, :], in1=s["scr512"][:, :], op=ALU.mult,
                          sem=self.dv)
                    b.wait("tensor", self.dv)
                    for g in range(1):
                        srcs = [s["scr512"][:, i * 128:(i + 1) * 128] for i in range(4)]
                        dst3 = s["vpos"][:, :].rearrange("p (mm d) -> p mm d", mm=NQM)[
                            :, :, oc * 128:(oc + 1) * 128]
                        emit_tgroup(self, srcs, dst3)

    def gates_mm(self, qt):
        b, s, p = self.b, self.s, self.p
        b.wait("tensor", self.ac)
        b.wait("tensor", self.dv)
        b.wait("tensor", self.dv2)
        for kc in range(NDC):
            b.run("tensor", "matmul", p["stat"][0:68, :],
                  s["wcat"][:, kc * 68:(kc + 1) * 68],
                  s["xlnT"][:, kc * LQ:(kc + 1) * LQ],
                  start=(kc == 0), stop=(kc == NDC - 1),
                  sem=(self.pe if kc == NDC - 1 else None))
        b.wait("vector", self.pe)
        b.run("vector", "tensor_scalar", out=s["gates"][0:68, :],
              in0=p["stat"][0:68, :], scalar1=s["gbias"][0:68, 0:1], scalar2=None,
              op0=ALU.add, sem=self.dv)

    def norm_pass(self, qt):
        b, s, p = self.b, self.s, self.p
        b.wait("scalar", self.dv)
        b.wait("tensor", self.dv)
        for t, base in (("qT", 0), ("knT", 32)):
            for oc in range(NDC):
                b.wait("scalar", self.pe)
                b.run("scalar", "activation", out=s["scr512"][:, :],
                      in_=s[t][:, oc * LQ:(oc + 1) * LQ],
                      func=AF.Square, sem=self.ac)
                b.wait("tensor", self.ac)
                if oc == 0:
                    b.wait("tensor", self.dv)
                    b.wait("tensor", self.dv2)
                h4 = 4 * (oc // 2)
                b.run("tensor", "matmul", p["stat"][0:4, :],
                      s["hsel"][:, h4:h4 + 4], s["scr512"][:, :],
                      start=(oc == 0), stop=(oc == NDC - 1), sem=self.pe)
            b.wait("vector", self.pe)
            b.run("vector", "tensor_copy",
                  out=s["rows4"][base:base + 4, :],
                  in_=p["stat"][0:4, :], sem=self.dv)
        for base in (0, 32):
            b.wait("vector", self.dv)
            b.run("vector", "tensor_scalar", out=s["nscr"][base:base + 4, 0:LQ],
                  in0=s["rows4"][base:base + 4, :], scalar1=1e-12, scalar2=None,
                  op0=ALU.add, sem=self.dv)
            b.wait("scalar", self.dv)
            b.run("scalar", "activation", out=s["rows4"][base:base + 4, :],
                  in_=s["nscr"][base:base + 4, 0:LQ], func=AF.Sqrt, sem=self.ac)
            b.wait("vector", self.ac)
            b.run("vector", "reciprocal", out=s["rows4"][base:base + 4, :],
                  in_=s["rows4"][base:base + 4, :], sem=self.dv)
            b.wait("vector", self.dv)
            emit_newton_rstd(b, self.dv, s["rows4"][base:base + 4, :],
                             s["nscr"][base:base + 4, 0:LQ],
                             s["nscr"][base:base + 4, LQ:2 * LQ],
                             s["nscr"][base:base + 4, LQ:2 * LQ])
        b.run("vector", "tensor_copy", out=s["possrc"][0:4, :], in_=s["rows4"][0:4, :],
              sem=self.dv)
        b.wait("tensor", self.dv)
        for mq in range(NQM):
            for g in range(2):
                srcs = [s["knT"][:, (g * 4 + i) * LQ + mq * 128:
                                 (g * 4 + i) * LQ + (mq + 1) * 128]
                        for i in range(4)]
                dst = s["knpos"][:, mq * 1024 + g * 512: mq * 1024 + (g + 1) * 512]
                emit_tgroup(self, srcs, dst)
        for mq in range(NQM):
            b.wait("tensor", self.dv)
            b.run("tensor", "transpose", self.ptr[:, 0:4],
                  s["rows4"][32:36, mq * 128:(mq + 1) * 128], s["ident_f"][32:36, 32:36],
                  sem=self.pe)
            b.wait("vector", self.pe)
            b.run("vector", "tensor_copy", out=s["pospan"][:, mq * 96 + 80: mq * 96 + 84],
                  in_=self.ptr[:, 0:4], sem=self.dv)
        b.wait("vector", self.dv)
        b.wait("vector", self.dv2)
        for mq in range(NQM):
            for h in range(H_A):
                b.run("vector", "tensor_scalar", sem=self.dv,
                      out=s["knpos"][:, mq * D + h * 256: mq * D + (h + 1) * 256],
                      in0=s["knpos"][:, mq * D + h * 256: mq * D + (h + 1) * 256],
                      scalar1=self.pos_rk(mq, h), scalar2=None, op0=ALU.mult)
        b.wait("tensor", self.dv)
        for mq in range(NQM):
            for g in range(2):
                srcs = [s["knpos"][:, mq * 1024 + (g * 4 + i) * 128:
                                   mq * 1024 + (g * 4 + i + 1) * 128]
                        for i in range(4)]
                dst = s["knT"][:, :].rearrange("p (oc l) -> p oc l", oc=NDC)[
                    :, g * 4:(g + 1) * 4, mq * 128:(mq + 1) * 128]
                emit_tgroup(self, srcs, dst)
        b.wait("tensor", self.dv)
        b.wait("tensor", self.dv2)

    def gates_rows(self, qt):
        b, s, p = self.b, self.s, self.p
        A = s["arena_a"]
        R = s["rows4"]
        G = s["gates"]
        N = s["nscr"]
        b.wait("scalar", self.dv)
        b.run("scalar", "activation", out=G[0:8, :], in_=G[0:8, :], func=AF.Sigmoid,
              sem=self.ac)
        b.wait("scalar", self.ac)
        b.run("scalar", "activation", out=N[0:8, 0:LQ], in_=G[0:8, :], func=AF.Sqrt,
              sem=self.ac)
        b.wait("vector", self.ac)
        b.run("vector", "reciprocal", out=N[0:8, 0:LQ], in_=N[0:8, 0:LQ], sem=self.dv)
        b.wait("vector", self.dv)
        emit_newton_rstd(b, self.dv, N[0:8, 0:LQ], G[0:8, :],
                         N[0:8, LQ:2 * LQ], N[0:8, LQ:2 * LQ])
        b.run("vector", "tensor_tensor", out=N[0:8, LQ:2 * LQ], in0=G[0:8, :],
              in1=N[0:8, 0:LQ], op=ALU.mult, sem=self.dv)
        b.wait("scalar", self.dv)
        b.run("scalar", "copy", out=s["possrc"][64:72, :], in_=N[0:8, LQ:2 * LQ],
              sem=self.ac)
        b.run("scalar", "copy", out=N[0:4, 0:LQ], in_=G[32:36, :])
        b.run("scalar", "copy", out=N[32:36, 0:LQ], in_=G[64:68, :], sem=self.ac)
        b.wait("vector", self.ac)
        self.ek.emit(b, self.dv, A[0:4, :], N[0:4, 0:LQ], LQ, npart=4)
        self.ek.emit(b, self.dv, s["nscr"][32:36, LQ:2 * LQ], N[32:36, 0:LQ], LQ,
                     npart=4, base=32)
        b.wait("scalar", self.dv)
        b.run("scalar", "copy", out=R[0:4, :], in_=s["nscr"][32:36, LQ:2 * LQ],
              sem=self.ac)
        b.wait("vector", self.ac)
        b.run("vector", "scalar_tensor_tensor", out=G[0:4, :], in0=A[0:4, :],
              scalar=1.0, in1=R[0:4, :], op0=ALU.add, op1=ALU.add, sem=self.dv)
        b.wait("vector", self.dv)
        b.run("vector", "reciprocal", out=G[0:4, :], in_=G[0:4, :], sem=self.dv)
        b.wait("vector", self.dv)
        b.run("vector", "tensor_tensor", out=A[0:4, :], in0=A[0:4, :],
              in1=G[0:4, :], op=ALU.mult)
        b.run("vector", "tensor_tensor", out=R[0:4, :], in0=R[0:4, :],
              in1=G[0:4, :], op=ALU.mult, sem=self.dv)
        for mq in range(NQM):
            base = mq * 96
            sl = slice(mq * 128, (mq + 1) * 128)
            b.wait("tensor", self.dv)
            b.run("tensor", "transpose", self.ptr[:, 0:72],
                  s["possrc"][0:72, sl], s["ident_f"][0:72, 0:72])
            b.run("tensor", "transpose", self.ptr[:, 72:76],
                  G[0:4, sl], s["ident_f"][0:4, 0:4])
            b.run("tensor", "transpose", self.ptr[:, 76:80],
                  A[0:4, sl], s["ident_f"][0:4, 0:4])
            b.run("tensor", "transpose", self.ptr[:, 80:84],
                  R[0:4, sl], s["ident_f"][0:4, 0:4], sem=self.pe)
            b.wait("vector", self.pe)
            b.run("vector", "tensor_copy", out=s["pospan"][:, base:base + 72],
                  in_=self.ptr[:, 0:72], sem=self.dv)
            b.wait("vector", self.dv)
            b.run("vector", "tensor_copy", out=s["pospan"][:, base + 32:base + 36],
                  in_=self.ptr[:, 72:76], sem=self.dv)
            b.run("vector", "tensor_copy", out=s["pospan"][:, base + 72:base + 76],
                  in_=self.ptr[:, 76:80], sem=self.dv)
            b.run("vector", "tensor_copy", out=s["pospan"][:, base + 76:base + 80],
                  in_=self.ptr[:, 80:84], sem=self.dv)
            b.wait("vector", self.dv)
            b.run("vector", "tensor_tensor", out=s["pospan"][:, base + 84:base + 88],
                  in0=s["pospan"][:, base + 72:base + 76],
                  in1=s["pospan"][:, base + 64:base + 68], op=ALU.mult, sem=self.dv)
            b.run("vector", "tensor_tensor", out=s["pospan"][:, base + 88:base + 92],
                  in0=s["pospan"][:, base + 76:base + 80],
                  in1=s["pospan"][:, base + 68:base + 72], op=ALU.mult, sem=self.dv)

    def pos_rq(self, mq, h):
        return self.s["pospan"][:, mq * 96 + h: mq * 96 + h + 1]

    def pos_wr(self, mq, h):
        return self.s["pospan"][:, mq * 96 + 32 + h: mq * 96 + 32 + h + 1]

    def pos_sqb(self, mq, scan, h):
        c = mq * 96 + 64 + scan * 4 + h
        return self.s["pospan"][:, c: c + 1]

    def pos_rk(self, mq, h):
        return self.s["pospan"][:, mq * 96 + 80 + h: mq * 96 + 80 + h + 1]

    def pos_wsb(self, mq, scan, h):
        c = mq * 96 + 84 + scan * 4 + h
        return self.s["pospan"][:, c: c + 1]

    def delta_pass(self, qt):
        b, s, p = self.b, self.s, self.p
        kT = lambda oc, m: s["knT"][:, oc * LQ + m * 128: oc * LQ + (m + 1) * 128]
        qTs = lambda oc, m: s["qT"][:, oc * LQ + m * 128: oc * LQ + (m + 1) * 128]
        if qt == 0:
            b.wait("vector", self.ac)
            b.run("vector", "memset", s["S_l"][:, :], 0.0)
            b.run("vector", "memset", s["S_g"][:, :], 0.0, sem=self.dv)
        for h in range(H_A):
            oc0, oc1 = 2 * h, 2 * h + 1
            Sh = {0: s["S_l"][:, h * 512:(h + 1) * 512],
                  1: s["S_g"][:, h * 512:(h + 1) * 512]}
            for m in range(NQM):
                pw = p["proj0"]
                pa = p["bc"]
                po = p["proj1"]
                b.wait("tensor", self.ac)
                b.wait("tensor", self.dv)
                b.wait("tensor", self.dv2)
                b.wait("tensor", self.ac_c[1])
                b.wait("tensor", self.dv_c[1])
                b.run("tensor", "matmul", pa[:, 0:128], kT(oc0, m), kT(oc0, m), start=True, stop=False)
                b.run("tensor", "matmul", pa[:, 0:128], kT(oc1, m), kT(oc1, m), start=False, stop=True)
                b.run("tensor", "matmul", pa[:, 128:256], kT(oc0, m), qTs(oc0, m), start=True, stop=False)
                b.run("tensor", "matmul", pa[:, 128:256], kT(oc1, m), qTs(oc1, m), start=False, stop=True, sem=self.pe)
                b.wait("scalar", self.pe)
                b.run("scalar", "copy", out=s["Fn"][:, :], in_=pa[:, 0:128], sem=self.ac)
                b.wait("vector", self.pe)
                b.wait("vector", self.ac)
                b.run("vector", "tensor_tensor", out=s["AnT"][:, :],
                      in0=s["mask_triu_inc"][:, :], in1=pa[:, 128:256],
                      op=ALU.mult, sem=self.dv)
                def chain(sc):
                    sfx = "_0" if sc == 0 else "_1"
                    T = lambda nm: s[nm + sfx][:, :]
                    Te = s["Te_l"] if sc == 0 else s["Te_g"]
                    ptrb = self.ptr[:, :] if sc == 0 else pa[:, 256:384]
                    pwb = p["proj0"] if sc == 0 else p["proj1"]
                    sqb = self.pos_sqb(m, sc, h)
                    wsb = self.pos_wsb(m, sc, h)
                    pe_, dv_, ac_ = self.pe_c[sc], self.dv_c[sc], self.ac_c[sc]
                    st_ = []

                    def S(f):
                        st_.append(f)
                    S(lambda: (b.wait("tensor", self.dv), b.wait("tensor", self.ac),
                               b.wait("tensor", self.dv2),
                               (b.wait("tensor", self.dv_e) if sc == 1 else None),
                               b.wait("vector", self.ac), b.wait("vector", self.pe),
                               b.wait("scalar", self.dv)))
                    S(lambda: (b.wait("vector", pe_),
                               b.run("vector", "scalar_tensor_tensor", out=T("Ra"),
                                     in0=s["Fn"][:, :], scalar=sqb,
                                     in1=s["mask_neg_bsl_bd"][:, :], op0=ALU.mult,
                                     op1=ALU.mult, sem=dv_)))
                    S(lambda: (b.wait("tensor", dv_), b.wait("tensor", ac_),
                               b.wait("tensor", dv_),
                               b.run("tensor", "transpose", ptrb, T("Ra"),
                                     s["ident_f"][:, :], sem=pe_)))
                    S(lambda: (b.wait("vector", pe_),
                               b.run("vector", "tensor_scalar", out=T("MT"),
                                     in0=ptrb, scalar1=sqb, scalar2=None,
                                     op0=ALU.mult, sem=dv_)))
                    S(lambda: (b.wait("tensor", dv_),
                               b.run("tensor", "transpose", ptrb, T("MT"),
                                     s["ident_f"][:, :], sem=pe_)))
                    S(lambda: (b.wait("vector", pe_),
                               b.run("vector", "tensor_copy", out=T("M"),
                                     in_=ptrb, sem=dv_)))
                    sq = [("N1", "MT", "M"), ("NT1", "M", "MT"),
                          ("N2", "NT1", "N1"), ("NT2", "N1", "NT1"),
                          ("N3", "NT2", "N2"), ("NT3", "N2", "NT2"),
                          ("N4", "NT3", "N3")]
                    first_sq = [True]
                    for dst, lh, rh in sq:
                        def mk(dst=dst, lh=lh, rh=rh):
                            b.wait("tensor", dv_)
                            b.wait("tensor", ac_)
                            if sc == 1 and first_sq[0]:
                                b.wait("tensor", self.dv_e)
                                first_sq[0] = False
                            b.run("tensor", "matmul", pwb[:, 0:128], T(lh), T(rh),
                                  start=True, stop=True, sem=pe_)
                            if dst in ("N1", "N2", "N3"):
                                b.wait("vector", pe_)
                                b.run("vector", "tensor_copy", out=T(dst),
                                      in_=pwb[:, 0:128], sem=dv_)
                            else:
                                b.wait("scalar", pe_)
                                b.run("scalar", "copy", out=T(dst),
                                      in_=pwb[:, 0:128], sem=ac_)
                        S(mk)
                    S(lambda: (b.wait("vector", dv_), b.wait("vector", ac_),
                               b.run("vector", "tensor_tensor", out=T("Ra"),
                                     in0=T("MT"), in1=s["ident_f"][:, :],
                                     op=ALU.add, sem=dv_)))
                    cur, nxt = "Ra", "Rb"
                    for nk in ("N1", "N2", "N3", "N4"):
                        def mk(nk=nk, cur=cur, nxt=nxt):
                            b.wait("tensor", dv_)
                            b.wait("tensor", ac_)
                            b.run("tensor", "matmul", pwb[:, 128:256], T(nk), T(cur),
                                  start=True, stop=True, sem=pe_)
                            b.wait("vector", pe_)
                            b.run("vector", "scalar_tensor_tensor", out=T(nxt),
                                  in0=pwb[:, 128:256], scalar=1.0, in1=T(cur),
                                  op0=ALU.mult, op1=ALU.add, sem=dv_)
                        S(mk)
                        cur, nxt = nxt, cur
                    TsymT = cur
                    S(lambda: (b.wait("tensor", dv_),
                               b.run("tensor", "transpose", ptrb, T(TsymT),
                                     s["ident_f"][:, :], sem=pe_)))
                    S(lambda: (b.wait("scalar", pe_),
                               b.run("scalar", "copy", out=T("Tsym_nT"), in_=ptrb,
                                     sem=ac_),
                               b.wait("vector", ac_), b.wait("vector", dv_),
                               b.run("vector", "tensor_scalar", out=T("Q1pp"),
                                     in0=T(TsymT), scalar1=sqb, scalar2=None,
                                     op0=ALU.mult, sem=dv_)))
                    S(lambda: (b.wait("tensor", dv_),
                               b.run("tensor", "matmul", pwb[:, 256:384], T("Q1pp"),
                                     s["Fn"][:, :], start=True, stop=True,
                                     sem=pe_)))
                    S(lambda: (b.wait("vector", pe_),
                               b.run("vector", "scalar_tensor_tensor", out=T("negLb"),
                                     in0=pwb[:, 256:384], scalar=wsb,
                                     in1=s["mask_neg_bsl_blk"][:, :], op0=ALU.mult,
                                     op1=ALU.mult, sem=dv_)))
                    S(lambda: (b.wait("tensor", dv_),
                               b.run("tensor", "transpose", ptrb, T("negLb"),
                                     s["ident_f"][:, :], sem=pe_)))
                    S(lambda: (b.wait("scalar", pe_),
                               b.run("scalar", "copy", out=T("negLbT"), in_=ptrb,
                                     sem=ac_),
                               b.wait("vector", ac_),
                               b.run("vector", "tensor_tensor", out=T("Rt0"),
                                     in0=T("negLbT"), in1=s["ident_f"][:, :],
                                     op=ALU.add, sem=dv_)))
                    S(lambda: (b.wait("tensor", dv_),
                               b.run("tensor", "matmul", pwb[:, 0:128], T("negLbT"),
                                     T("negLb"), start=True, stop=True, sem=pe_)))
                    S(lambda: (b.wait("scalar", pe_),
                               b.run("scalar", "copy", out=T("Lb2"), in_=pwb[:, 0:128],
                                     sem=ac_)))
                    S(lambda: (b.wait("tensor", ac_),
                               b.run("tensor", "matmul", pwb[:, 128:256],
                                     s["ident_f"][:, :], T("Rt0"), start=True, stop=False),
                               b.run("tensor", "matmul", pwb[:, 128:256], T("Lb2"),
                                     T("Rt0"), start=False, stop=True, sem=pe_)))
                    S(lambda: (b.wait("vector", pe_),
                               b.run("vector", "tensor_scalar", out=T("TbTp"),
                                     in0=pwb[:, 128:256], scalar1=wsb, scalar2=None,
                                     op0=ALU.mult, sem=dv_)))
                    S(lambda: (b.wait("tensor", dv_),
                               b.run("tensor", "matmul", pwb[:, 256:384],
                                     T("Tsym_nT"), T("TbTp"), start=True, stop=True,
                                     sem=pe_)))
                    S(lambda: (b.wait("vector", pe_),
                               b.run("vector", "tensor_scalar", out=Te[:, :],
                                     in0=pwb[:, 256:384], scalar1=sqb, scalar2=None,
                                     op0=ALU.mult, sem=dv_)))
                    return st_
                cA = chain(0)
                cB = chain(1)
                for i in range(len(cA)):
                    cA[i]()
                    cB[i]()
                for scq in (0, 1):
                    b.wait("tensor", self.dv_c[scq])
                    b.wait("tensor", self.ac_c[scq])
                    b.wait("tensor", self.pe_c[scq])
                vsl = s["vpos"][:, m * D + h * 256: m * D + (h + 1) * 256]

                def chain2(sc):
                    pe_, dv_, ac_ = self.pe_c[sc], self.dv_c[sc], self.ac_c[sc]
                    S_sb = Sh[sc]
                    Te = s["Te_l"] if sc == 0 else s["Te_g"]
                    yb = self.pyu if sc == 0 else pa
                    Ysb = s["Y_sb_0"] if sc == 0 else s["Y_sb_1"]
                    Usb = s["U_sb_0"] if sc == 0 else s["U_sb_1"]
                    st_ = []

                    def S(f):
                        st_.append(f)
                    S(lambda: (b.wait("tensor", self.dv), b.wait("tensor", self.ac),
                               b.wait("tensor", dv_), b.wait("tensor", ac_),
                               b.run("tensor", "matmul", yb[:, 0:256], kT(oc0, m),
                                     S_sb[:, 0:256], start=True, stop=False),
                               b.run("tensor", "matmul", yb[:, 0:256], kT(oc1, m),
                                     S_sb[:, 256:512], start=False, stop=True, sem=pe_)))
                    S(lambda: (b.wait("vector", pe_),
                               b.run("vector", "scalar_tensor_tensor", out=Ysb[:, :],
                                     in0=yb[:, 0:256], scalar=-1.0, in1=vsl,
                                     op0=ALU.mult, op1=ALU.add, sem=dv_)))
                    S(lambda: (b.wait("tensor", dv_),
                               b.run("tensor", "matmul", yb[:, 256:512], Te[:, :],
                                     Ysb[:, :], start=True, stop=True, sem=pe_)))
                    S(lambda: (b.wait("scalar", pe_),
                               b.run("scalar", "copy", out=Usb[:, :],
                                     in_=yb[:, 256:512], sem=ac_)))
                    return st_
                c2A = chain2(0)
                c2B = chain2(1)
                for i in range(len(c2A)):
                    c2A[i]()
                    c2B[i]()
                b.wait("tensor", self.ac_c[0])
                b.wait("tensor", self.ac_c[1])
                b.wait("tensor", self.dv_e)
                for sc in (0, 1):
                    S_sb = Sh[sc]
                    Usb = s["U_sb_0"] if sc == 0 else s["U_sb_1"]
                    b.run("tensor", "matmul", po[:, 0:256], qTs(oc0, m), S_sb[:, 0:256],
                          start=(sc == 0), stop=False)
                    b.run("tensor", "matmul", po[:, 0:256], qTs(oc1, m), S_sb[:, 256:512],
                          start=False, stop=False)
                    b.run("tensor", "matmul", po[:, 0:256], s["AnT"][:, :], Usb[:, :],
                          start=False, stop=(sc == 1), sem=(self.pe if sc == 1 else None))
                for sc in (0, 1):
                    S_sb = Sh[sc]
                    Usb = s["U_sb_0"] if sc == 0 else s["U_sb_1"]
                    kp0 = s["knpos"][:, m * D + h * 256: m * D + h * 256 + 128]
                    kp1 = s["knpos"][:, m * D + h * 256 + 128: m * D + (h + 1) * 256]
                    b.run("tensor", "matmul", self.pS[sc][:, 0:256], kp0, Usb[:, :],
                          start=True, stop=False)
                    b.run("tensor", "matmul", self.pS[sc][:, 256:512], kp1, Usb[:, :],
                          start=False, stop=True, sem=self.pe)
                    b.wait("vector", self.pe)
                    b.run("vector", "scalar_tensor_tensor", out=S_sb[:, :],
                          in0=self.pS[sc][:, :], scalar=1.0, in1=S_sb[:, :],
                          op0=ALU.mult, op1=ALU.add, sem=self.dv)
                def ep_chain(m=m, h=h, vsl=vsl):
                    st_ = []

                    def S(f):
                        st_.append(f)
                    S(lambda: (b.wait("vector", self.pe),
                               b.wait("vector", self.dv_e),
                               b.run("vector", "tensor_scalar", out=s["osc"][:, 0:256],
                                     in0=vsl, scalar1=self.pos_wr(m, h), scalar2=None,
                                     op0=ALU.mult, sem=self.dv_e)))
                    S(lambda: (b.wait("vector", self.dv_e),
                               b.run("vector", "scalar_tensor_tensor", out=s["o2"][:, :],
                                     in0=po[:, 0:256], scalar=self.pos_rq(m, h),
                                     in1=s["osc"][:, 0:256], op0=ALU.mult, op1=ALU.add,
                                     sem=self.dv_e)))
                    S(lambda: (b.wait("scalar", self.dv_e),
                               b.run("scalar", "activation", out=s["osc"][:, 0:256],
                                     in_=s["o2"][:, :], func=AF.Square,
                                     accum_out=s["osc"][:, 256:257], sem=self.ac_e)))
                    S(lambda: (b.wait("vector", self.ac_e),
                               b.run("vector", "tensor_scalar", out=s["osc"][:, 257:258],
                                     in0=s["osc"][:, 256:257], scalar1=1.0 / 256,
                                     scalar2=1e-5, op0=ALU.mult, op1=ALU.add,
                                     sem=self.dv_e)))
                    S(lambda: (b.wait("scalar", self.dv_e),
                               b.run("scalar", "activation", out=s["osc"][:, 258:259],
                                     in_=s["osc"][:, 257:258], func=AF.Sqrt,
                                     sem=self.ac_e)))
                    S(lambda: (b.wait("vector", self.ac_e),
                               b.run("vector", "reciprocal", out=s["osc"][:, 259:260],
                                     in_=s["osc"][:, 258:259], sem=self.dv_e)))
                    S(lambda: (b.wait("vector", self.dv_e),
                               emit_newton_rstd(b, self.dv_e, s["osc"][:, 259:260],
                                                s["osc"][:, 257:258],
                                                s["osc"][:, 253:254], s["osc"][:, 254:255])))
                    S(lambda: b.run("vector", "scalar_tensor_tensor",
                                    out=s["knpos"][:, m * D + h * 256: m * D + (h + 1) * 256],
                                    in0=s["o2"][:, :], scalar=s["osc"][:, 259:260],
                                    in1=s["normw"][:, :], op0=ALU.mult, op1=ALU.mult,
                                    sem=self.dv_e))
                    return st_
                for f in ep_chain():
                    f()

    def wo_pass(self, wo_dram, x_dram, x1_out, hln_out, qt):
        b, s, p = self.b, self.s, self.p
        b.wait("gpsimd", self.pe)
        b.wait("gpsimd", self.dv)
        b.wait("gpsimd", self.pe_c[0])
        b.wait("gpsimd", self.pe_c[1])
        b.dma("gpsimd", s["wo"].rearrange("p (kc c) -> p kc c", kc=NDC),
              wo_dram.rearrange("(kc p) c -> p kc c", p=128), self.ldw)
        b.wait("tensor", self.ldw)
        stt_dv = {}
        st_mark = {}
        tg_dv = {}
        tg_dv2 = {}

        def o3_tgroups(lt2):
            for g in range(2):
                srcs = [s["knpos"][:, lt2 * 1024 + (g * 4 + i) * 128:
                                   lt2 * 1024 + (g * 4 + i + 1) * 128]
                        for i in range(4)]
                dst = s["o3T_t"][:, g * 512:(g + 1) * 512]
                emit_tgroup(self, srcs, dst)
            tg_dv[lt2] = self.dv.n
            tg_dv2[lt2] = self.dv2.n
        b.wait("tensor", self.dv_e)
        b.wait("tensor", self.dv)
        b.wait("tensor", self.ac)
        o3_tgroups(0)
        for lt in range(NQT):
            par = lt % 2
            o = par * 1024
            st = s["stat"][:, par * 16:(par + 1) * 16]
            stp = self.st_out if par == 0 else self.st_out2
            xb2 = par * 1024
            ldxp = self.ldx if par == 0 else self.ldx2
            if lt >= 2:
                b.wait("sync", self.dv, stt_dv[lt - 2])
            b.dma("sync", s["xrow"][:, xb2:xb2 + 1024],
                  x_dram[qt * LQ + lt * 128: qt * LQ + (lt + 1) * 128, :], ldxp)
            b.wait("tensor", ldxp)
            b.wait("tensor", self.dv, tg_dv[lt])
            b.wait("tensor", self.dv2, tg_dv2[lt])
            for half in range(2):
                ps = p["proj0"] if half == 0 else p["proj1"]
                if lt >= 1:
                    b.wait("tensor", self.dv, stt_dv[lt - 1])
                for kc in range(NDC):
                    b.run("tensor", "matmul", ps[:, :],
                          s["o3T_t"][:, kc * 128:(kc + 1) * 128],
                          s["wo"][:, kc * D + half * 512: kc * D + half * 512 + 512],
                          start=(kc == 0), stop=(kc == NDC - 1),
                          sem=(self.pe if kc == NDC - 1 else None))
            if lt + 1 < NQT:
                o3_tgroups(lt + 1)
            b.wait("vector", self.pe)
            b.wait("vector", ldxp)
            if lt >= 2:
                b.wait("vector", stp, st_mark[lt - 2])
            b.run("vector", "scalar_tensor_tensor", out=s["x1_sb"][:, o:o + 512],
                  in0=p["proj0"][:, :], scalar=1.0, in1=s["xrow"][:, xb2:xb2 + 512],
                  op0=ALU.mult, op1=ALU.add, sem=self.dv)
            b.run("vector", "scalar_tensor_tensor", out=s["x1_sb"][:, o + 512:o + 1024],
                  in0=p["proj1"][:, :], scalar=1.0, in1=s["xrow"][:, xb2 + 512:xb2 + 1024],
                  op0=ALU.mult, op1=ALU.add, sem=self.dv)
            stt_dv[lt] = self.dv.n
            b.wait("vector", self.dv, stt_dv[lt])
            b.run("vector", "tensor_scalar", out=s["xsq"][:, o:o + D],
                  in0=s["x1_sb"][:, o:o + D],
                  scalar1=1.0, scalar2=0.0, op0=ALU.mult, op1=ALU.add,
                  accum_out=st[:, 0:1], sem=self.dv)
            b.wait("scalar", self.dv)
            b.run("scalar", "activation", out=s["xsq"][:, o:o + D],
                  in_=s["x1_sb"][:, o:o + D],
                  func=AF.Square, accum_out=st[:, 3:4], sem=self.ac)
            b.wait("vector", self.ac)
            b.run("vector", "tensor_scalar", out=st[:, 6:7], in0=st[:, 0:1],
                  scalar1=1.0 / D, scalar2=None, op0=ALU.mult, sem=self.dv)
            b.run("vector", "tensor_scalar", out=st[:, 12:13], in0=st[:, 3:4],
                  scalar1=1.0 / D, scalar2=1e-5, op0=ALU.mult, op1=ALU.add, sem=self.dv)
            b.wait("vector", self.dv)
            b.run("vector", "tensor_tensor", out=st[:, 8:9], in0=st[:, 6:7],
                  in1=st[:, 6:7], op=ALU.mult, sem=self.dv)
            b.wait("vector", self.dv)
            b.run("vector", "tensor_sub", out=st[:, 12:13], in0=st[:, 12:13],
                  in1=st[:, 8:9], sem=self.dv)
            b.wait("scalar", self.dv)
            b.run("scalar", "activation", out=st[:, 10:11], in_=st[:, 12:13],
                  func=AF.Sqrt, sem=self.ac)
            b.wait("vector", self.ac)
            b.run("vector", "reciprocal", out=st[:, 11:12], in_=st[:, 10:11], sem=self.dv)
            b.wait("vector", self.dv)
            emit_newton_rstd(b, self.dv, st[:, 11:12], st[:, 12:13],
                             st[:, 13:14], st[:, 14:15])
            b.run("vector", "tensor_scalar", out=st[:, 15:16], in0=st[:, 6:7],
                  scalar1=-1.0, scalar2=None, op0=ALU.mult, sem=self.dv)
            b.wait("vector", self.dv)
            b.run("vector", "tensor_scalar", out=s["lnout"][:, o:o + 1024],
                  in0=s["x1_sb"][:, o:o + D],
                  scalar1=st[:, 15:16], scalar2=st[:, 11:12],
                  op0=ALU.add, op1=ALU.mult, sem=self.dv)
            b.wait("sync", self.dv)
            r0 = qt * LQ + lt * 128
            b.dma("sync", x1_out[r0:r0 + 128, :], s["x1_sb"][:, o:o + D], stp)
            b.dma("sync", hln_out[r0:r0 + 128, :], s["lnout"][:, o:o + 1024], stp)
            st_mark[lt] = stp.n


def build_p1(B_loc=4):
    nc = bass.Bass("TRN2", target_bir_lowering=False, debug=False)
    cds = declare_consts(nc)
    x_in = nc.dram_tensor("x_in", [B_loc * L, D], F32, kind="ExternalInput").ap()
    w = {}
    for nm, shape in (("Wq", [D, D]), ("Wk", [D, D]), ("Wv", [D, D]), ("Wo", [D, D]),
                      ("Wcat", [D, 68]), ("convcat", [D, 12]), ("gbias", [128, 1]),
                      ("normw", [1, 256])):
        w[nm] = nc.dram_tensor("w_" + nm, shape, F32, kind="ExternalInput").ap()
    x1_out = nc.dram_tensor("x1_out", [B_loc * L, D], F32, kind="ExternalOutput").ap()
    hln_out = nc.dram_tensor("hln_out", [B_loc * L, D], F32, kind="ExternalOutput").ap()
    with ExitStack() as stack:
        b = B(nc, stack)
        core = AttnACore(nc, b, B_loc)
        core.load_consts(cds)
        core.load_weights(w)
        core.normw_setup(w["normw"])
        for bb in range(B_loc):
            xsl = x_in[bb * L:(bb + 1) * L, :]
            for qt in range(NQUART):
                core.ln_pass(xsl, qt)
                core.projections(w, qt)
                core.gates_mm(qt)
                core.norm_pass(qt)
                core.gates_rows(qt)
                core.delta_pass(qt)
                core.wo_pass(w["Wo"], xsl, x1_out[bb * L:(bb + 1) * L, :],
                             hln_out[bb * L:(bb + 1) * L, :], qt)
        b.wait("sync", core.st_out)
        b.wait("sync", core.st_out2)
        b.finalize()
    return nc


def p1_weight_arrays(I):
    Wcat = np.zeros((D, 68), np.float32)
    Wcat[:, 0:8] = I["L0_attn_a_Wb"]
    Wcat[:, 32:36] = I["L0_attn_a_Wg"]
    Wcat[:, 64:68] = I["L0_attn_a_Wh"]
    gbias = np.zeros((128, 1), np.float32)
    gbias[32:36, 0] = I["L0_attn_a_bg"]
    gbias[64:68, 0] = I["L0_attn_a_bh"]
    convcat = np.concatenate([I["L0_attn_a_conv_q"], I["L0_attn_a_conv_k"],
                              I["L0_attn_a_conv_v"]], axis=1).astype(np.float32)
    return {
        "w_Wq": np.ascontiguousarray(I["L0_attn_a_Wq"]),
        "w_Wk": np.ascontiguousarray(I["L0_attn_a_Wk"]),
        "w_Wv": np.ascontiguousarray(I["L0_attn_a_Wv"]),
        "w_Wo": np.ascontiguousarray(I["L0_attn_a_Wo"]),
        "w_Wcat": Wcat, "w_convcat": convcat, "w_gbias": gbias,
        "w_normw": np.ascontiguousarray(I["L0_attn_a_norm_w"]).reshape(1, 256),
    }


def const_arrays():
    return {"c_" + k: v for k, v in host_consts().items()}


class AttnBCore:

    def __init__(self, nc, b):
        self.nc = nc
        self.b = b
        s = self.s = {}
        e = b.stack.enter_context
        s["xlnT"] = e(nc.sbuf_tensor("b_xlnT", [128, NDC * L], F32))
        s["xrow"] = e(nc.sbuf_tensor("b_xrow", [128, 2 * 1024], F32))
        s["lnout"] = e(nc.sbuf_tensor("b_lnout", [128, 1024], F32))
        s["c0"] = e(nc.sbuf_tensor("b_c0", [128, L], F32))
        s["c1"] = e(nc.sbuf_tensor("b_c1", [128, L], F32))
        s["c2"] = e(nc.sbuf_tensor("b_c2", [128, L], F32))
        s["c0h"] = e(nc.sbuf_tensor("b_c0h", [128, L], BF16))
        s["c0l"] = e(nc.sbuf_tensor("b_c0l", [128, L], BF16))
        s["c1h"] = e(nc.sbuf_tensor("b_c1h", [128, L], BF16))
        s["c1l"] = e(nc.sbuf_tensor("b_c1l", [128, L], BF16))
        s["vposT"] = e(nc.sbuf_tensor("b_vpT", [128, NLT * 130], F32))
        s["wring"] = e(nc.sbuf_tensor("b_wring", [128, NDC * 384], F32))
        s["woring"] = e(nc.sbuf_tensor("b_woring", [128, NDC * D], F32))
        s["bqkv"] = e(nc.sbuf_tensor("b_bqkv", [128, 24], F32))
        s["bo_row"] = e(nc.sbuf_tensor("b_bo", [1, D], F32))
        s["pT"] = e(nc.sbuf_tensor("b_pT", [128, 2 * 1024], F32))
        s["pv_sb"] = e(nc.sbuf_tensor("b_pv", [64, 1024], F32))
        s["rsum"] = e(nc.sbuf_tensor("b_rsum", [1, 1024], F32))
        s["obuf"] = e(nc.sbuf_tensor("b_obuf", [128, L], F32))
        s["stat"] = e(nc.sbuf_tensor("b_stat", [128, 32], F32))
        s["xsq"] = e(nc.sbuf_tensor("b_xsq", [128, D], F32))
        s["x1_sb"] = e(nc.sbuf_tensor("b_x1", [128, D], F32))
        s["ones_row"] = e(nc.sbuf_tensor("b_onesr", [1, 128], F32))
        s["ones_col"] = e(nc.sbuf_tensor("b_onesc", [128, 1], F32))
        s["ident_f"] = e(nc.sbuf_tensor("b_identf", [128, 128], F32))
        s["o_back"] = s["c0"]
        p = self.p = {}
        p["sc0"] = e(nc.psum_tensor("bp_sc0", [128, 512], F32))
        p["sc1"] = e(nc.psum_tensor("bp_sc1", [128, 512], F32))
        p["pv"] = e(nc.psum_tensor("bp_pv", [128, 1024], F32))
        p["r"] = e(nc.psum_tensor("bp_r", [128, 1024], F32))
        p["o0"] = e(nc.psum_tensor("bp_o0", [128, 512], F32))
        p["o1"] = e(nc.psum_tensor("bp_o1", [128, 512], F32))
        p["tr"] = p["o0"]
        p["tr2"] = p["o1"]
        self.ldw = b.sem("b_ldw")
        self.ldwh = b.sem("b_ldwh")
        self.ldx = b.sem("b_ldx")
        self.ldx2 = b.sem("b_ldx2")
        self.dv_l = {0: b.sem("b_dvl0"), 1: b.sem("b_dvl1")}
        self.ac_l = {0: b.sem("b_acl0"), 1: b.sem("b_acl1")}
        self.st_o = b.sem("b_sto")
        self.st_out = b.sem("b_stout")
        self.pe = b.sem("b_pe")
        self.dv = b.sem("b_dv")
        self.dv2 = b.sem("b_dv2")
        self.ac = b.sem("b_ac")

    def load_consts(self, cds, w):
        b, s = self.b, self.s
        b.dma("sync", s["ones_row"][:, :], cds["ones_row"][:, :], self.ldwh)
        b.dma("sync", s["ones_col"][:, :], cds["ones_row"].rearrange("o p -> p o"), self.ldwh)
        b.dma("sync", s["ident_f"][:, :], cds["ident"][:, :], self.ldwh)
        b.dma("sync", s["bqkv"][:, :], w["bqkv_sb"][:, :], self.ldwh)
        b.dma("gpsimd", s["bo_row"][:, :], w["bo_row"][:, :], self.ldw)
        b.run("vector", "memset",
              s["vposT"][:, :].rearrange("p (t c) -> p t c", t=NLT)[:, :, 64:65], 1.0)
        b.run("vector", "memset",
              s["vposT"][:, :].rearrange("p (t c) -> p t c", t=NLT)[:, :, 129:130], 1.0,
              sem=self.dv)
        b.wait("tensor", self.ldwh)
        b.wait("vector", self.ldwh)

    def ln_pass(self, x_dram):
        b, s = self.b, self.s

        def ln_chain(lt):
            par = lt % 2
            hb = par * 1024
            st = s["stat"][:, par * 16:(par + 1) * 16]
            dv_, ac_ = self.dv_l[par], self.ac_l[par]
            ldxp = self.ldx if par == 0 else self.ldx2
            xf = s["xrow"][:, hb:hb + 1024]
            xsq = s["xsq"][:, 0:D] if par == 0 else s["obuf"][:, 0:1024]
            lnout = s["lnout"][:, 0:1024] if par == 0 else s["obuf"][:, 1024:2048]
            st_ = []

            def S(f):
                st_.append(f)

            def s_load():
                b.wait("sync", dv_)
                b.wait("sync", ac_)
                b.wait("sync", self.dv)
                b.dma("sync", xf, x_dram[lt * 128:(lt + 1) * 128, :], ldxp)
            S(s_load)
            S(lambda: (b.wait("vector", ldxp),
                       b.wait("vector", self.st_o),
                       b.run("vector", "tensor_scalar", sem=dv_,
                             out=xsq, in0=xf, scalar1=1.0, scalar2=0.0,
                             op0=ALU.mult, op1=ALU.add, accum_out=st[:, 0:1])))
            S(lambda: (b.wait("scalar", dv_),
                       b.wait("scalar", self.st_o),
                       b.run("scalar", "activation", sem=ac_,
                             out=xsq, in_=xf, func=AF.Square, accum_out=st[:, 1:2])))
            S(lambda: (b.wait("vector", ac_),
                       b.run("vector", "tensor_scalar", out=st[:, 2:3], in0=st[:, 0:1],
                             scalar1=1.0 / D, scalar2=None, op0=ALU.mult, sem=dv_),
                       b.run("vector", "tensor_scalar", out=st[:, 12:13], in0=st[:, 1:2],
                             scalar1=1.0 / D, scalar2=1e-5, op0=ALU.mult, op1=ALU.add,
                             sem=dv_)))
            S(lambda: (b.wait("vector", dv_),
                       b.run("vector", "tensor_tensor", out=st[:, 4:5], in0=st[:, 2:3],
                             in1=st[:, 2:3], op=ALU.mult, sem=dv_)))
            S(lambda: (b.wait("vector", dv_),
                       b.run("vector", "tensor_sub", out=st[:, 12:13], in0=st[:, 12:13],
                             in1=st[:, 4:5], sem=dv_)))
            S(lambda: (b.wait("scalar", dv_),
                       b.run("scalar", "activation", out=st[:, 6:7], in_=st[:, 12:13],
                             func=AF.Sqrt, sem=ac_)))
            S(lambda: (b.wait("vector", ac_),
                       b.run("vector", "reciprocal", out=st[:, 7:8], in_=st[:, 6:7],
                             sem=dv_)))
            S(lambda: (b.wait("vector", dv_),
                       emit_newton_rstd(b, dv_, st[:, 7:8], st[:, 12:13],
                                        st[:, 13:14], st[:, 14:15]),
                       b.run("vector", "tensor_scalar", out=st[:, 15:16], in0=st[:, 2:3],
                             scalar1=-1.0, scalar2=None, op0=ALU.mult, sem=dv_)))
            S(lambda: (b.wait("vector", dv_),
                       b.wait("vector", self.pe),
                       b.wait("vector", self.st_o),
                       b.run("vector", "tensor_scalar", sem=dv_,
                             out=lnout, in0=xf,
                             scalar1=st[:, 15:16], scalar2=st[:, 7:8],
                             op0=ALU.add, op1=ALU.mult)))

            def s_tg():
                b.wait("tensor", dv_)
                for g in range(2):
                    srcs = [lnout[:, (g * 4 + i) * 128:(g * 4 + i + 1) * 128]
                            for i in range(4)]
                    dst = s["xlnT"][:, :].rearrange("p (kc l) -> p kc l", kc=NDC)[
                        :, g * 4:(g + 1) * 4, lt * 128:(lt + 1) * 128]
                    emit_tgroup(self, srcs, dst)
            S(s_tg)
            return st_
        for pr in range(NLT // 2):
            cA = ln_chain(2 * pr)
            cB = ln_chain(2 * pr + 1)
            for i in range(len(cA)):
                cA[i]()
                cB[i]()

    def qkv_pass(self, wqkv_dram, o_dram):
        b, s, p = self.b, self.s, self.p
        b.wait("tensor", self.ldw)
        for hp in range(8):
            b.wait("gpsimd", self.pe)
            b.wait("gpsimd", self.ldw)
            b.dma("gpsimd", s["wring"][:, :].rearrange("p (kc c) -> p kc c", kc=NDC),
                  wqkv_dram[:, hp * 384:(hp + 1) * 384].rearrange("(kc p) c -> p kc c", p=128),
                  self.ldw)
            b.wait("tensor", self.ldw)
            evac_dv = {}
            pe0 = self.pe.n
            nblk = 0
            for lq in range(4):
                for ch, (ps, dstn) in enumerate((("sc0", "c0"), ("sc1", "c1"), ("pv", "c2"))):
                    pst = p[ps] if ps != "pv" else p["pv"][:, 0:512]
                    if nblk == 0:
                        b.wait("tensor", self.ac)
                        b.wait("tensor", self.dv)
                    elif nblk >= 3:
                        b.wait("tensor", self.dv, evac_dv[nblk - 3])
                    for kc in range(NDC):
                        b.run("tensor", "matmul", pst[:, 0:512] if ps != "pv" else pst,
                              s["wring"][:, kc * 384 + ch * 128: kc * 384 + (ch + 1) * 128],
                              s["xlnT"][:, kc * L + lq * 512: kc * L + (lq + 1) * 512],
                              start=(kc == 0), stop=(kc == NDC - 1),
                              sem=(self.pe if kc == NDC - 1 else None))
                    b.wait("vector", self.pe, pe0 + nblk + 1)
                    b.run("vector", "tensor_scalar", sem=self.dv,
                          out=s[dstn][:, lq * 512:(lq + 1) * 512],
                          in0=pst[:, 0:512] if ps != "pv" else pst,
                          scalar1=s["bqkv"][:, hp * 3 + ch: hp * 3 + ch + 1],
                          scalar2=None, op0=ALU.add)
                    evac_dv[nblk] = self.dv.n
                    nblk += 1
            b.wait("vector", self.dv)
            b.wait("vector", self.pe)
            for cn in ("c0", "c1"):
                b.run("vector", "tensor_copy", out=s[cn + "h"][:, :], in_=s[cn][:, :],
                      sem=self.dv)
            b.wait("vector", self.dv)
            for cn in ("c0", "c1"):
                b.run("vector", "scalar_tensor_tensor", out=s[cn + "l"][:, :],
                      in0=s[cn][:, :], scalar=1.0, in1=s[cn + "h"][:, :],
                      op0=ALU.mult, op1=ALU.subtract, sem=self.dv)
            b.wait("tensor", self.dv)
            self.attention(hp)
            b.wait("sync", self.dv)
            b.wait("sync", self.st_o)
            b.dma("sync", o_dram[hp * 128:(hp + 1) * 128, :], s["obuf"][:, :], self.st_o)

    def attention(self, hp):
        b, s, p = self.b, self.s, self.p
        for g in range(4):
            flip = getattr(self, "_trflip", 0)
            self._trflip = 1 - flip
            bank = p["tr"] if flip == 0 else p["tr2"]
            sem_done = self.dv if flip == 0 else self.dv2
            b.wait("tensor", sem_done)
            for i in range(4):
                b.run("tensor", "transpose", bank[:, i * 128:(i + 1) * 128],
                      s["c2"][:, (g * 4 + i) * 128:(g * 4 + i + 1) * 128],
                      s["ident_f"][:, :],
                      sem=(self.pe if i == 3 else None))
            b.wait("vector", self.pe)
            vt3 = s["vposT"][:, :].rearrange("p (t c) -> p t c", t=NLT)
            b.run("vector", "tensor_copy",
                  out=vt3[:, g * 4:(g + 1) * 4, 0:64],
                  in_=bank[:, 0:512].rearrange("p (g c) -> p g c", g=4)[:, :, 0:64])
            b.run("vector", "tensor_copy",
                  out=vt3[:, g * 4:(g + 1) * 4, 65:129],
                  in_=bank[:, 0:512].rearrange("p (g c) -> p g c", g=4)[:, :, 64:128],
                  sem=sem_done)
        for hb in (0, 64):
            hb65 = 0 if hb == 0 else 65
            b.wait("tensor", self.dv)
            b.wait("tensor", self.dv2)
            for qc in range(2):
                b.wait("tensor", self.dv)
                sc_base = self.pe.n
                ex_base = self.ac.n
                pv_base = self.dv2.n
                for kt in range(NLT + 1):
                    if kt < NLT:
                        sb0 = p["sc0"] if kt % 2 == 0 else p["o0"]
                        sb1 = p["sc1"] if kt % 2 == 0 else p["o1"]
                        if kt >= 2:
                            b.wait("tensor", self.ac, ex_base + kt - 1)
                        for ti, (lsfx, rsfx) in enumerate((("h", "h"), ("h", "l"), ("l", "h"))):
                            b.run("tensor", "matmul", sb0[:, :],
                                  s["c1" + lsfx][hb:hb + 64, kt * 128:(kt + 1) * 128],
                                  s["c0" + rsfx][hb:hb + 64, qc * 1024: qc * 1024 + 512],
                                  start=(ti == 0), stop=(ti == 2))
                        for ti, (lsfx, rsfx) in enumerate((("h", "h"), ("h", "l"), ("l", "h"))):
                            b.run("tensor", "matmul", sb1[:, :],
                                  s["c1" + lsfx][hb:hb + 64, kt * 128:(kt + 1) * 128],
                                  s["c0" + rsfx][hb:hb + 64, qc * 1024 + 512: (qc + 1) * 1024],
                                  start=(ti == 0), stop=(ti == 2),
                                  sem=(self.pe if ti == 2 else None))
                    if kt >= 1:
                        j = kt - 1
                        b.wait("tensor", self.ac, ex_base + j + 1)
                        pb = (j % 2) * 1024
                        b.run("tensor", "matmul", p["pv"][0:65, 0:512],
                              s["vposT"][:, j * 130 + hb65: j * 130 + hb65 + 65],
                              s["pT"][:, pb:pb + 512], start=(j == 0), stop=(j == NLT - 1))
                        b.run("tensor", "matmul", p["pv"][0:65, 512:1024],
                              s["vposT"][:, j * 130 + hb65: j * 130 + hb65 + 65],
                              s["pT"][:, pb + 512:pb + 1024], start=(j == 0),
                              stop=(j == NLT - 1), sem=self.dv2)
                for kt in range(NLT):
                    sb0 = p["sc0"] if kt % 2 == 0 else p["o0"]
                    sb1 = p["sc1"] if kt % 2 == 0 else p["o1"]
                    pb = (kt % 2) * 1024
                    b.wait("scalar", self.pe, sc_base + kt + 1)
                    if kt >= 2:
                        b.wait("scalar", self.dv2, pv_base + kt - 1)
                    b.run("scalar", "activation", out=s["pT"][:, pb:pb + 512],
                          in_=sb0[:, :], func=AF.Exp, scale=0.125)
                    b.run("scalar", "activation", out=s["pT"][:, pb + 512:pb + 1024],
                          in_=sb1[:, :], func=AF.Exp, scale=0.125, sem=self.ac)
                b.wait("vector", self.dv2, pv_base + NLT)
                b.run("vector", "reciprocal", out=s["rsum"][0:1, 0:512],
                      in_=p["pv"][64:65, 0:512], sem=self.dv)
                b.run("vector", "reciprocal", out=s["rsum"][0:1, 512:1024],
                      in_=p["pv"][64:65, 512:1024], sem=self.dv)
                b.wait("tensor", self.dv)
                b.run("tensor", "matmul", p["r"][0:64, 0:512], s["ones_row"][0:1, 0:64],
                      s["rsum"][0:1, 0:512], start=True, stop=True)
                b.run("tensor", "matmul", p["r"][0:64, 512:1024], s["ones_row"][0:1, 0:64],
                      s["rsum"][0:1, 512:1024], start=True, stop=True, sem=self.pe)
                b.wait("scalar", self.pe)
                b.run("scalar", "copy", out=s["pv_sb"][0:64, :], in_=p["pv"][0:64, :],
                      sem=self.ac)
                b.wait("vector", self.ac)
                b.wait("vector", self.pe)
                b.wait("vector", self.st_o)
                b.run("vector", "scalar_tensor_tensor", sem=self.dv,
                      out=s["obuf"][hb:hb + 64, qc * 1024:(qc + 1) * 1024],
                      in0=p["r"][0:64, :], scalar=1.0, in1=s["pv_sb"][0:64, :],
                      op0=ALU.mult, op1=ALU.mult)

    def wo_pass(self, wo_dram, o_dram, x_dram, x1_out, hln_out):
        b, s, p = self.b, self.s, self.p
        b.wait("gpsimd", self.pe)
        b.wait("gpsimd", self.ldw)
        b.dma("gpsimd", s["woring"][:, :].rearrange("p (kc c) -> p kc c", kc=NDC),
              wo_dram.rearrange("(kc p) c -> p kc c", p=128), self.ldw)
        b.wait("tensor", self.ldw)
        b.wait("tensor", self.st_o)
        b.wait("sync", self.st_o)
        b.wait("tensor", self.dv)
        b.wait("tensor", self.dv2)
        st = s["stat"]
        for lt in range(NLT):
            ob = (lt % 2) * 1024
            b.wait("sync", self.pe)
            b.dma("sync", s["o_back"][:, ob:ob + 1024].rearrange("p (hp c) -> p hp c", hp=8),
                  o_dram[:, lt * 128:(lt + 1) * 128].rearrange("(hp p) c -> p hp c", p=128),
                  self.ldx)
            xb2 = (lt % 2) * 1024
            b.wait("sync", self.dv)
            b.dma("sync", s["xrow"][:, xb2:xb2 + 1024],
                  x_dram[lt * 128:(lt + 1) * 128, :], self.ldx)
            b.wait("tensor", self.ldx)
            for half in range(2):
                ps = p["o0"] if half == 0 else p["o1"]
                b.wait("tensor", self.dv)
                for hp in range(NDC):
                    b.run("tensor", "matmul", ps[:, :],
                          s["o_back"][:, ob + hp * 128: ob + (hp + 1) * 128],
                          s["woring"][:, hp * D + half * 512: hp * D + half * 512 + 512],
                          start=(hp == 0), stop=False)
                b.run("tensor", "matmul", ps[:, :], s["ones_row"][0:1, 0:128],
                      s["bo_row"][0:1, half * 512:(half + 1) * 512],
                      start=False, stop=True, sem=self.pe)
            b.wait("vector", self.pe)
            b.wait("vector", self.ldx)
            b.wait("vector", self.st_out)
            b.run("vector", "scalar_tensor_tensor", out=s["x1_sb"][:, 0:512],
                  in0=p["o0"][:, :], scalar=1.0, in1=s["xrow"][:, xb2:xb2 + 512],
                  op0=ALU.mult, op1=ALU.add, sem=self.dv)
            b.run("vector", "scalar_tensor_tensor", out=s["x1_sb"][:, 512:1024],
                  in0=p["o1"][:, :], scalar=1.0, in1=s["xrow"][:, xb2 + 512:xb2 + 1024],
                  op0=ALU.mult, op1=ALU.add, sem=self.dv)
            b.wait("vector", self.dv)
            b.run("vector", "tensor_scalar", out=s["xsq"][:, 0:D], in0=s["x1_sb"][:, 0:D],
                  scalar1=1.0, scalar2=0.0, op0=ALU.mult, op1=ALU.add,
                  accum_out=st[:, 0:1], sem=self.dv)
            b.wait("scalar", self.dv)
            b.run("scalar", "activation", out=s["xsq"][:, 0:D], in_=s["x1_sb"][:, 0:D],
                  func=AF.Square, accum_out=st[:, 3:4], sem=self.ac)
            b.wait("vector", self.ac)
            b.run("vector", "tensor_scalar", out=st[:, 6:7], in0=st[:, 0:1],
                  scalar1=1.0 / D, scalar2=None, op0=ALU.mult, sem=self.dv)
            b.wait("vector", self.dv)
            b.run("vector", "tensor_scalar", out=st[:, 7:8], in0=st[:, 3:4],
                  scalar1=1.0 / D, scalar2=None, op0=ALU.mult, sem=self.dv)
            b.run("vector", "tensor_tensor", out=st[:, 8:9], in0=st[:, 6:7],
                  in1=st[:, 6:7], op=ALU.mult, sem=self.dv)
            b.wait("vector", self.dv)
            b.run("vector", "tensor_sub", out=st[:, 9:10], in0=st[:, 7:8], in1=st[:, 8:9],
                  sem=self.dv)
            b.wait("vector", self.dv)
            b.run("vector", "tensor_scalar", out=st[:, 12:13], in0=st[:, 9:10],
                  scalar1=1e-5, scalar2=None, op0=ALU.add, sem=self.dv)
            b.wait("scalar", self.dv)
            b.run("scalar", "activation", out=st[:, 10:11], in_=st[:, 12:13],
                  func=AF.Sqrt, sem=self.ac)
            b.wait("vector", self.ac)
            b.run("vector", "reciprocal", out=st[:, 11:12], in_=st[:, 10:11], sem=self.dv)
            b.wait("vector", self.dv)
            emit_newton_rstd(b, self.dv, st[:, 11:12], st[:, 12:13],
                             st[:, 13:14], st[:, 14:15])
            b.run("vector", "tensor_scalar", out=st[:, 15:16], in0=st[:, 6:7],
                  scalar1=-1.0, scalar2=None, op0=ALU.mult, sem=self.dv)
            b.wait("vector", self.dv)
            b.run("vector", "tensor_scalar", out=s["lnout"][:, :], in0=s["x1_sb"][:, 0:D],
                  scalar1=st[:, 15:16], scalar2=st[:, 11:12],
                  op0=ALU.add, op1=ALU.mult, sem=self.dv)
            b.wait("sync", self.dv)
            b.dma("sync", x1_out[lt * 128:(lt + 1) * 128, :], s["x1_sb"][:, 0:D], self.st_out)
            b.dma("sync", hln_out[lt * 128:(lt + 1) * 128, :], s["lnout"][:, :], self.st_out)


def build_p3(B_loc=4):
    nc = bass.Bass("TRN2", target_bir_lowering=False, debug=False)
    cds = declare_consts(nc)
    x_in = nc.dram_tensor("x_in", [B_loc * L, D], F32, kind="ExternalInput").ap()
    w = {}
    for nm, shape in (("wqkv", [D, 8 * 384]), ("Wo", [D, D]),
                      ("bqkv_sb", [128, 24]), ("bo_row", [1, D])):
        w[nm] = nc.dram_tensor("w_" + nm, shape, F32, kind="ExternalInput").ap()
    x3_out = nc.dram_tensor("x3_out", [B_loc * L, D], F32, kind="ExternalOutput").ap()
    hln_out = nc.dram_tensor("hln_out", [B_loc * L, D], F32, kind="ExternalOutput").ap()
    o_dram = nc.dram_tensor("o_dram", [8 * 128, L], F32, kind="Internal").ap()
    with ExitStack() as stack:
        b = B(nc, stack)
        core = AttnBCore(nc, b)
        core.load_consts(cds, w)
        for bb in range(B_loc):
            xsl = x_in[bb * L:(bb + 1) * L, :]
            core.ln_pass(xsl)
            core.qkv_pass(w["wqkv"], o_dram)
            core.wo_pass(w["Wo"], o_dram, xsl, x3_out[bb * L:(bb + 1) * L, :],
                         hln_out[bb * L:(bb + 1) * L, :])
        b.wait("sync", core.st_out)
        b.finalize()
    return nc


def p3_weight_arrays(I):
    Wq = I["L0_attn_b_W_qkv"]
    bq = I["L0_attn_b_b_qkv"]
    cols = []
    bcols = np.zeros((128, 24), np.float32)
    for hp in range(8):
        A, Bh = 2 * hp, 2 * hp + 1
        cols += [Wq[A][:, 0:64], Wq[Bh][:, 0:64],
                 Wq[A][:, 64:128], Wq[Bh][:, 64:128],
                 Wq[A][:, 128:192], Wq[Bh][:, 128:192]]
        bcols[0:64, hp * 3 + 0] = bq[A][0:64]
        bcols[64:128, hp * 3 + 0] = bq[Bh][0:64]
        bcols[0:64, hp * 3 + 1] = bq[A][64:128]
        bcols[64:128, hp * 3 + 1] = bq[Bh][64:128]
        bcols[0:64, hp * 3 + 2] = bq[A][128:192]
        bcols[64:128, hp * 3 + 2] = bq[Bh][128:192]
    wqkv = np.concatenate(cols, axis=1).astype(np.float32)
    return {
        "w_wqkv": np.ascontiguousarray(wqkv),
        "w_Wo": np.ascontiguousarray(I["L0_attn_b_W_o"]),
        "w_bqkv_sb": bcols,
        "w_bo_row": np.ascontiguousarray(I["L0_attn_b_b_o"]).reshape(1, D),
    }


NROW = 8192
NPG = 2

EXP_INV_LN2 = 1.4426950408889634
EXP_LN2_HI = 0.693359375
EXP_LN2_LO = -0.00021219444005469057
EXP_C = [1.0, 1.0, 0.5, 1.0 / 6, 1.0 / 24, 1.0 / 120, 1.0 / 720, 1.0 / 5040,
         1.0 / 40320, 1.0 / 362880]


class ExpKit:

    def __init__(self, nc, stack, prefix, W):
        e = stack.enter_context
        self.W = W
        self.r = e(nc.sbuf_tensor(prefix + "_exr", [128, W], F32))
        self.u = e(nc.sbuf_tensor(prefix + "_exu", [128, W], F32))
        self.p = e(nc.sbuf_tensor(prefix + "_exq", [128, W], F32))
        self.ki = e(nc.sbuf_tensor(prefix + "_exi", [128, W], I32))

    def emit(self, b, dv, dst, x, n, npart=128, base=0):
        r = self.r[base:base + npart, 0:n]
        u = self.u[base:base + npart, 0:n]
        p = self.p[base:base + npart, 0:n]
        ki = self.ki[base:base + npart, 0:n]

        def w():
            b.wait("vector", dv)
        b.run("vector", "tensor_scalar", out=r, in0=x, scalar1=EXP_INV_LN2,
              scalar2=8192.5, op0=ALU.mult, op1=ALU.add, sem=dv)
        w()
        b.run("vector", "tensor_scalar", out=r, in0=r, scalar1=8072.5,
              scalar2=8312.5, op0=ALU.max, op1=ALU.min, sem=dv)
        w()
        b.run("vector", "tensor_copy", out=ki, in_=r, sem=dv)
        w()
        b.run("vector", "tensor_copy", out=r, in_=ki, sem=dv)
        w()
        b.run("vector", "tensor_scalar", out=r, in0=r, scalar1=-8192.0,
              scalar2=None, op0=ALU.add, sem=dv)
        w()
        b.run("vector", "scalar_tensor_tensor", out=u, in0=r,
              scalar=-EXP_LN2_HI, in1=x, op0=ALU.mult, op1=ALU.add, sem=dv)
        w()
        b.run("vector", "scalar_tensor_tensor", out=u, in0=r,
              scalar=-EXP_LN2_LO, in1=u, op0=ALU.mult, op1=ALU.add, sem=dv)
        w()
        b.run("vector", "memset", p, EXP_C[9], sem=dv)
        w()
        for k in range(8, -1, -1):
            b.run("vector", "tensor_tensor", out=p, in0=p, in1=u,
                  op=ALU.mult, sem=dv)
            w()
            b.run("vector", "tensor_scalar", out=p, in0=p, scalar1=EXP_C[k],
                  scalar2=None, op0=ALU.add, sem=dv)
            w()
        b.run("vector", "tensor_scalar", out=ki, in0=ki, scalar1=127 - 8192,
              scalar2=8388608, op0=ALU.add, op1=ALU.mult, sem=dv)
        w()
        b.run("vector", "tensor_tensor", out=dst, in0=p,
              in1=ki.bitcast(F32), op=ALU.mult, sem=dv)
        w()


def emit_newton_rstd(b, dv, r_col, vpe_col, scr0, scr1):
    b.run("vector", "tensor_tensor", out=scr0, in0=r_col, in1=r_col,
          op=ALU.mult, sem=dv)
    b.wait("vector", dv)
    b.run("vector", "tensor_tensor", out=scr1, in0=scr0, in1=vpe_col,
          op=ALU.mult, sem=dv)
    b.wait("vector", dv)
    b.run("vector", "tensor_scalar", out=scr1, in0=scr1, scalar1=-0.5,
          scalar2=1.5, op0=ALU.mult, op1=ALU.add, sem=dv)
    b.wait("vector", dv)
    b.run("vector", "tensor_tensor", out=r_col, in0=r_col, in1=scr1,
          op=ALU.mult, sem=dv)
    b.wait("vector", dv)


def emit_tgroup(core, srcs, dst, win=128):
    b, s, p = core.b, core.s, core.p
    flip = getattr(core, "_trflip", 0)
    core._trflip = 1 - flip
    bank = p["tr"] if flip == 0 else p["tr2"]
    sem_done = core.dv if flip == 0 else core.dv2
    b.wait("tensor", sem_done)
    for i, src in enumerate(srcs):
        nin = src.shape[0]
        b.run("tensor", "transpose", bank[:, i * win:i * win + nin], src,
              core.s["ident_f"][0:nin, 0:nin],
              sem=(core.pe if i == len(srcs) - 1 else None))
    b.wait("vector", core.pe)
    n = len(srcs)
    src_ap = bank[:, 0:n * win]
    if dst.ndim == 3:
        src_ap = src_ap.rearrange("p (g c) -> p g c", g=n)
    b.run("vector", "tensor_copy", out=dst, in_=src_ap, sem=sem_done)


class FfnACore:

    def __init__(self, nc, b):
        self.nc = nc
        self.b = b
        s = self.s = {}
        e = b.stack.enter_context
        s["hlnT"] = e(nc.sbuf_tensor("f_hlnT", [128, NDC * 2048], F32))
        s["hrow"] = e(nc.sbuf_tensor("f_hrow", [128, 2 * 1024], F32))
        s["lgev"] = e(nc.sbuf_tensor("f_lgev", [16, 2 * 512], F32))
        s["L4"] = e(nc.sbuf_tensor("f_L4", [32, 16 * 256], F32))
        s["ltt"] = e(nc.sbuf_tensor("f_ltt", [128, 32 * 32], F32))
        s["expb"] = e(nc.sbuf_tensor("f_expb", [128, 512], F32))
        s["Ssum"] = e(nc.sbuf_tensor("f_Ssum", [128, 32], F32))
        s["lse"] = e(nc.sbuf_tensor("f_lse", [128, 2 * 32], F32))
        s["lscr"] = e(nc.sbuf_tensor("f_lscr", [128, 32], F32))
        s["rt"] = e(nc.sbuf_tensor("f_rt", [128, 32], F32))
        s["top8"] = e(nc.sbuf_tensor("f_top8", [128, 8], F32))
        s["idxu"] = e(nc.sbuf_tensor("f_idxu", [128, 8], U32))
        s["idxf"] = e(nc.sbuf_tensor("f_idxf", [128, 4], F32))
        s["posf"] = e(nc.sbuf_tensor("f_posf", [128, 2], F32))
        s["gofff"] = e(nc.sbuf_tensor("f_goff", [128, 2], F32))
        s["gidx"] = e(nc.sbuf_tensor("f_gidx", [128, 64], I32))
        s["ghg"] = e(nc.sbuf_tensor("f_ghg", [128, 2 * 1024], F32))
        s["hgT"] = e(nc.sbuf_tensor("f_hgT", [128, NDC * 512], F32))
        s["wg"] = e(nc.sbuf_tensor("f_wg", [128, NDC * 16], F32))
        s["wqkv"] = e(nc.sbuf_tensor("f_wqkv", [128, 2 * NDC * 192], F32))
        s["bqkv"] = e(nc.sbuf_tensor("f_bqkv", [128, 32], F32))
        s["qA"] = e(nc.sbuf_tensor("f_qA", [128, 512], F32))
        s["qB"] = e(nc.sbuf_tensor("f_qB", [64, 512], F32))
        s["kT0"] = e(nc.sbuf_tensor("f_kT0", [64, 512], F32))
        s["Z"] = e(nc.sbuf_tensor("f_Z", [64, 1024], F32))
        s["attall"] = e(nc.sbuf_tensor("f_attall", [16, 1024], F32))
        s["eattall"] = e(nc.sbuf_tensor("f_eattall", [16, 1024], F32))
        s["att_r"] = e(nc.sbuf_tensor("f_attr", [1, 1024], F32))
        s["sl"] = e(nc.sbuf_tensor("f_sl", [16, 4], F32))
        s["ones_row"] = e(nc.sbuf_tensor("f_onesr", [1, 128], F32))
        s["ones_col"] = e(nc.sbuf_tensor("f_onesc", [128, 1], F32))
        s["ident_f"] = e(nc.sbuf_tensor("f_identf", [128, 128], F32))
        self.ek = ExpKit(nc, b.stack, "f", 512)
        p = self.p = {}
        p["tr"] = e(nc.psum_tensor("fp_tr", [128, 512], F32))
        p["tr2"] = e(nc.psum_tensor("fp_tr2", [128, 512], F32))
        p["lg"] = e(nc.psum_tensor("fp_lg", [128, 512], F32))
        p["bc"] = e(nc.psum_tensor("fp_bc", [128, 512], F32))
        p["A"] = e(nc.psum_tensor("fp_A", [128, 512], F32))
        p["B"] = e(nc.psum_tensor("fp_B", [128, 512], F32))
        p["att"] = e(nc.psum_tensor("fp_att", [128, 1024], F32))
        self.ldw = b.sem("f_ldw")
        self.ldwh = b.sem("f_ldwh")
        self.ldx = b.sem("f_ldx")
        self.gat = b.sem("f_gat")
        self.st_lg = b.sem("f_stlg")
        self.st_q = b.sem("f_stq")
        self.st_out = b.sem("f_stout")
        self.pe = b.sem("f_pe")
        self.dv = b.sem("f_dv")
        self.dv2 = b.sem("f_dv2")
        self.ac = b.sem("f_ac")
        self.ldx2s = b.sem("f_ldx2")

    def emit(self, hln_in, w, cds, lg_dram, qa_out, qb_out,
             eatt_out, slocal_out, gidx_out):
        b, s, p = self.b, self.s, self.p
        b.dma("sync", s["ones_row"][:, :], cds["ones_row"][:, :], self.ldwh)
        b.dma("sync", s["ones_col"][:, :], cds["ones_row"].rearrange("o p -> p o"), self.ldwh)
        b.dma("sync", s["ident_f"][:, :], cds["ident"][:, :], self.ldwh)
        b.dma("sync", s["posf"][:, :], w["posf"][:, :], self.ldwh)
        b.dma("sync", s["wg"][:, :].rearrange("p (kc c) -> p kc c", kc=NDC),
              w["Wg"].rearrange("(kc p) c -> p kc c", p=128), self.ldwh)
        b.dma("sync", s["bqkv"][:, :], w["bqkv_sb"][:, :], self.ldwh)
        b.wait("tensor", self.ldwh)
        b.wait("vector", self.ldwh)
        for ch in range(4):
            b.wait("tensor", self.st_lg)
            for t in range(NLT):
                hb = (t % 2) * 1024
                b.wait("sync", self.pe)
                b.dma("sync", s["hrow"][:, hb:hb + 1024],
                      hln_in[ch * 2048 + t * 128: ch * 2048 + (t + 1) * 128, :],
                      self.ldx)
                b.wait("tensor", self.ldx)
                for g in range(2):
                    srcs = [s["hrow"][:, hb + (g * 4 + i) * 128: hb + (g * 4 + i + 1) * 128]
                            for i in range(4)]
                    dst = s["hlnT"][:, :].rearrange("p (kc l) -> p kc l", kc=NDC)[
                        :, g * 4:(g + 1) * 4, t * 128:(t + 1) * 128]
                    emit_tgroup(self, srcs, dst)
            for lq in range(4):
                b.wait("tensor", self.dv)
                b.wait("tensor", self.dv2)
                b.wait("tensor", self.ac)
                for kc in range(NDC):
                    b.run("tensor", "matmul", p["lg"][0:16, :],
                          s["wg"][:, kc * 16:(kc + 1) * 16],
                          s["hlnT"][:, kc * 2048 + lq * 512: kc * 2048 + (lq + 1) * 512],
                          start=(kc == 0), stop=(kc == NDC - 1),
                          sem=(self.pe if kc == NDC - 1 else None))
                lb = (lq % 2) * 512
                b.wait("scalar", self.pe)
                b.wait("scalar", self.st_lg)
                b.run("scalar", "copy", out=s["lgev"][0:16, lb:lb + 512],
                      in_=p["lg"][0:16, :], sem=self.ac)
                b.wait("sync", self.ac)
                b.dma("sync",
                      lg_dram[:, ch * 2048 + lq * 512: ch * 2048 + (lq + 1) * 512],
                      s["lgev"][0:16, lb:lb + 512], self.st_lg)
        b.wait("sync", self.st_lg)
        b.dma("sync", s["L4"][:, :].rearrange("p (h c) -> p h c", h=16),
              lg_dram.rearrange("h (t c) -> t h c", t=32), self.ldx2s)
        b.wait("tensor", self.ldx2s)
        for tg in range(8):
            srcs = [s["L4"][0:32, ((tg * 4 + i) // 2) * 256 + ((tg * 4 + i) % 2) * 128:
                            ((tg * 4 + i) // 2) * 256 + ((tg * 4 + i) % 2) * 128 + 128]
                    for i in range(4)]
            dst = s["ltt"][:, tg * 128:(tg + 1) * 128]
            emit_tgroup(self, srcs, dst, win=32)
        for pg in range(NPG):
            b.wait("vector", self.dv)
            b.wait("vector", self.dv2)
            b.run("vector", "tensor_copy",
                  out=s["expb"][:, :].rearrange("p (h c) -> p h c", h=16),
                  in_=s["ltt"][:, :].rearrange("p (h g) -> p h g", h=16)[
                      :, :, pg * 32:(pg + 1) * 32],
                  sem=self.dv)
            b.wait("vector", self.dv)
            self.ek.emit(b, self.dv, s["expb"][:, :], s["expb"][:, :], 512)
            stride = 8
            while stride >= 1:
                b.run("vector", "tensor_tensor",
                      out=s["expb"][:, :].rearrange("p (h c) -> p h c", h=16)[:, 0:stride, :],
                      in0=s["expb"][:, :].rearrange("p (h c) -> p h c", h=16)[:, 0:stride, :],
                      in1=s["expb"][:, :].rearrange("p (h c) -> p h c", h=16)[:, stride:2 * stride, :],
                      op=ALU.add, sem=self.dv)
                b.wait("vector", self.dv)
                stride //= 2
            b.run("vector", "tensor_copy", out=s["Ssum"][:, :],
                  in_=s["expb"][:, 0:32], sem=self.dv)
            b.wait("scalar", self.dv)
            b.run("scalar", "activation", out=s["lse"][:, pg * 32:(pg + 1) * 32],
                  in_=s["Ssum"][:, :], func=AF.Ln, sem=self.ac)
            b.wait("vector", self.ac)
            b.run("vector", "tensor_scalar", out=s["lscr"][:, :],
                  in0=s["lse"][:, pg * 32:(pg + 1) * 32], scalar1=-1.0,
                  scalar2=None, op0=ALU.mult, sem=self.dv)
            b.wait("vector", self.dv)
            self.ek.emit(b, self.dv, s["lscr"][:, :], s["lscr"][:, :], 32)
            b.run("vector", "tensor_tensor", out=s["lscr"][:, :],
                  in0=s["lscr"][:, :], in1=s["Ssum"][:, :], op=ALU.mult,
                  sem=self.dv)
            b.wait("vector", self.dv)
            b.run("vector", "tensor_scalar", out=s["lscr"][:, :],
                  in0=s["lscr"][:, :], scalar1=-1.0, scalar2=None,
                  op0=ALU.add, sem=self.dv)
            b.wait("vector", self.dv)
            b.run("vector", "tensor_tensor",
                  out=s["lse"][:, pg * 32:(pg + 1) * 32],
                  in0=s["lse"][:, pg * 32:(pg + 1) * 32],
                  in1=s["lscr"][:, :], op=ALU.add, sem=self.dv)
            b.wait("vector", self.dv)
        for h in range(16):
            for pg in range(NPG):
                t = h * NPG + pg
                b.wait("vector", self.dv)
                b.run("vector", "tensor_sub", out=s["rt"][:, :],
                      in0=s["ltt"][:, t * 32:(t + 1) * 32],
                      in1=s["lse"][:, pg * 32:(pg + 1) * 32], sem=self.dv)
                b.wait("vector", self.dv)
                b.run("vector", "max", s["top8"][:, :], s["rt"][:, :], sem=self.dv)
                b.wait("vector", self.dv)
                b.run("vector", "max_index", s["idxu"][:, :], s["top8"][:, :],
                      s["rt"][:, :], sem=self.dv)
                b.wait("vector", self.dv)
                b.run("vector", "tensor_copy", out=s["idxf"][:, 0:2], in_=s["idxu"][:, 0:2],
                      sem=self.dv)
                b.wait("vector", self.dv)
                b.run("vector", "tensor_tensor", out=s["idxf"][:, 2:3], in0=s["idxf"][:, 0:1],
                      in1=s["idxf"][:, 1:2], op=ALU.min)
                b.run("vector", "tensor_tensor", out=s["idxf"][:, 3:4], in0=s["idxf"][:, 0:1],
                      in1=s["idxf"][:, 1:2], op=ALU.max, sem=self.dv)
                b.wait("vector", self.dv)
                for i in range(2):
                    b.run("vector", "scalar_tensor_tensor", out=s["gofff"][:, i:i + 1],
                          in0=s["idxf"][:, 2 + i:3 + i], scalar=256.0,
                          in1=s["posf"][:, pg:pg + 1], op0=ALU.mult, op1=ALU.add,
                          sem=self.dv)
                b.wait("vector", self.dv)
                b.run("vector", "tensor_copy",
                      out=s["gidx"][:, t * 2: t * 2 + 2], in_=s["gofff"][:, 0:2],
                      sem=self.dv)
        b.wait("sync", self.dv)
        b.dma("sync", gidx_out[:, :], s["gidx"][:, :], self.st_out)
        b.wait("gpsimd", self.dv)
        for h in range(16):
            whalf = h % 2
            wr = s["wqkv"][:, whalf * NDC * 192:(whalf + 1) * NDC * 192]
            b.wait("gpsimd", self.pe)
            b.wait("gpsimd", self.ldw)
            b.dma("gpsimd", wr.rearrange("p (kc c) -> p kc c", kc=NDC),
                  w["Wqkvf"][:, h * 192:(h + 1) * 192].rearrange("(kc p) c -> p kc c", p=128),
                  self.ldw)
            for sl4 in range(4):
                gg = h * 4 + sl4
                half = sl4 % 2
                b.wait("gpsimd", self.pe)
                b.op("gpsimd", lambda E, s=s, hln_in=hln_in, gg=gg, half=half, sem=self.gat:
                     E.indirect_dma_start(
                         out=s["ghg"][:, half * 1024:(half + 1) * 1024],
                         out_offset=None, in_=hln_in,
                         in_offset=bass.IndirectOffsetOnAxis(ap=s["gidx"][:, gg:gg + 1], axis=0),
                     ).then_inc(sem.h, 16))
                self.gat.bump(16)
                b.wait("tensor", self.gat)
                for g in range(2):
                    srcs = [s["ghg"][:, half * 1024 + (g * 4 + i) * 128:
                                     half * 1024 + (g * 4 + i + 1) * 128]
                            for i in range(4)]
                    dst = s["hgT"][:, :].rearrange("p (kc r) -> p kc r", kc=NDC)[
                        :, g * 4:(g + 1) * 4, sl4 * 128:(sl4 + 1) * 128]
                    emit_tgroup(self, srcs, dst)
            b.wait("tensor", self.ldw)
            b.wait("tensor", self.dv)
            b.wait("tensor", self.dv2)
            b.wait("tensor", self.ac)
            for kc in range(NDC):
                b.run("tensor", "matmul", p["A"][:, :],
                      wr[:, kc * 192: kc * 192 + 128],
                      s["hgT"][:, kc * 512:(kc + 1) * 512],
                      start=(kc == 0), stop=(kc == NDC - 1))
            for kc in range(NDC):
                b.run("tensor", "matmul", p["B"][0:64, :],
                      wr[:, kc * 192 + 128: kc * 192 + 192],
                      s["hgT"][:, kc * 512:(kc + 1) * 512],
                      start=(kc == 0), stop=(kc == NDC - 1),
                      sem=(self.pe if kc == NDC - 1 else None))
            b.wait("vector", self.pe)
            b.wait("vector", self.st_q)
            b.run("vector", "tensor_scalar", out=s["qA"][:, :], in0=p["A"][:, :],
                  scalar1=s["bqkv"][:, h * 2: h * 2 + 1], scalar2=None, op0=ALU.add,
                  sem=self.dv)
            b.run("vector", "tensor_scalar", out=s["qB"][0:64, :], in0=p["B"][0:64, :],
                  scalar1=s["bqkv"][0:64, h * 2 + 1: h * 2 + 2], scalar2=None, op0=ALU.add,
                  sem=self.dv)
            b.wait("sync", self.dv)
            b.dma("sync", qa_out[h * 128:(h + 1) * 128, :], s["qA"][:, :], self.st_q)
            b.dma("sync", qb_out[h * 64:(h + 1) * 64, :], s["qB"][0:64, :], self.st_q)
            b.wait("scalar", self.dv)
            b.run("scalar", "copy", out=s["kT0"][0:64, :], in_=s["qA"][64:128, :], sem=self.ac)
            b.wait("vector", self.ac)
            for i in range(2):
                for j in range(2):
                    ij = i * 2 + j
                    for pg in range(NPG):
                        b.run("vector", "tensor_tensor",
                              out=s["Z"][0:64, ij * 256 + pg * 128: ij * 256 + (pg + 1) * 128],
                              in0=s["qA"][0:64, pg * 256 + i * 128: pg * 256 + (i + 1) * 128],
                              in1=s["kT0"][0:64, pg * 256 + j * 128: pg * 256 + (j + 1) * 128],
                              op=ALU.mult,
                              sem=(self.dv if (ij == 3 and pg == 1) else None))
            b.wait("tensor", self.dv)
            b.run("tensor", "matmul", p["att"][0:1, 0:512], s["ones_col"][0:64, 0:1],
                  s["Z"][0:64, 0:512], start=True, stop=True)
            b.run("tensor", "matmul", p["att"][0:1, 512:1024], s["ones_col"][0:64, 0:1],
                  s["Z"][0:64, 512:1024], start=True, stop=True, sem=self.pe)
            b.wait("vector", self.pe)
            b.wait("vector", self.st_out)
            b.run("vector", "tensor_scalar", out=s["att_r"][0:1, :],
                  in0=p["att"][0:1, 0:1024], scalar1=0.125, scalar2=None,
                  op0=ALU.mult, sem=self.dv)
            b.wait("sync", self.dv)
            b.dma("sync", s["attall"][h:h + 1, :], s["att_r"][0:1, :], self.st_out)
        b.wait("vector", self.st_out)
        b.wait("vector", self.dv)
        self.ek.emit(b, self.dv, s["eattall"][0:16, 0:512], s["attall"][0:16, 0:512], 512, npart=16)
        self.ek.emit(b, self.dv, s["eattall"][0:16, 512:1024], s["attall"][0:16, 512:1024], 512, npart=16)
        b.wait("sync", self.dv)
        b.dma("sync", eatt_out[:, :], s["eattall"][0:16, :], self.st_out)
        b.run("vector", "reduce_sum", s["sl"][0:16, 0:4],
              s["eattall"][0:16, :].rearrange("h (ij c) -> h ij c", ij=4),
              axis=AX.X, sem=self.dv)
        b.wait("sync", self.dv)
        b.dma("sync", slocal_out[:, :], s["sl"][0:16, 0:4], self.st_out)
        b.wait("sync", self.st_out)
        b.wait("sync", self.st_q)


class FfnBCore:

    def __init__(self, nc, b):
        self.nc = nc
        self.b = b
        s = self.s = {}
        e = b.stack.enter_context
        s["sallT"] = e(nc.sbuf_tensor("g_sallT", [64, 8], F32))
        s["rS64"] = e(nc.sbuf_tensor("g_rS64", [64, 1], F32))
        s["rSrow"] = e(nc.sbuf_tensor("g_rSrow", [1, 64], F32))
        s["eatt_h"] = e(nc.sbuf_tensor("g_eatth", [1, 2 * 1024], F32))
        s["p_hf"] = e(nc.sbuf_tensor("g_ph", [1, 2 * 1024], F32))
        s["ones5"] = e(nc.sbuf_tensor("g_ones5", [1, 512], F32))
        s["vT"] = e(nc.sbuf_tensor("g_vT", [64, 2 * 512], F32))
        s["avT"] = e(nc.sbuf_tensor("g_avT", [64, 2 * 512], F32))
        s["t0"] = e(nc.sbuf_tensor("g_t0", [64, 2 * 128], F32))
        s["aoT"] = e(nc.sbuf_tensor("g_aoT", [128, 2 * NDC * 512], F32))
        s["aoS"] = e(nc.sbuf_tensor("g_aoS", [128, 2 * 4 * 1024], F32))
        s["gidx"] = e(nc.sbuf_tensor("g_gidx", [128, 64], I32))
        s["wof"] = e(nc.sbuf_tensor("g_wof", [64, 2 * 1024], F32))
        s["bo_row"] = e(nc.sbuf_tensor("g_bo", [1, D], F32))
        s["bo_col"] = e(nc.sbuf_tensor("g_bocol", [128, NDC], F32))
        s["lng"] = e(nc.sbuf_tensor("g_lng", [1, D], F32))
        s["lnb"] = e(nc.sbuf_tensor("g_lnb", [1, D], F32))
        s["Gt"] = e(nc.sbuf_tensor("g_Gt", [128, D], F32))
        s["Bt"] = e(nc.sbuf_tensor("g_Bt", [128, D], F32))
        s["y_f"] = e(nc.sbuf_tensor("g_yf", [128, 2 * D], F32))
        s["xr_f"] = e(nc.sbuf_tensor("g_xrf", [128, 2 * D], F32))
        s["z_f"] = e(nc.sbuf_tensor("g_zf", [128, 2 * D], F32))
        s["t2"] = e(nc.sbuf_tensor("g_t2", [128, 2 * D], F32))
        s["wpre"] = e(nc.sbuf_tensor("g_wpre", [128, 2 * D], F32))
        s["xn"] = e(nc.sbuf_tensor("g_xn", [128, 2 * D], F32))
        s["stat"] = e(nc.sbuf_tensor("g_stat", [128, 32], F32))
        s["xsq"] = e(nc.sbuf_tensor("g_xsq", [128, 2 * D], F32))
        s["ones_row"] = e(nc.sbuf_tensor("g_onesr", [1, 128], F32))
        s["ident_f"] = e(nc.sbuf_tensor("g_identf", [128, 128], F32))
        s["lscr"] = e(nc.sbuf_tensor("g_lscr", [128, 32], F32))
        p = self.p = {}
        p["tr"] = e(nc.psum_tensor("gp_tr", [128, 512], F32))
        p["tr2"] = e(nc.psum_tensor("gp_tr2", [128, 512], F32))
        p["bcB"] = e(nc.psum_tensor("gp_bcB", [128, 512], F32))
        p["bcB2"] = e(nc.psum_tensor("gp_bcB2", [128, 512], F32))
        p["ao"] = e(nc.psum_tensor("gp_ao", [128, 512], F32))
        p["ao2"] = e(nc.psum_tensor("gp_ao2", [128, 512], F32))
        p["ao3"] = e(nc.psum_tensor("gp_ao3", [128, 512], F32))
        p["bc"] = e(nc.psum_tensor("gp_bc", [128, 512], F32))
        self.ldw = b.sem("g_ldw")
        self.ldwh = b.sem("g_ldwh")
        self.ldx = b.sem("g_ldx")
        self.ldx2 = b.sem("g_ldx2")
        self.pe_h = {0: b.sem("g_peh0"), 1: b.sem("g_peh1")}
        self.dv_h = {0: b.sem("g_dvh0"), 1: b.sem("g_dvh1")}
        self.ac_h = {0: b.sem("g_ach0"), 1: b.sem("g_ach1")}
        self.xb = b.sem("g_xb")
        self.sct = b.sem("g_sct")
        self.yini2 = b.sem("g_yini2")
        self.yini3 = b.sem("g_yini3")
        self.yini4 = b.sem("g_yini4")
        self.st_out = b.sem("g_stout")
        self.st_out2 = b.sem("g_stout2")
        self.yini = b.sem("g_yini")
        self.pe = b.sem("g_pe")
        self.dv = b.sem("g_dv")
        self.dv2 = b.sem("g_dv2")
        self.ac = b.sem("g_ac")

    def emit(self, hln_in, xres, eatt_in, sall_in, gidx_in, qb_in, w, cds,
             y_dram, xnext_out):
        b, s, p = self.b, self.s, self.p
        self._sct_mark = {}
        b.dma("sync", s["ones_row"][:, :], cds["ones_row"][:, :], self.ldwh)
        b.dma("sync", s["ident_f"][:, :], cds["ident"][:, :], self.ldwh)
        b.run("vector", "memset", s["ones5"][0:1, :], 1.0, sem=self.dv)
        b.dma("sync", s["sallT"][:, :], sall_in[:, :], self.ldwh)
        b.dma("sync", s["gidx"][:, :], gidx_in[:, :], self.ldwh)
        b.dma("gpsimd", s["bo_row"][:, :], w["bo_row"][:, :], self.ldw)
        b.dma("gpsimd", s["bo_col"][:, :],
              w["bo_row"].rearrange("o (dc p) -> p (o dc)", p=128), self.ldw,
              allow_slow_non_contiguous=True)
        b.dma("gpsimd", s["lng"][:, :], w["lng_row"][:, :], self.ldw)
        b.dma("gpsimd", s["lnb"][:, :], w["lnb_row"][:, :], self.ldw)
        b.wait("vector", self.ldwh)
        b.run("vector", "reduce_sum", s["rS64"][:, 0:1], s["sallT"][:, :], axis=AX.X,
              sem=self.dv)
        b.wait("vector", self.dv)
        b.run("vector", "reciprocal", out=s["rS64"][:, 0:1], in_=s["rS64"][:, 0:1],
              sem=self.dv)
        b.wait("tensor", self.dv)
        b.run("tensor", "transpose", p["bc"][0:1, 0:64], s["rS64"][0:64, 0:1],
              s["ident_f"][0:64, 0:64], sem=self.pe)
        b.wait("vector", self.pe)
        b.run("vector", "tensor_copy", out=s["rSrow"][0:1, :], in_=p["bc"][0:1, 0:64],
              sem=self.dv)
        b.wait("tensor", self.ldw)
        for src, dst in (("lng", "Gt"), ("lnb", "Bt")):
            for hf in range(2):
                b.wait("tensor", self.dv)
                b.run("tensor", "matmul", p["bc"][:, 0:512], s["ones_row"][0:1, :],
                      s[src][0:1, hf * 512:(hf + 1) * 512], start=True, stop=True,
                      sem=self.pe)
                b.wait("vector", self.pe)
                b.run("vector", "tensor_copy", out=s[dst][:, hf * 512:(hf + 1) * 512],
                      in_=p["bc"][:, 0:512], sem=self.dv)
        _yq = (("sync", self.yini), ("scalar", self.yini2),
               ("gpsimd", self.yini4))
        for sl in range(8):
            eng, sem = _yq[sl % 3]
            b.dma(eng, y_dram[sl * 1024:(sl + 1) * 1024, :],
                  hln_in[sl * 1024:(sl + 1) * 1024, :], sem)
        def head_chain(h):
            par = h % 2
            po_ = par * 1024
            pe_, dv_ = self.pe_h[par], self.dv_h[par]
            ldxp = self.ldx if par == 0 else self.ldx2
            wof = s["wof"][:, par * 1024:(par + 1) * 1024]
            eatt_h = s["eatt_h"][0:1, po_:po_ + 1024]
            p_hf = s["p_hf"][0:1, po_:po_ + 1024]
            vT = s["vT"][0:64, po_ // 2:po_ // 2 + 512]
            avT = s["avT"][0:64, po_ // 2:po_ // 2 + 512]
            t0 = s["t0"][0:64, par * 128:(par + 1) * 128]
            aoT = s["aoT"][:, par * 4096:(par + 1) * 4096]
            bankA = p["ao"] if par == 0 else p["ao2"]
            bankB = p["bc"] if par == 0 else p["ao3"]
            bcb = p["bcB"] if par == 0 else p["bcB2"]
            hpar = par * 4096
            st_ = []

            def S(f):
                st_.append(f)

            def s_loads():
                b.wait("sync", dv_)
                b.dma("sync", eatt_h, eatt_in[h:h + 1, :], ldxp)
                b.wait("sync", pe_)
                b.dma("sync", vT, qb_in[h * 64:(h + 1) * 64, :], ldxp)
                b.wait("gpsimd", pe_)
                b.wait("gpsimd", self.ldw)
                b.dma("gpsimd", wof[0:64, :], w["Wof"][h * 64:(h + 1) * 64, :], self.ldw)
                b.wait("vector", ldxp)
                b.wait("vector", dv_)
                b.run("vector", "tensor_tensor",
                      out=p_hf.rearrange("o (ij c) -> o ij c", ij=4),
                      in0=eatt_h.rearrange("o (ij c) -> o ij c", ij=4),
                      in1=s["rSrow"][0:1, h * 4:(h + 1) * 4]
                          .rearrange("o (ij u) -> o ij u", u=1).broadcast_to([1, 4, 256]),
                      op=ALU.mult, sem=dv_)
            S(s_loads)
            for pg in range(NPG):
                for i in range(2):
                    def s_av(pg=pg, i=i):
                        b.wait("tensor", dv_)
                        b.run("tensor", "matmul", bcb[0:64, 0:128], s["ones_row"][0:1, 0:64],
                              p_hf[0:1, (i * 2 + 0) * 256 + pg * 128: (i * 2 + 0) * 256 + (pg + 1) * 128],
                              start=True, stop=True)
                        b.run("tensor", "matmul", bcb[0:64, 128:256], s["ones_row"][0:1, 0:64],
                              p_hf[0:1, (i * 2 + 1) * 256 + pg * 128: (i * 2 + 1) * 256 + (pg + 1) * 128],
                              start=True, stop=True, sem=pe_)
                        b.wait("vector", pe_)
                        b.run("vector", "scalar_tensor_tensor", out=t0,
                              in0=bcb[0:64, 0:128], scalar=1.0,
                              in1=vT[0:64, pg * 256: pg * 256 + 128],
                              op0=ALU.mult, op1=ALU.mult, sem=dv_)
                        b.wait("vector", dv_)
                        b.run("vector", "scalar_tensor_tensor",
                              out=avT[0:64, pg * 256 + i * 128: pg * 256 + (i + 1) * 128],
                              in0=bcb[0:64, 128:256], scalar=1.0,
                              in1=vT[0:64, pg * 256 + 128: pg * 256 + 256],
                              op0=ALU.mult, op1=ALU.mult, sem=dv_)
                        b.wait("vector", dv_)
                        b.run("vector", "tensor_tensor",
                              out=avT[0:64, pg * 256 + i * 128: pg * 256 + (i + 1) * 128],
                              in0=avT[0:64, pg * 256 + i * 128: pg * 256 + (i + 1) * 128],
                              in1=t0, op=ALU.add, sem=dv_)
                    S(s_av)

            def s_aowait():
                b.wait("tensor", self.ldw)
                self._pe0[par] = pe_.n
                self._evm[par] = {}
            S(s_aowait)
            for dc in range(NDC):
                def s_ao(dc=dc):
                    bank = bankA if dc % 2 == 0 else bankB
                    if dc == 0:
                        b.wait("tensor", dv_)
                    elif dc >= 2:
                        b.wait("tensor", dv_, self._evm[par][dc - 2])
                    b.run("tensor", "matmul", bank[:, :], wof[0:64, dc * 128:(dc + 1) * 128],
                          avT, start=True, stop=True, sem=pe_)
                    b.wait("vector", pe_, self._pe0[par] + dc + 1)
                    if dc == 0:
                        b.wait("vector", pe_)
                        b.wait("vector", self.pe)
                    b.run("vector", "tensor_scalar", out=aoT[:, dc * 512:(dc + 1) * 512],
                          in0=bank[:, :], scalar1=s["bo_col"][:, dc:dc + 1], scalar2=None,
                          op0=ALU.add, sem=dv_)
                    self._evm[par][dc] = dv_.n
                S(s_ao)
            for t in range(4):
                def s_tg(t=t):
                    if t == 0:
                        b.wait("tensor", dv_)
                        if h >= 2:
                            b.wait("tensor", self.sct, self._sct_mark[h - 2])
                    for g in range(2):
                        srcs = [aoT[:, (g * 4 + i) * 512 + t * 128:
                                    (g * 4 + i) * 512 + (t + 1) * 128]
                                for i in range(4)]
                        dst = s["aoS"][:, hpar + t * 1024 + g * 512:
                                       hpar + t * 1024 + (g + 1) * 512]
                        emit_tgroup(self, srcs, dst)
                S(s_tg)

            def s_scatter():
                b.wait("gpsimd", self.dv)
                b.wait("gpsimd", self.dv2)
                b.wait("gpsimd", self.yini)
                b.wait("gpsimd", self.yini2)
                b.wait("gpsimd", self.yini3)
                b.wait("gpsimd", self.yini4)
                b.wait("gpsimd", self.sct)
                for t in range(4):
                    gg = h * 4 + t
                    b.op("gpsimd", lambda E, s=s, y_dram=y_dram, t=t, gg=gg, hpar=hpar,
                         sem=self.sct:
                         E.indirect_dma_start(
                             out=y_dram,
                             out_offset=bass.IndirectOffsetOnAxis(ap=s["gidx"][:, gg:gg + 1], axis=0),
                             in_=s["aoS"][:, hpar + t * 1024: hpar + (t + 1) * 1024],
                             in_offset=None,
                             compute_op=ALU.add,
                         ).then_inc(sem.h, 16))
                    self.sct.bump(16)
                b.wait("gpsimd", self.sct)
                self._sct_mark[h] = self.sct.n
            S(s_scatter)
            return st_
        self._pe0 = {}
        self._evm = {}
        for hp in range(8):
            cA = head_chain(2 * hp)
            cB = head_chain(2 * hp + 1)
            for i in range(len(cA)):
                cA[i]()
                cB[i]()
        b.wait("sync", self.sct)
        b.wait("sync", self.yini)
        b.wait("sync", self.yini2)
        b.wait("sync", self.yini4)

        def ln_chain(lt):
            par = lt % 2
            o = par * D
            st = s["stat"][:, par * 16:(par + 1) * 16]
            stp = self.st_out if par == 0 else self.st_out2
            ldxp = self.ldx if par == 0 else self.ldx2
            dv_, ac_ = self.dv_h[par], self.ac_h[par]
            y_f = s["y_f"][:, o:o + D]
            xr_f = s["xr_f"][:, o:o + D]
            xsq = s["xsq"][:, o:o + D]
            z_f = s["z_f"][:, o:o + D]
            t2 = s["t2"][:, o:o + D]
            wpre = s["wpre"][:, o:o + D]
            xn = s["xn"][:, o:o + D]
            st_ = []

            def S(f):
                st_.append(f)

            def s_load():
                b.wait("sync", dv_)
                b.wait("sync", ac_)
                b.wait("sync", stp)
                b.dma("sync", y_f, y_dram[lt * 128:(lt + 1) * 128, :], ldxp)
                b.dma("sync", xr_f, xres[lt * 128:(lt + 1) * 128, :], ldxp)
            S(s_load)
            S(lambda: (b.wait("vector", ldxp),
                       b.run("vector", "tensor_scalar", out=xsq, in0=y_f,
                             scalar1=1.0, scalar2=0.0, op0=ALU.mult, op1=ALU.add,
                             accum_out=st[:, 0:1], sem=dv_)))
            S(lambda: (b.wait("scalar", dv_),
                       b.run("scalar", "activation", out=xsq, in_=y_f,
                             func=AF.Square, accum_out=st[:, 1:2], sem=ac_)))
            S(lambda: (b.wait("vector", ac_),
                       b.run("vector", "tensor_scalar", out=st[:, 6:7], in0=st[:, 0:1],
                             scalar1=1.0 / D, scalar2=None, op0=ALU.mult, sem=dv_)))
            S(lambda: (b.wait("vector", dv_),
                       b.run("vector", "tensor_scalar", out=st[:, 13:14], in0=st[:, 1:2],
                             scalar1=1.0 / D, scalar2=1e-5, op0=ALU.mult, op1=ALU.add,
                             sem=dv_),
                       b.run("vector", "tensor_tensor", out=st[:, 8:9], in0=st[:, 6:7],
                             in1=st[:, 6:7], op=ALU.mult, sem=dv_)))
            S(lambda: (b.wait("vector", dv_),
                       b.run("vector", "tensor_sub", out=st[:, 13:14], in0=st[:, 13:14],
                             in1=st[:, 8:9], sem=dv_)))
            S(lambda: (b.wait("scalar", dv_),
                       b.run("scalar", "activation", out=st[:, 10:11], in_=st[:, 13:14],
                             func=AF.Sqrt, sem=ac_)))
            S(lambda: (b.wait("vector", ac_),
                       b.run("vector", "reciprocal", out=st[:, 11:12], in_=st[:, 10:11],
                             sem=dv_)))
            S(lambda: (b.wait("vector", dv_),
                       emit_newton_rstd(b, dv_, st[:, 11:12], st[:, 13:14],
                                        st[:, 14:15], st[:, 15:16]),
                       b.run("vector", "tensor_scalar", out=st[:, 12:13], in0=st[:, 6:7],
                             scalar1=-1.0, scalar2=None, op0=ALU.mult, sem=dv_)))
            S(lambda: (b.wait("vector", dv_),
                       b.run("vector", "tensor_scalar", out=z_f, in0=y_f,
                             scalar1=st[:, 12:13], scalar2=st[:, 11:12],
                             op0=ALU.add, op1=ALU.mult, sem=dv_),
                       b.run("vector", "tensor_add", out=wpre, in0=s["Bt"][:, :],
                             in1=xr_f, sem=dv_)))
            S(lambda: (b.wait("vector", dv_),
                       b.run("vector", "tensor_tensor", out=t2, in0=z_f,
                             in1=s["Gt"][:, :], op=ALU.mult, sem=dv_)))
            S(lambda: (b.wait("vector", dv_),
                       b.run("vector", "tensor_add", out=xn, in0=t2, in1=wpre,
                             sem=dv_)))
            def s_store():
                b.wait("sync", dv_)
                b.dma("sync", xnext_out[lt * 128:(lt + 1) * 128, :], xn, stp)
            S(s_store)
            return st_
        for pr in range(NROW // 256):
            cA = ln_chain(2 * pr)
            cB = ln_chain(2 * pr + 1)
            for i in range(len(cA)):
                cA[i]()
                cB[i]()
        b.wait("sync", self.st_out)
        b.wait("sync", self.st_out2)


def build_p2a():
    nc = bass.Bass("TRN2", target_bir_lowering=False, debug=False)
    cds = declare_consts(nc)
    hln_in = nc.dram_tensor("hln_in", [NROW, D], F32, kind="ExternalInput").ap()
    w = {}
    for nm, shape in (("Wg", [D, 16]), ("Wqkvf", [D, 16 * 192]),
                      ("bqkv_sb", [128, 32]), ("posf", [128, 2])):
        w[nm] = nc.dram_tensor("w_" + nm, shape, F32, kind="ExternalInput").ap()
    lg_dram = nc.dram_tensor("lg_dram", [16, NROW], F32, kind="Internal").ap()
    qa_out = nc.dram_tensor("qa_out", [16 * 128, 512], F32, kind="ExternalOutput").ap()
    qb_out = nc.dram_tensor("qb_out", [16 * 64, 512], F32, kind="ExternalOutput").ap()
    eatt_out = nc.dram_tensor("eatt_out", [16, 1024], F32, kind="ExternalOutput").ap()
    slocal_out = nc.dram_tensor("slocal_out", [16, 4], F32, kind="ExternalOutput").ap()
    gidx_out = nc.dram_tensor("gidx_out", [128, 64], I32, kind="ExternalOutput").ap()
    with ExitStack() as stack:
        b = B(nc, stack)
        core = FfnACore(nc, b)
        core.emit(hln_in, w, cds, lg_dram, qa_out, qb_out,
                  eatt_out, slocal_out, gidx_out)
        b.finalize()
    return nc


def build_p2b():
    nc = bass.Bass("TRN2", target_bir_lowering=False, debug=False)
    cds = declare_consts(nc)
    hln_in = nc.dram_tensor("hln_in", [NROW, D], F32, kind="ExternalInput").ap()
    xres = nc.dram_tensor("xres", [NROW, D], F32, kind="ExternalInput").ap()
    eatt_in = nc.dram_tensor("eatt_in", [16, 1024], F32, kind="ExternalInput").ap()
    sall_in = nc.dram_tensor("sall_in", [64, 8], F32, kind="ExternalInput").ap()
    gidx_in = nc.dram_tensor("gidx_in", [128, 64], I32, kind="ExternalInput").ap()
    qb_in = nc.dram_tensor("qb_in", [16 * 64, 512], F32, kind="ExternalInput").ap()
    w = {}
    for nm, shape in (("Wof", [16 * 64, D]), ("bo_row", [1, D]),
                      ("lng_row", [1, D]), ("lnb_row", [1, D])):
        w[nm] = nc.dram_tensor("w_" + nm, shape, F32, kind="ExternalInput").ap()
    y_dram = nc.dram_tensor("y_dram", [NROW, D], F32, kind="Internal").ap()
    xnext_out = nc.dram_tensor("xnext_out", [NROW, D], F32, kind="ExternalOutput").ap()
    with ExitStack() as stack:
        b = B(nc, stack)
        core = FfnBCore(nc, b)
        core.emit(hln_in, xres, eatt_in, sall_in, gidx_in, qb_in, w, cds,
                  y_dram, xnext_out)
        b.finalize()
    return nc


def p2_weight_arrays(I):
    Wq = I["L0_ffn_Wq"]
    Wkv = I["L0_ffn_Wkv"]
    bq = I["L0_ffn_bq"]
    bkv = I["L0_ffn_bkv"]
    cols = []
    bcols = np.zeros((128, 32), np.float32)
    for h in range(16):
        cols.append(Wq[h])
        cols.append(Wkv[h])
        bcols[0:64, h * 2] = bq[h]
        bcols[64:128, h * 2] = bkv[h][0:64]
        bcols[0:64, h * 2 + 1] = bkv[h][64:128]
    posf = np.zeros((128, 2), np.float32)
    posf[:, 0] = np.arange(128)
    posf[:, 1] = np.arange(128) + 128
    return {
        "w_Wg": np.ascontiguousarray(I["L0_ffn_Wg"]).astype(np.float32),
        "w_Wqkvf": np.ascontiguousarray(np.concatenate(cols, axis=1)).astype(np.float32),
        "w_bqkv_sb": bcols, "w_posf": posf,
    }


def p2b_weight_arrays(I):
    return {
        "w_Wof": np.ascontiguousarray(I["L0_ffn_Wo"].reshape(16 * 64, D)).astype(np.float32),
        "w_bo_row": np.ascontiguousarray(I["L0_ffn_bo"]).reshape(1, D).astype(np.float32),
        "w_lng_row": np.ascontiguousarray(I["L0_ffn_ln_g"]).reshape(1, D).astype(np.float32),
        "w_lnb_row": np.ascontiguousarray(I["L0_ffn_ln_b"]).reshape(1, D).astype(np.float32),
    }


B_LOC = 4
N_CORES = 8


def _batch_shards(arr):
    return [np.ascontiguousarray(arr[c * B_LOC:(c + 1) * B_LOC].reshape(B_LOC * L, D))
            for c in range(N_CORES)]


def _unbatch(shards):
    return np.concatenate([np.asarray(s).reshape(B_LOC, L, D) for s in shards], axis=0)


def _pos_shards(full):
    return [np.ascontiguousarray(full[:, c * 256:(c + 1) * 256, :].reshape(NROW, D))
            for c in range(N_CORES)]


def _unpos(shards):
    return np.concatenate([np.asarray(s).reshape(32, 256, D) for s in shards], axis=1)


def _ffn_layer(run, p2a, p2b, w2, w2b, carr, x_res, hln):
    hln_p = _pos_shards(hln)
    xres_p = _pos_shards(x_res)
    ins = [{"hln_in": hln_p[c], **w2, **carr}
           for c in range(N_CORES)]
    ra = run(p2a, ins)
    sall = np.stack([np.asarray(ra[c]["slocal_out"]).reshape(64) for c in range(N_CORES)])
    sallT = np.ascontiguousarray(sall.T)
    ins = [{"hln_in": hln_p[c], "xres": xres_p[c],
            "eatt_in": np.ascontiguousarray(np.asarray(ra[c]["eatt_out"])),
            "sall_in": sallT,
            "gidx_in": np.ascontiguousarray(np.asarray(ra[c]["gidx_out"])),
            "qb_in": np.ascontiguousarray(np.asarray(ra[c]["qb_out"])),
            **w2b, **carr} for c in range(N_CORES)]
    rb = run(p2b, ins)
    return _unpos([np.asarray(rb[c]["xnext_out"], np.float32) for c in range(N_CORES)])


def kernel(**inputs):
    from concourse.bass_utils import run_bass_kernel_spmd
    I = inputs
    I.pop("L0_ffn_attn_mask", None)
    x = np.ascontiguousarray(np.asarray(I["L0_attn_a_hidden_states"], np.float32))
    carr = const_arrays()
    w1 = p1_weight_arrays(I)
    w2 = p2_weight_arrays(I)
    w2b = p2b_weight_arrays(I)
    w3 = p3_weight_arrays(I)
    p1 = build_p1(B_LOC)
    p2a = build_p2a()
    p2b = build_p2b()
    p3 = build_p3(B_LOC)

    def run(nc, ins):
        res = run_bass_kernel_spmd(nc, ins, core_ids=list(range(N_CORES)))
        return res.results

    ins = [{"x_in": xs, **w1, **carr} for xs in _batch_shards(x)]
    r1 = run(p1, ins)
    x1 = _unbatch([np.asarray(r1[c]["x1_out"], np.float32) for c in range(N_CORES)])
    hln = _unbatch([np.asarray(r1[c]["hln_out"], np.float32) for c in range(N_CORES)])
    x2 = _ffn_layer(run, p2a, p2b, w2, w2b, carr, x1, hln)
    ins = [{"x_in": xs, **w3, **carr} for xs in _batch_shards(x2)]
    r3 = run(p3, ins)
    x3 = _unbatch([np.asarray(r3[c]["x3_out"], np.float32) for c in range(N_CORES)])
    hln3 = _unbatch([np.asarray(r3[c]["hln_out"], np.float32) for c in range(N_CORES)])
    x4 = _ffn_layer(run, p2a, p2b, w2, w2b, carr, x3, hln3)
    return np.ascontiguousarray(x4, dtype=np.float32)


# version 12 (final)
# speedup vs baseline: 1.0167x; 1.0167x over previous
import numpy as np
import ml_dtypes
import concourse.bass as bass
import concourse.mybir as mybir
from contextlib import ExitStack

F32 = mybir.dt.float32
BF16 = mybir.dt.bfloat16
I32 = mybir.dt.int32
U32 = mybir.dt.uint32
AF = mybir.ActivationFunctionType
ALU = mybir.AluOpType
AX = mybir.AxisListType

ENGINES = ("sync", "gpsimd", "vector", "scalar", "tensor")

D = 1024
L = 2048
NLT = L // 128
NDC = D // 128
H_A = 4
DK = 256
MACRO = 128
NMAC = L // MACRO


class Sem:
    __slots__ = ("h", "n")

    def __init__(self, h):
        self.h = h
        self.n = 0

    def bump(self, k):
        self.n += k
        return self.n


class B:

    def __init__(self, nc, stack):
        self.nc = nc
        self.stack = stack
        self.qs = {n: [] for n in ENGINES}
        self._semid = 0

    def sem(self, name=None):
        self._semid += 1
        h = self.stack.enter_context(self.nc.semaphore(name or f"s{self._semid}"))
        return Sem(h)

    def op(self, eng, fn):
        self.qs[eng].append(fn)

    def run(self, eng, meth, *args, sem=None, inc=1, lowprec=False, **kw):
        if sem is not None:
            sem.bump(inc)
        nc = self.nc

        def fn(E, meth=meth, args=args, kw=kw, sem=sem, inc=inc, lowprec=lowprec):
            try:
                if lowprec:
                    with nc.allow_low_precision(reason="bf16 PE transpose"):
                        ins = getattr(E, meth)(*args, **kw)
                else:
                    ins = getattr(E, meth)(*args, **kw)
            except Exception as ex:
                raise RuntimeError(f"emit fail {meth} args={[getattr(a, 'tensor', None) and a.tensor.name for a in args]} kw={list(kw)}: {ex}") from ex
            if sem is not None:
                ins.then_inc(sem.h, inc)
        self.op(eng, fn)

    def dma(self, eng, out, in_, sem, **kw):
        sem.bump(16)

        def fn(E, out=out, in_=in_, kw=kw, sem=sem):
            E.dma_start(out=out, in_=in_, **kw).then_inc(sem.h, 16)
        self.op(eng, fn)

    def wait(self, eng, sem, val=None):
        v = sem.n if val is None else val
        self.op(eng, lambda E, v=v, sem=sem: E.wait_ge(sem.h, v))

    def finalize(self):
        blk = self.stack.enter_context(self.nc.Block())
        for name in ENGINES:
            q = self.qs[name]
            if q:
                getattr(blk, name)(lambda E, q=q: [f(E) for f in q])


def host_consts():
    ii = np.arange(MACRO)
    blk = ii // 32
    c = {}
    c["ident"] = np.eye(128, dtype=np.float32)
    c["neg_ident"] = (-np.eye(128)).astype(np.float32)
    c["ones_row"] = np.ones((1, 128), np.float32)
    c["mask_neg_bsl_bd"] = np.where((blk[:, None] == blk[None, :]) & (ii[:, None] > ii[None, :]), -1.0, 0.0).astype(np.float32)
    c["mask_neg_bsu_bd"] = np.ascontiguousarray(c["mask_neg_bsl_bd"].T)
    c["mask_neg_bsl_blk"] = np.where(blk[:, None] > blk[None, :], -1.0, 0.0).astype(np.float32)
    c["mask_neg_bsu_blk"] = np.ascontiguousarray(c["mask_neg_bsl_blk"].T)
    c["mask_triu_inc"] = (ii[:, None] <= ii[None, :]).astype(np.float32)
    hsel = np.zeros((128, 16), np.float32)
    for h in range(4):
        hsel[:, 4 * h + h] = 1.0
    c["hsel"] = hsel
    return c


def declare_consts(nc):
    t = {}
    for name, shape in [
        ("ident", [128, 128]), ("neg_ident", [128, 128]), ("ones_row", [1, 128]),
        ("mask_neg_bsl_bd", [128, 128]), ("mask_neg_bsu_bd", [128, 128]),
        ("mask_neg_bsl_blk", [128, 128]), ("mask_neg_bsu_blk", [128, 128]),
        ("mask_triu_inc", [128, 128]), ("hsel", [128, 16]),
    ]:
        t[name] = nc.dram_tensor("c_" + name, shape, F32, kind="ExternalInput").ap()
    return t


LQ = 512
NQT = LQ // 128
NQM = LQ // MACRO
NQUART = L // LQ


class AttnACore:

    def __init__(self, nc, b, B_loc):
        self.nc = nc
        self.b = b
        self.B_loc = B_loc
        s = self.s = {}
        e = b.stack.enter_context
        s["poolQK"] = e(nc.sbuf_tensor("poolQK", [128, 2 * NDC * LQ], F32))
        s["qT"] = s["poolQK"][:, 0:NDC * LQ]
        s["knT"] = s["poolQK"][:, NDC * LQ:2 * NDC * LQ]
        s["wo"] = s["poolQK"][:, 0:NDC * D]
        s["xlnT"] = e(nc.sbuf_tensor("xlnT", [128, NDC * LQ], F32))
        s["vpos"] = e(nc.sbuf_tensor("vpos", [128, NQM * D], F32))
        s["knpos"] = e(nc.sbuf_tensor("knpos", [128, NQM * D], F32))
        s["S_l"] = e(nc.sbuf_tensor("S_l", [128, H_A * 512], F32))
        s["S_g"] = e(nc.sbuf_tensor("S_g", [128, H_A * 512], F32))
        s["wring"] = e(nc.sbuf_tensor("wring", [128, 2 * 1024], F32))
        s["wcat"] = e(nc.sbuf_tensor("wcat", [128, NDC * 68], F32))
        s["conv"] = e(nc.sbuf_tensor("conv", [128, 3 * NDC * 4], F32))
        s["gbias"] = e(nc.sbuf_tensor("gbias", [128, 1], F32))
        s["ccar"] = e(nc.sbuf_tensor("ccar", [128, 3 * NDC * 3], F32))
        s["ident_f"] = e(nc.sbuf_tensor("identf", [128, 128], F32))
        s["neg_ident"] = e(nc.sbuf_tensor("nidentf", [128, 128], F32))
        s["ones_row"] = e(nc.sbuf_tensor("onesr", [1, 128], F32))
        s["ones_col"] = e(nc.sbuf_tensor("onesc", [128, 1], F32))
        for m in ("mask_neg_bsl_bd", "mask_neg_bsu_bd", "mask_neg_bsl_blk",
                  "mask_neg_bsu_blk", "mask_triu_inc"):
            s[m] = e(nc.sbuf_tensor(m, [128, 128], F32))
        s["hsel"] = e(nc.sbuf_tensor("hsel", [128, 16], F32))
        s["xrow"] = e(nc.sbuf_tensor("xrow", [128, 2 * 1024], F32))
        s["lnout"] = e(nc.sbuf_tensor("lnout", [128, 2 * 1024], F32))
        s["x1_sb"] = e(nc.sbuf_tensor("x1_sb", [128, 2 * 1024], F32))
        s["xsq"] = e(nc.sbuf_tensor("xsq", [128, 2 * 1024], F32))
        s["stat"] = e(nc.sbuf_tensor("stat", [128, 32], F32))
        s["o3T_t"] = e(nc.sbuf_tensor("o3T_t", [128, NDC * 128], F32))
        s["raw"] = e(nc.sbuf_tensor("raw", [128, 2 * 520], F32))
        s["acc0"] = e(nc.sbuf_tensor("acc0", [128, 512], F32))
        s["acc1"] = e(nc.sbuf_tensor("acc1", [128, 512], F32))
        s["scr512"] = e(nc.sbuf_tensor("scr512", [128, 512], F32))
        s["gates"] = e(nc.sbuf_tensor("gates", [128, LQ], F32))
        s["possrc"] = e(nc.sbuf_tensor("possrc", [128, LQ], F32))
        s["arena_a"] = e(nc.sbuf_tensor("arena_a", [128, LQ], F32))
        s["rows4"] = e(nc.sbuf_tensor("rows4", [64, LQ], F32))
        s["nscr"] = e(nc.sbuf_tensor("nscr", [64, 2 * LQ], F32))
        s["pospan"] = e(nc.sbuf_tensor("pospan", [128, NQM * 96], F32))
        s["eps"] = e(nc.sbuf_tensor("eps", [128, 2], F32))
        for nm in ("Fn", "AnT", "Te_l", "Te_g"):
            s[nm] = e(nc.sbuf_tensor(nm, [128, 128], F32))
        for nm in ("M", "MT", "N1", "N2", "N3",
                   "NT1", "NT2", "NT3", "N4", "Ra", "Rb", "Tsym_nT", "Q1pp",
                   "negLb", "negLbT", "Rt0", "Lb2", "TbTp"):
            for sfx in ("_0", "_1"):
                s[nm + sfx] = e(nc.sbuf_tensor(nm + sfx, [128, 128], F32))
        for sfx in ("_0", "_1"):
            s["Y_sb" + sfx] = e(nc.sbuf_tensor("Y_sb" + sfx, [128, 256], F32))
            s["U_sb" + sfx] = e(nc.sbuf_tensor("U_sb" + sfx, [128, 256], F32))
        s["o2"] = e(nc.sbuf_tensor("o2", [128, 256], F32))
        s["osc"] = e(nc.sbuf_tensor("osc", [128, 260], F32))
        s["normw_row"] = e(nc.sbuf_tensor("normw_row", [1, 256], F32))
        s["normw"] = e(nc.sbuf_tensor("normw", [128, 256], F32))
        self.ek = ExpKit(nc, b.stack, "a", 512)
        p = self.p = {}
        p["proj0"] = e(nc.psum_tensor("ps_proj0", [128, 512], F32))
        p["proj1"] = e(nc.psum_tensor("ps_proj1", [128, 512], F32))
        p["stat"] = e(nc.psum_tensor("ps_stat", [128, 512], F32))
        p["bc"] = e(nc.psum_tensor("ps_bc", [128, 512], F32))
        p["tr"] = p["stat"]
        p["tr2"] = p["bc"]
        self.ptr = e(nc.psum_tensor("ps_ptr", [128, 128], F32))
        self.pyu = e(nc.psum_tensor("ps_yu", [128, 512], F32))
        self.pS = {0: e(nc.psum_tensor("ps_S_l", [128, 512], F32)),
                   1: e(nc.psum_tensor("ps_S_g", [128, 512], F32))}
        self.ldw = b.sem("ldw")
        self.ldwh = b.sem("ldwh")
        self.ldx = b.sem("ldx")
        self.ldx2 = b.sem("ldx2")
        self.st_out = b.sem("st_out")
        self.st_out2 = b.sem("st_out2")
        self.pe = b.sem("pe")
        self.dv = b.sem("dv")
        self.dv2 = b.sem("dv2")
        self.ac = b.sem("ac")
        self.pe_c = {0: b.sem("pe_a"), 1: b.sem("pe_b")}
        self.dv_c = {0: b.sem("dv_a"), 1: b.sem("dv_b")}
        self.ac_c = {0: b.sem("ac_a"), 1: b.sem("ac_b")}
        self.dv_e = self.dv
        self.ac_e = self.ac

    def load_consts(self, cds):
        b, s = self.b, self.s
        b.run("vector", "memset", s["possrc"][:, :], 0.0, sem=self.dv)
        for m in ("mask_neg_bsl_bd", "mask_neg_bsu_bd", "mask_neg_bsl_blk",
                  "mask_neg_bsu_blk", "mask_triu_inc"):
            b.dma("sync", s[m][:, :], cds[m][:, :], self.ldwh)
        b.dma("sync", s["ident_f"][:, :], cds["ident"][:, :], self.ldwh)
        b.dma("sync", s["neg_ident"][:, :], cds["neg_ident"][:, :], self.ldwh)
        b.dma("sync", s["ones_row"][:, :], cds["ones_row"][:, :], self.ldwh)
        b.dma("sync", s["ones_col"][:, :], cds["ones_row"].rearrange("o p -> p o"), self.ldwh)
        b.dma("sync", s["hsel"][:, :], cds["hsel"][:, :], self.ldwh)
        b.run("vector", "memset", s["eps"][:, 0:1], 1e-5)
        b.run("vector", "memset", s["eps"][:, 1:2], 1e-12, sem=self.dv)

    def load_weights(self, w):
        b, s = self.b, self.s
        b.dma("sync", s["wcat"][:, :].rearrange("p (kc c) -> p kc c", kc=NDC),
              w["Wcat"].rearrange("(kc p) c -> p kc c", p=128), self.ldwh)
        b.dma("sync", s["conv"][:, :].rearrange("p (proj dc j) -> p proj dc j", proj=3, dc=NDC),
              w["convcat"].rearrange("(dc p) (proj j) -> p proj dc j", p=128, j=4), self.ldwh)
        b.dma("sync", s["gbias"][:, :], w["gbias"][:, :], self.ldwh)

    def normw_setup(self, w_normw):
        b, s, p = self.b, self.s, self.p
        b.dma("sync", s["normw_row"][:, :], w_normw[:, :], self.ldwh)
        b.wait("tensor", self.ldwh)
        b.run("tensor", "matmul", p["bc"][:, 0:256], s["ones_row"][0:1, :],
              s["normw_row"][0:1, :], start=True, stop=True, sem=self.pe)
        b.wait("vector", self.pe)
        b.run("vector", "tensor_copy", out=s["normw"][:, :], in_=p["bc"][:, 0:256], sem=self.dv)

    def ln_pass(self, x_dram, qt):
        b, s = self.b, self.s
        for lt in range(NQT):
            par = lt % 2
            hb = par * 1024
            o = par * 1024
            st = s["stat"][:, par * 16:(par + 1) * 16]
            b.wait("sync", self.dv)
            b.wait("sync", self.ac)
            ldxp = self.ldx if par == 0 else self.ldx2
            b.dma("sync", s["xrow"][:, hb:hb + 1024],
                  x_dram[qt * LQ + lt * 128: qt * LQ + (lt + 1) * 128, :], ldxp)
            xf = s["xrow"][:, hb:hb + 1024]
            b.wait("vector", ldxp)
            b.run("vector", "tensor_scalar", sem=self.dv,
                  out=s["xsq"][:, o:o + 1024], in0=xf, scalar1=1.0, scalar2=0.0,
                  op0=ALU.mult, op1=ALU.add, accum_out=st[:, 0:1])
            b.wait("scalar", ldxp)
            b.wait("scalar", self.dv)
            b.run("scalar", "activation", sem=self.ac,
                  out=s["xsq"][:, o:o + 1024], in_=xf, func=AF.Square, accum_out=st[:, 1:2])
            b.wait("vector", self.ac)
            b.run("vector", "tensor_scalar", out=st[:, 2:3], in0=st[:, 0:1],
                  scalar1=1.0 / D, scalar2=None, op0=ALU.mult, sem=self.dv)
            b.run("vector", "tensor_scalar", out=st[:, 12:13], in0=st[:, 1:2],
                  scalar1=1.0 / D, scalar2=1e-5, op0=ALU.mult, op1=ALU.add, sem=self.dv)
            b.wait("vector", self.dv)
            b.run("vector", "tensor_tensor", out=st[:, 4:5], in0=st[:, 2:3],
                  in1=st[:, 2:3], op=ALU.mult, sem=self.dv)
            b.wait("vector", self.dv)
            b.run("vector", "tensor_sub", out=st[:, 12:13], in0=st[:, 12:13],
                  in1=st[:, 4:5], sem=self.dv)
            b.wait("scalar", self.dv)
            b.run("scalar", "activation", out=st[:, 6:7], in_=st[:, 12:13],
                  func=AF.Sqrt, sem=self.ac)
            b.wait("vector", self.ac)
            b.run("vector", "reciprocal", out=st[:, 7:8], in_=st[:, 6:7], sem=self.dv)
            b.wait("vector", self.dv)
            emit_newton_rstd(b, self.dv, st[:, 7:8], st[:, 12:13],
                             st[:, 13:14], st[:, 14:15])
            b.run("vector", "tensor_scalar", out=st[:, 15:16], in0=st[:, 2:3],
                  scalar1=-1.0, scalar2=None, op0=ALU.mult, sem=self.dv)
            b.wait("vector", self.dv)
            b.wait("vector", self.pe)
            b.wait("vector", self.st_out)
            b.wait("vector", self.st_out2)
            b.run("vector", "tensor_scalar", sem=self.dv,
                  out=s["lnout"][:, o:o + 1024], in0=xf,
                  scalar1=st[:, 15:16], scalar2=st[:, 7:8],
                  op0=ALU.add, op1=ALU.mult)
            b.wait("tensor", self.dv)
            for g in range(2):
                srcs = [s["lnout"][:, o + (g * 4 + i) * 128: o + (g * 4 + i + 1) * 128]
                        for i in range(4)]
                dst = s["xlnT"][:, :].rearrange("p (kc l) -> p kc l", kc=NDC)[
                    :, g * 4:(g + 1) * 4, lt * 128:(lt + 1) * 128]
                emit_tgroup(self, srcs, dst)

    def projections(self, wdram, qt):
        b, s, p = self.b, self.s, self.p
        b.wait("tensor", self.ldwh)
        b.wait("vector", self.pe)
        rawcopy_ac = {}
        taps_dv = {}
        sig_ac = {}
        nblk = 0
        for pi, (proj, wnm) in enumerate((("q", "Wq"), ("k", "Wk"), ("v", "Wv"))):
            for oc in range(NDC):
                k = nblk
                half = k % 2
                wr = s["wring"][:, half * 1024:(half + 1) * 1024]
                if k >= 2:
                    b.wait("gpsimd", self.ac, rawcopy_ac[k - 2])
                b.wait("gpsimd", self.ldw)
                b.dma("gpsimd", wr.rearrange("p (kc c) -> p kc c", kc=NDC),
                      wdram[wnm][:, oc * 128:(oc + 1) * 128].rearrange("(kc p) c -> p kc c", p=128),
                      self.ldw)
                b.wait("tensor", self.ldw)
                nblk += 1
                ps = p["proj0"] if k % 2 == 0 else p["proj1"]
                if k >= 2:
                    b.wait("tensor", self.ac, rawcopy_ac[k - 2])
                if k == 0:
                    b.wait("tensor", self.dv)
                    b.wait("tensor", self.dv2)
                for kc in range(NDC):
                    b.run("tensor", "matmul", ps[:, :],
                          wr[:, kc * 128:(kc + 1) * 128],
                          s["xlnT"][:, kc * LQ: (kc + 1) * LQ],
                          start=(kc == 0), stop=(kc == NDC - 1),
                          sem=(self.pe if kc == NDC - 1 else None))
                raw = s["raw"][:, (k % 2) * 520:(k % 2) * 520 + 520]
                cslot = s["ccar"][:, (pi * NDC + oc) * 3: (pi * NDC + oc) * 3 + 3]
                b.wait("scalar", self.pe)
                if k >= 2:
                    b.wait("scalar", self.dv, taps_dv[k - 2])
                if qt == 0:
                    b.run("scalar", "memzero", raw[:, 0:4], sem=self.ac)
                else:
                    b.run("scalar", "copy", out=raw[:, 1:4], in_=cslot, sem=self.ac)
                b.wait("scalar", self.ac)
                b.run("scalar", "copy", out=raw[:, 4:516], in_=ps[:, :], sem=self.ac)
                b.wait("scalar", self.ac)
                b.run("scalar", "copy", out=cslot, in_=raw[:, 513:516], sem=self.ac)
                rawcopy_ac[k] = self.ac.n
                b.wait("vector", self.ac, rawcopy_ac[k])
                cc = lambda j: s["conv"][:, (pi * NDC + oc) * 4 + j: (pi * NDC + oc) * 4 + j + 1]
                b.run("vector", "tensor_scalar", out=s["acc0"][:, :],
                      in0=raw[:, 1:513], scalar1=cc(0), scalar2=None, op0=ALU.mult,
                      sem=self.dv)
                b.wait("vector", self.dv)
                b.run("vector", "scalar_tensor_tensor", out=s["acc1"][:, :],
                      in0=raw[:, 2:514], scalar=cc(1), in1=s["acc0"][:, :],
                      op0=ALU.mult, op1=ALU.add, sem=self.dv)
                b.wait("vector", self.dv)
                b.run("vector", "scalar_tensor_tensor", out=s["acc0"][:, :],
                      in0=raw[:, 3:515], scalar=cc(2), in1=s["acc1"][:, :],
                      op0=ALU.mult, op1=ALU.add, sem=self.dv)
                b.wait("vector", self.dv)
                b.run("vector", "scalar_tensor_tensor", out=s["acc1"][:, :],
                      in0=raw[:, 4:516], scalar=cc(3), in1=s["acc0"][:, :],
                      op0=ALU.mult, op1=ALU.add, sem=self.dv)
                taps_dv[k] = self.dv.n
                b.wait("scalar", self.dv, taps_dv[k])
                b.run("scalar", "activation", out=s["scr512"][:, :], in_=s["acc1"][:, :],
                      func=AF.Sigmoid, sem=self.ac)
                sig_ac[k] = self.ac.n
                b.wait("vector", self.ac, sig_ac[k])
                if proj == "q":
                    dst = s["qT"][:, oc * LQ:(oc + 1) * LQ]
                    b.run("vector", "tensor_tensor", out=dst, in0=s["acc1"][:, :],
                          in1=s["scr512"][:, :], op=ALU.mult, sem=self.dv)
                elif proj == "k":
                    dst = s["knT"][:, oc * LQ:(oc + 1) * LQ]
                    b.run("vector", "tensor_tensor", out=dst, in0=s["acc1"][:, :],
                          in1=s["scr512"][:, :], op=ALU.mult, sem=self.dv)
                else:
                    b.run("vector", "tensor_tensor", out=s["scr512"][:, :],
                          in0=s["acc1"][:, :], in1=s["scr512"][:, :], op=ALU.mult,
                          sem=self.dv)
                    b.wait("tensor", self.dv)
                    for g in range(1):
                        srcs = [s["scr512"][:, i * 128:(i + 1) * 128] for i in range(4)]
                        dst3 = s["vpos"][:, :].rearrange("p (mm d) -> p mm d", mm=NQM)[
                            :, :, oc * 128:(oc + 1) * 128]
                        emit_tgroup(self, srcs, dst3)

    def gates_mm(self, qt):
        b, s, p = self.b, self.s, self.p
        b.wait("tensor", self.ac)
        b.wait("tensor", self.dv)
        b.wait("tensor", self.dv2)
        for kc in range(NDC):
            b.run("tensor", "matmul", p["stat"][0:68, :],
                  s["wcat"][:, kc * 68:(kc + 1) * 68],
                  s["xlnT"][:, kc * LQ:(kc + 1) * LQ],
                  start=(kc == 0), stop=(kc == NDC - 1),
                  sem=(self.pe if kc == NDC - 1 else None))
        b.wait("vector", self.pe)
        b.run("vector", "tensor_scalar", out=s["gates"][0:68, :],
              in0=p["stat"][0:68, :], scalar1=s["gbias"][0:68, 0:1], scalar2=None,
              op0=ALU.add, sem=self.dv)

    def norm_pass(self, qt):
        b, s, p = self.b, self.s, self.p
        b.wait("scalar", self.dv)
        b.wait("tensor", self.dv)
        for t, base in (("qT", 0), ("knT", 32)):
            for oc in range(NDC):
                b.wait("scalar", self.pe)
                b.run("scalar", "activation", out=s["scr512"][:, :],
                      in_=s[t][:, oc * LQ:(oc + 1) * LQ],
                      func=AF.Square, sem=self.ac)
                b.wait("tensor", self.ac)
                if oc == 0:
                    b.wait("tensor", self.dv)
                    b.wait("tensor", self.dv2)
                h4 = 4 * (oc // 2)
                b.run("tensor", "matmul", p["stat"][0:4, :],
                      s["hsel"][:, h4:h4 + 4], s["scr512"][:, :],
                      start=(oc == 0), stop=(oc == NDC - 1), sem=self.pe)
            b.wait("vector", self.pe)
            b.run("vector", "tensor_copy",
                  out=s["rows4"][base:base + 4, :],
                  in_=p["stat"][0:4, :], sem=self.dv)
        for base in (0, 32):
            b.wait("vector", self.dv)
            b.run("vector", "tensor_scalar", out=s["nscr"][base:base + 4, 0:LQ],
                  in0=s["rows4"][base:base + 4, :], scalar1=1e-12, scalar2=None,
                  op0=ALU.add, sem=self.dv)
            b.wait("scalar", self.dv)
            b.run("scalar", "activation", out=s["rows4"][base:base + 4, :],
                  in_=s["nscr"][base:base + 4, 0:LQ], func=AF.Sqrt, sem=self.ac)
            b.wait("vector", self.ac)
            b.run("vector", "reciprocal", out=s["rows4"][base:base + 4, :],
                  in_=s["rows4"][base:base + 4, :], sem=self.dv)
            b.wait("vector", self.dv)
            emit_newton_rstd(b, self.dv, s["rows4"][base:base + 4, :],
                             s["nscr"][base:base + 4, 0:LQ],
                             s["nscr"][base:base + 4, LQ:2 * LQ],
                             s["nscr"][base:base + 4, LQ:2 * LQ])
        b.run("vector", "tensor_copy", out=s["possrc"][0:4, :], in_=s["rows4"][0:4, :],
              sem=self.dv)
        b.wait("tensor", self.dv)
        for mq in range(NQM):
            for g in range(2):
                srcs = [s["knT"][:, (g * 4 + i) * LQ + mq * 128:
                                 (g * 4 + i) * LQ + (mq + 1) * 128]
                        for i in range(4)]
                dst = s["knpos"][:, mq * 1024 + g * 512: mq * 1024 + (g + 1) * 512]
                emit_tgroup(self, srcs, dst)
        for mq in range(NQM):
            b.wait("tensor", self.dv)
            b.run("tensor", "transpose", self.ptr[:, 0:4],
                  s["rows4"][32:36, mq * 128:(mq + 1) * 128], s["ident_f"][32:36, 32:36],
                  sem=self.pe)
            b.wait("vector", self.pe)
            b.run("vector", "tensor_copy", out=s["pospan"][:, mq * 96 + 80: mq * 96 + 84],
                  in_=self.ptr[:, 0:4], sem=self.dv)
        b.wait("vector", self.dv)
        b.wait("vector", self.dv2)
        for mq in range(NQM):
            for h in range(H_A):
                b.run("vector", "tensor_scalar", sem=self.dv,
                      out=s["knpos"][:, mq * D + h * 256: mq * D + (h + 1) * 256],
                      in0=s["knpos"][:, mq * D + h * 256: mq * D + (h + 1) * 256],
                      scalar1=self.pos_rk(mq, h), scalar2=None, op0=ALU.mult)
        b.wait("tensor", self.dv)
        for mq in range(NQM):
            for g in range(2):
                srcs = [s["knpos"][:, mq * 1024 + (g * 4 + i) * 128:
                                   mq * 1024 + (g * 4 + i + 1) * 128]
                        for i in range(4)]
                dst = s["knT"][:, :].rearrange("p (oc l) -> p oc l", oc=NDC)[
                    :, g * 4:(g + 1) * 4, mq * 128:(mq + 1) * 128]
                emit_tgroup(self, srcs, dst)
        b.wait("tensor", self.dv)
        b.wait("tensor", self.dv2)

    def gates_rows(self, qt):
        b, s, p = self.b, self.s, self.p
        A = s["arena_a"]
        R = s["rows4"]
        G = s["gates"]
        N = s["nscr"]
        b.wait("scalar", self.dv)
        b.run("scalar", "activation", out=G[0:8, :], in_=G[0:8, :], func=AF.Sigmoid,
              sem=self.ac)
        b.wait("scalar", self.ac)
        b.run("scalar", "activation", out=N[0:8, 0:LQ], in_=G[0:8, :], func=AF.Sqrt,
              sem=self.ac)
        b.wait("vector", self.ac)
        b.run("vector", "reciprocal", out=N[0:8, 0:LQ], in_=N[0:8, 0:LQ], sem=self.dv)
        b.wait("vector", self.dv)
        emit_newton_rstd(b, self.dv, N[0:8, 0:LQ], G[0:8, :],
                         N[0:8, LQ:2 * LQ], N[0:8, LQ:2 * LQ])
        b.run("vector", "tensor_tensor", out=N[0:8, LQ:2 * LQ], in0=G[0:8, :],
              in1=N[0:8, 0:LQ], op=ALU.mult, sem=self.dv)
        b.wait("scalar", self.dv)
        b.run("scalar", "copy", out=s["possrc"][64:72, :], in_=N[0:8, LQ:2 * LQ],
              sem=self.ac)
        b.run("scalar", "copy", out=N[0:4, 0:LQ], in_=G[32:36, :])
        b.run("scalar", "copy", out=N[32:36, 0:LQ], in_=G[64:68, :], sem=self.ac)
        b.wait("vector", self.ac)
        self.ek.emit(b, self.dv, A[0:4, :], N[0:4, 0:LQ], LQ, npart=4)
        self.ek.emit(b, self.dv, s["nscr"][32:36, LQ:2 * LQ], N[32:36, 0:LQ], LQ,
                     npart=4, base=32)
        b.wait("scalar", self.dv)
        b.run("scalar", "copy", out=R[0:4, :], in_=s["nscr"][32:36, LQ:2 * LQ],
              sem=self.ac)
        b.wait("vector", self.ac)
        b.run("vector", "scalar_tensor_tensor", out=G[0:4, :], in0=A[0:4, :],
              scalar=1.0, in1=R[0:4, :], op0=ALU.add, op1=ALU.add, sem=self.dv)
        b.wait("vector", self.dv)
        b.run("vector", "reciprocal", out=G[0:4, :], in_=G[0:4, :], sem=self.dv)
        b.wait("vector", self.dv)
        b.run("vector", "tensor_tensor", out=A[0:4, :], in0=A[0:4, :],
              in1=G[0:4, :], op=ALU.mult)
        b.run("vector", "tensor_tensor", out=R[0:4, :], in0=R[0:4, :],
              in1=G[0:4, :], op=ALU.mult, sem=self.dv)
        for mq in range(NQM):
            base = mq * 96
            sl = slice(mq * 128, (mq + 1) * 128)
            b.wait("tensor", self.dv)
            b.run("tensor", "transpose", self.ptr[:, 0:72],
                  s["possrc"][0:72, sl], s["ident_f"][0:72, 0:72])
            b.run("tensor", "transpose", self.ptr[:, 72:76],
                  G[0:4, sl], s["ident_f"][0:4, 0:4])
            b.run("tensor", "transpose", self.ptr[:, 76:80],
                  A[0:4, sl], s["ident_f"][0:4, 0:4])
            b.run("tensor", "transpose", self.ptr[:, 80:84],
                  R[0:4, sl], s["ident_f"][0:4, 0:4], sem=self.pe)
            b.wait("vector", self.pe)
            b.run("vector", "tensor_copy", out=s["pospan"][:, base:base + 72],
                  in_=self.ptr[:, 0:72], sem=self.dv)
            b.wait("vector", self.dv)
            b.run("vector", "tensor_copy", out=s["pospan"][:, base + 32:base + 36],
                  in_=self.ptr[:, 72:76], sem=self.dv)
            b.run("vector", "tensor_copy", out=s["pospan"][:, base + 72:base + 76],
                  in_=self.ptr[:, 76:80], sem=self.dv)
            b.run("vector", "tensor_copy", out=s["pospan"][:, base + 76:base + 80],
                  in_=self.ptr[:, 80:84], sem=self.dv)
            b.wait("vector", self.dv)
            b.run("vector", "tensor_tensor", out=s["pospan"][:, base + 84:base + 88],
                  in0=s["pospan"][:, base + 72:base + 76],
                  in1=s["pospan"][:, base + 64:base + 68], op=ALU.mult, sem=self.dv)
            b.run("vector", "tensor_tensor", out=s["pospan"][:, base + 88:base + 92],
                  in0=s["pospan"][:, base + 76:base + 80],
                  in1=s["pospan"][:, base + 68:base + 72], op=ALU.mult, sem=self.dv)

    def pos_rq(self, mq, h):
        return self.s["pospan"][:, mq * 96 + h: mq * 96 + h + 1]

    def pos_wr(self, mq, h):
        return self.s["pospan"][:, mq * 96 + 32 + h: mq * 96 + 32 + h + 1]

    def pos_sqb(self, mq, scan, h):
        c = mq * 96 + 64 + scan * 4 + h
        return self.s["pospan"][:, c: c + 1]

    def pos_rk(self, mq, h):
        return self.s["pospan"][:, mq * 96 + 80 + h: mq * 96 + 80 + h + 1]

    def pos_wsb(self, mq, scan, h):
        c = mq * 96 + 84 + scan * 4 + h
        return self.s["pospan"][:, c: c + 1]

    def delta_pass(self, qt):
        b, s, p = self.b, self.s, self.p
        kT = lambda oc, m: s["knT"][:, oc * LQ + m * 128: oc * LQ + (m + 1) * 128]
        qTs = lambda oc, m: s["qT"][:, oc * LQ + m * 128: oc * LQ + (m + 1) * 128]
        if qt == 0:
            b.wait("vector", self.ac)
            b.run("vector", "memset", s["S_l"][:, :], 0.0)
            b.run("vector", "memset", s["S_g"][:, :], 0.0, sem=self.dv)
        for h in range(H_A):
            oc0, oc1 = 2 * h, 2 * h + 1
            Sh = {0: s["S_l"][:, h * 512:(h + 1) * 512],
                  1: s["S_g"][:, h * 512:(h + 1) * 512]}
            for m in range(NQM):
                pw = p["proj0"]
                pa = p["bc"]
                po = p["proj1"]
                b.wait("tensor", self.ac)
                b.wait("tensor", self.dv)
                b.wait("tensor", self.dv2)
                b.wait("tensor", self.ac_c[1])
                b.wait("tensor", self.dv_c[1])
                b.run("tensor", "matmul", pa[:, 0:128], kT(oc0, m), kT(oc0, m), start=True, stop=False)
                b.run("tensor", "matmul", pa[:, 0:128], kT(oc1, m), kT(oc1, m), start=False, stop=True)
                b.run("tensor", "matmul", pa[:, 128:256], kT(oc0, m), qTs(oc0, m), start=True, stop=False)
                b.run("tensor", "matmul", pa[:, 128:256], kT(oc1, m), qTs(oc1, m), start=False, stop=True, sem=self.pe)
                b.wait("scalar", self.pe)
                b.run("scalar", "copy", out=s["Fn"][:, :], in_=pa[:, 0:128], sem=self.ac)
                b.wait("vector", self.pe)
                b.wait("vector", self.ac)
                b.run("vector", "tensor_tensor", out=s["AnT"][:, :],
                      in0=s["mask_triu_inc"][:, :], in1=pa[:, 128:256],
                      op=ALU.mult, sem=self.dv)
                def chain(sc):
                    sfx = "_0" if sc == 0 else "_1"
                    T = lambda nm: s[nm + sfx][:, :]
                    Te = s["Te_l"] if sc == 0 else s["Te_g"]
                    ptrb = self.ptr[:, :] if sc == 0 else pa[:, 256:384]
                    pwb = p["proj0"] if sc == 0 else p["proj1"]
                    sqb = self.pos_sqb(m, sc, h)
                    wsb = self.pos_wsb(m, sc, h)
                    pe_, dv_, ac_ = self.pe_c[sc], self.dv_c[sc], self.ac_c[sc]
                    st_ = []

                    def S(f):
                        st_.append(f)
                    S(lambda: (b.wait("tensor", self.dv), b.wait("tensor", self.ac),
                               b.wait("tensor", self.dv2),
                               (b.wait("tensor", self.dv_e) if sc == 1 else None),
                               b.wait("vector", self.ac), b.wait("vector", self.pe),
                               b.wait("scalar", self.dv)))
                    S(lambda: (b.wait("vector", pe_),
                               b.run("vector", "scalar_tensor_tensor", out=T("Ra"),
                                     in0=s["Fn"][:, :], scalar=sqb,
                                     in1=s["mask_neg_bsl_bd"][:, :], op0=ALU.mult,
                                     op1=ALU.mult, sem=dv_)))
                    S(lambda: (b.wait("tensor", dv_), b.wait("tensor", ac_),
                               b.wait("tensor", dv_),
                               b.run("tensor", "transpose", ptrb, T("Ra"),
                                     s["ident_f"][:, :], sem=pe_)))
                    S(lambda: (b.wait("vector", pe_),
                               b.run("vector", "tensor_scalar", out=T("MT"),
                                     in0=ptrb, scalar1=sqb, scalar2=None,
                                     op0=ALU.mult, sem=dv_)))
                    S(lambda: (b.wait("tensor", dv_),
                               b.run("tensor", "transpose", ptrb, T("MT"),
                                     s["ident_f"][:, :], sem=pe_)))
                    S(lambda: (b.wait("vector", pe_),
                               b.run("vector", "tensor_copy", out=T("M"),
                                     in_=ptrb, sem=dv_)))
                    sq = [("N1", "MT", "M"), ("NT1", "M", "MT"),
                          ("N2", "NT1", "N1"), ("NT2", "N1", "NT1"),
                          ("N3", "NT2", "N2"), ("NT3", "N2", "NT2"),
                          ("N4", "NT3", "N3")]
                    first_sq = [True]
                    for dst, lh, rh in sq:
                        def mk(dst=dst, lh=lh, rh=rh):
                            b.wait("tensor", dv_)
                            b.wait("tensor", ac_)
                            if sc == 1 and first_sq[0]:
                                b.wait("tensor", self.dv_e)
                                first_sq[0] = False
                            b.run("tensor", "matmul", pwb[:, 0:128], T(lh), T(rh),
                                  start=True, stop=True, sem=pe_)
                            if dst in ("N1", "N2", "N3"):
                                b.wait("vector", pe_)
                                b.run("vector", "tensor_copy", out=T(dst),
                                      in_=pwb[:, 0:128], sem=dv_)
                            else:
                                b.wait("scalar", pe_)
                                b.run("scalar", "copy", out=T(dst),
                                      in_=pwb[:, 0:128], sem=ac_)
                        S(mk)
                    S(lambda: (b.wait("vector", dv_), b.wait("vector", ac_),
                               b.run("vector", "tensor_tensor", out=T("Ra"),
                                     in0=T("MT"), in1=s["ident_f"][:, :],
                                     op=ALU.add, sem=dv_)))
                    cur, nxt = "Ra", "Rb"
                    for nk in ("N1", "N2", "N3", "N4"):
                        def mk(nk=nk, cur=cur, nxt=nxt):
                            b.wait("tensor", dv_)
                            b.wait("tensor", ac_)
                            b.run("tensor", "matmul", pwb[:, 128:256], T(nk), T(cur),
                                  start=True, stop=True, sem=pe_)
                            b.wait("vector", pe_)
                            b.run("vector", "scalar_tensor_tensor", out=T(nxt),
                                  in0=pwb[:, 128:256], scalar=1.0, in1=T(cur),
                                  op0=ALU.mult, op1=ALU.add, sem=dv_)
                        S(mk)
                        cur, nxt = nxt, cur
                    TsymT = cur
                    S(lambda: (b.wait("tensor", dv_),
                               b.run("tensor", "transpose", ptrb, T(TsymT),
                                     s["ident_f"][:, :], sem=pe_)))
                    S(lambda: (b.wait("scalar", pe_),
                               b.run("scalar", "copy", out=T("Tsym_nT"), in_=ptrb,
                                     sem=ac_),
                               b.wait("vector", ac_), b.wait("vector", dv_),
                               b.run("vector", "tensor_scalar", out=T("Q1pp"),
                                     in0=T(TsymT), scalar1=sqb, scalar2=None,
                                     op0=ALU.mult, sem=dv_)))
                    S(lambda: (b.wait("tensor", dv_),
                               b.run("tensor", "matmul", pwb[:, 256:384], T("Q1pp"),
                                     s["Fn"][:, :], start=True, stop=True,
                                     sem=pe_)))
                    S(lambda: (b.wait("vector", pe_),
                               b.run("vector", "scalar_tensor_tensor", out=T("negLb"),
                                     in0=pwb[:, 256:384], scalar=wsb,
                                     in1=s["mask_neg_bsl_blk"][:, :], op0=ALU.mult,
                                     op1=ALU.mult, sem=dv_)))
                    S(lambda: (b.wait("tensor", dv_),
                               b.run("tensor", "transpose", ptrb, T("negLb"),
                                     s["ident_f"][:, :], sem=pe_)))
                    S(lambda: (b.wait("scalar", pe_),
                               b.run("scalar", "copy", out=T("negLbT"), in_=ptrb,
                                     sem=ac_),
                               b.wait("vector", ac_),
                               b.run("vector", "tensor_tensor", out=T("Rt0"),
                                     in0=T("negLbT"), in1=s["ident_f"][:, :],
                                     op=ALU.add, sem=dv_)))
                    S(lambda: (b.wait("tensor", dv_),
                               b.run("tensor", "matmul", pwb[:, 0:128], T("negLbT"),
                                     T("negLb"), start=True, stop=True, sem=pe_)))
                    S(lambda: (b.wait("scalar", pe_),
                               b.run("scalar", "copy", out=T("Lb2"), in_=pwb[:, 0:128],
                                     sem=ac_)))
                    S(lambda: (b.wait("tensor", ac_),
                               b.run("tensor", "matmul", pwb[:, 128:256],
                                     s["ident_f"][:, :], T("Rt0"), start=True, stop=False),
                               b.run("tensor", "matmul", pwb[:, 128:256], T("Lb2"),
                                     T("Rt0"), start=False, stop=True, sem=pe_)))
                    S(lambda: (b.wait("vector", pe_),
                               b.run("vector", "tensor_scalar", out=T("TbTp"),
                                     in0=pwb[:, 128:256], scalar1=wsb, scalar2=None,
                                     op0=ALU.mult, sem=dv_)))
                    S(lambda: (b.wait("tensor", dv_),
                               b.run("tensor", "matmul", pwb[:, 256:384],
                                     T("Tsym_nT"), T("TbTp"), start=True, stop=True,
                                     sem=pe_)))
                    S(lambda: (b.wait("vector", pe_),
                               b.run("vector", "tensor_scalar", out=Te[:, :],
                                     in0=pwb[:, 256:384], scalar1=sqb, scalar2=None,
                                     op0=ALU.mult, sem=dv_)))
                    return st_
                cA = chain(0)
                cB = chain(1)
                for i in range(len(cA)):
                    cA[i]()
                    cB[i]()
                for scq in (0, 1):
                    b.wait("tensor", self.dv_c[scq])
                    b.wait("tensor", self.ac_c[scq])
                    b.wait("tensor", self.pe_c[scq])
                vsl = s["vpos"][:, m * D + h * 256: m * D + (h + 1) * 256]

                def chain2(sc):
                    pe_, dv_, ac_ = self.pe_c[sc], self.dv_c[sc], self.ac_c[sc]
                    S_sb = Sh[sc]
                    Te = s["Te_l"] if sc == 0 else s["Te_g"]
                    yb = self.pyu if sc == 0 else pa
                    Ysb = s["Y_sb_0"] if sc == 0 else s["Y_sb_1"]
                    Usb = s["U_sb_0"] if sc == 0 else s["U_sb_1"]
                    st_ = []

                    def S(f):
                        st_.append(f)
                    S(lambda: (b.wait("tensor", self.dv), b.wait("tensor", self.ac),
                               b.wait("tensor", dv_), b.wait("tensor", ac_),
                               b.run("tensor", "matmul", yb[:, 0:256], kT(oc0, m),
                                     S_sb[:, 0:256], start=True, stop=False),
                               b.run("tensor", "matmul", yb[:, 0:256], kT(oc1, m),
                                     S_sb[:, 256:512], start=False, stop=True, sem=pe_)))
                    S(lambda: (b.wait("vector", pe_),
                               b.run("vector", "scalar_tensor_tensor", out=Ysb[:, :],
                                     in0=yb[:, 0:256], scalar=-1.0, in1=vsl,
                                     op0=ALU.mult, op1=ALU.add, sem=dv_)))
                    S(lambda: (b.wait("tensor", dv_),
                               b.run("tensor", "matmul", yb[:, 256:512], Te[:, :],
                                     Ysb[:, :], start=True, stop=True, sem=pe_)))
                    S(lambda: (b.wait("scalar", pe_),
                               b.run("scalar", "copy", out=Usb[:, :],
                                     in_=yb[:, 256:512], sem=ac_)))
                    return st_
                c2A = chain2(0)
                c2B = chain2(1)
                for i in range(len(c2A)):
                    c2A[i]()
                    c2B[i]()
                b.wait("tensor", self.ac_c[0])
                b.wait("tensor", self.ac_c[1])
                b.wait("tensor", self.dv_e)
                for sc in (0, 1):
                    S_sb = Sh[sc]
                    Usb = s["U_sb_0"] if sc == 0 else s["U_sb_1"]
                    b.run("tensor", "matmul", po[:, 0:256], qTs(oc0, m), S_sb[:, 0:256],
                          start=(sc == 0), stop=False)
                    b.run("tensor", "matmul", po[:, 0:256], qTs(oc1, m), S_sb[:, 256:512],
                          start=False, stop=False)
                    b.run("tensor", "matmul", po[:, 0:256], s["AnT"][:, :], Usb[:, :],
                          start=False, stop=(sc == 1), sem=(self.pe if sc == 1 else None))
                for sc in (0, 1):
                    S_sb = Sh[sc]
                    Usb = s["U_sb_0"] if sc == 0 else s["U_sb_1"]
                    kp0 = s["knpos"][:, m * D + h * 256: m * D + h * 256 + 128]
                    kp1 = s["knpos"][:, m * D + h * 256 + 128: m * D + (h + 1) * 256]
                    b.run("tensor", "matmul", self.pS[sc][:, 0:256], kp0, Usb[:, :],
                          start=True, stop=False)
                    b.run("tensor", "matmul", self.pS[sc][:, 256:512], kp1, Usb[:, :],
                          start=False, stop=True, sem=self.pe)
                    b.wait("vector", self.pe)
                    b.run("vector", "scalar_tensor_tensor", out=S_sb[:, :],
                          in0=self.pS[sc][:, :], scalar=1.0, in1=S_sb[:, :],
                          op0=ALU.mult, op1=ALU.add, sem=self.dv)
                def ep_chain(m=m, h=h, vsl=vsl):
                    st_ = []

                    def S(f):
                        st_.append(f)
                    S(lambda: (b.wait("vector", self.pe),
                               b.wait("vector", self.dv_e),
                               b.run("vector", "tensor_scalar", out=s["osc"][:, 0:256],
                                     in0=vsl, scalar1=self.pos_wr(m, h), scalar2=None,
                                     op0=ALU.mult, sem=self.dv_e)))
                    S(lambda: (b.wait("vector", self.dv_e),
                               b.run("vector", "scalar_tensor_tensor", out=s["o2"][:, :],
                                     in0=po[:, 0:256], scalar=self.pos_rq(m, h),
                                     in1=s["osc"][:, 0:256], op0=ALU.mult, op1=ALU.add,
                                     sem=self.dv_e)))
                    S(lambda: (b.wait("scalar", self.dv_e),
                               b.run("scalar", "activation", out=s["osc"][:, 0:256],
                                     in_=s["o2"][:, :], func=AF.Square,
                                     accum_out=s["osc"][:, 256:257], sem=self.ac_e)))
                    S(lambda: (b.wait("vector", self.ac_e),
                               b.run("vector", "tensor_scalar", out=s["osc"][:, 257:258],
                                     in0=s["osc"][:, 256:257], scalar1=1.0 / 256,
                                     scalar2=1e-5, op0=ALU.mult, op1=ALU.add,
                                     sem=self.dv_e)))
                    S(lambda: (b.wait("scalar", self.dv_e),
                               b.run("scalar", "activation", out=s["osc"][:, 258:259],
                                     in_=s["osc"][:, 257:258], func=AF.Sqrt,
                                     sem=self.ac_e)))
                    S(lambda: (b.wait("vector", self.ac_e),
                               b.run("vector", "reciprocal", out=s["osc"][:, 259:260],
                                     in_=s["osc"][:, 258:259], sem=self.dv_e)))
                    S(lambda: (b.wait("vector", self.dv_e),
                               emit_newton_rstd(b, self.dv_e, s["osc"][:, 259:260],
                                                s["osc"][:, 257:258],
                                                s["osc"][:, 253:254], s["osc"][:, 254:255])))
                    S(lambda: b.run("vector", "scalar_tensor_tensor",
                                    out=s["knpos"][:, m * D + h * 256: m * D + (h + 1) * 256],
                                    in0=s["o2"][:, :], scalar=s["osc"][:, 259:260],
                                    in1=s["normw"][:, :], op0=ALU.mult, op1=ALU.mult,
                                    sem=self.dv_e))
                    return st_
                for f in ep_chain():
                    f()

    def wo_pass(self, wo_dram, x_dram, x1_out, hln_out, qt):
        b, s, p = self.b, self.s, self.p
        b.wait("gpsimd", self.pe)
        b.wait("gpsimd", self.dv)
        b.wait("gpsimd", self.pe_c[0])
        b.wait("gpsimd", self.pe_c[1])
        b.dma("gpsimd", s["wo"].rearrange("p (kc c) -> p kc c", kc=NDC),
              wo_dram.rearrange("(kc p) c -> p kc c", p=128), self.ldw)
        b.wait("tensor", self.ldw)
        stt_dv = {}
        st_mark = {}
        tg_dv = {}
        tg_dv2 = {}

        def o3_tgroups(lt2):
            for g in range(2):
                srcs = [s["knpos"][:, lt2 * 1024 + (g * 4 + i) * 128:
                                   lt2 * 1024 + (g * 4 + i + 1) * 128]
                        for i in range(4)]
                dst = s["o3T_t"][:, g * 512:(g + 1) * 512]
                emit_tgroup(self, srcs, dst)
            tg_dv[lt2] = self.dv.n
            tg_dv2[lt2] = self.dv2.n
        b.wait("tensor", self.dv_e)
        b.wait("tensor", self.dv)
        b.wait("tensor", self.ac)
        o3_tgroups(0)
        for lt in range(NQT):
            par = lt % 2
            o = par * 1024
            st = s["stat"][:, par * 16:(par + 1) * 16]
            stp = self.st_out if par == 0 else self.st_out2
            xb2 = par * 1024
            ldxp = self.ldx if par == 0 else self.ldx2
            if lt >= 2:
                b.wait("sync", self.dv, stt_dv[lt - 2])
            b.dma("sync", s["xrow"][:, xb2:xb2 + 1024],
                  x_dram[qt * LQ + lt * 128: qt * LQ + (lt + 1) * 128, :], ldxp)
            b.wait("tensor", ldxp)
            b.wait("tensor", self.dv, tg_dv[lt])
            b.wait("tensor", self.dv2, tg_dv2[lt])
            for half in range(2):
                ps = p["proj0"] if half == 0 else p["proj1"]
                if lt >= 1:
                    b.wait("tensor", self.dv, stt_dv[lt - 1])
                for kc in range(NDC):
                    b.run("tensor", "matmul", ps[:, :],
                          s["o3T_t"][:, kc * 128:(kc + 1) * 128],
                          s["wo"][:, kc * D + half * 512: kc * D + half * 512 + 512],
                          start=(kc == 0), stop=(kc == NDC - 1),
                          sem=(self.pe if kc == NDC - 1 else None))
            if lt + 1 < NQT:
                o3_tgroups(lt + 1)
            b.wait("vector", self.pe)
            b.wait("vector", ldxp)
            if lt >= 2:
                b.wait("vector", stp, st_mark[lt - 2])
            b.run("vector", "scalar_tensor_tensor", out=s["x1_sb"][:, o:o + 512],
                  in0=p["proj0"][:, :], scalar=1.0, in1=s["xrow"][:, xb2:xb2 + 512],
                  op0=ALU.mult, op1=ALU.add, sem=self.dv)
            b.run("vector", "scalar_tensor_tensor", out=s["x1_sb"][:, o + 512:o + 1024],
                  in0=p["proj1"][:, :], scalar=1.0, in1=s["xrow"][:, xb2 + 512:xb2 + 1024],
                  op0=ALU.mult, op1=ALU.add, sem=self.dv)
            stt_dv[lt] = self.dv.n
            b.wait("vector", self.dv, stt_dv[lt])
            b.run("vector", "tensor_scalar", out=s["xsq"][:, o:o + D],
                  in0=s["x1_sb"][:, o:o + D],
                  scalar1=1.0, scalar2=0.0, op0=ALU.mult, op1=ALU.add,
                  accum_out=st[:, 0:1], sem=self.dv)
            b.wait("scalar", self.dv)
            b.run("scalar", "activation", out=s["xsq"][:, o:o + D],
                  in_=s["x1_sb"][:, o:o + D],
                  func=AF.Square, accum_out=st[:, 3:4], sem=self.ac)
            b.wait("vector", self.ac)
            b.run("vector", "tensor_scalar", out=st[:, 6:7], in0=st[:, 0:1],
                  scalar1=1.0 / D, scalar2=None, op0=ALU.mult, sem=self.dv)
            b.run("vector", "tensor_scalar", out=st[:, 12:13], in0=st[:, 3:4],
                  scalar1=1.0 / D, scalar2=1e-5, op0=ALU.mult, op1=ALU.add, sem=self.dv)
            b.wait("vector", self.dv)
            b.run("vector", "tensor_tensor", out=st[:, 8:9], in0=st[:, 6:7],
                  in1=st[:, 6:7], op=ALU.mult, sem=self.dv)
            b.wait("vector", self.dv)
            b.run("vector", "tensor_sub", out=st[:, 12:13], in0=st[:, 12:13],
                  in1=st[:, 8:9], sem=self.dv)
            b.wait("scalar", self.dv)
            b.run("scalar", "activation", out=st[:, 10:11], in_=st[:, 12:13],
                  func=AF.Sqrt, sem=self.ac)
            b.wait("vector", self.ac)
            b.run("vector", "reciprocal", out=st[:, 11:12], in_=st[:, 10:11], sem=self.dv)
            b.wait("vector", self.dv)
            emit_newton_rstd(b, self.dv, st[:, 11:12], st[:, 12:13],
                             st[:, 13:14], st[:, 14:15])
            b.run("vector", "tensor_scalar", out=st[:, 15:16], in0=st[:, 6:7],
                  scalar1=-1.0, scalar2=None, op0=ALU.mult, sem=self.dv)
            b.wait("vector", self.dv)
            b.run("vector", "tensor_scalar", out=s["lnout"][:, o:o + 1024],
                  in0=s["x1_sb"][:, o:o + D],
                  scalar1=st[:, 15:16], scalar2=st[:, 11:12],
                  op0=ALU.add, op1=ALU.mult, sem=self.dv)
            b.wait("sync", self.dv)
            r0 = qt * LQ + lt * 128
            b.dma("sync", x1_out[r0:r0 + 128, :], s["x1_sb"][:, o:o + D], stp)
            b.dma("sync", hln_out[r0:r0 + 128, :], s["lnout"][:, o:o + 1024], stp)
            st_mark[lt] = stp.n


def build_p1(B_loc=4):
    nc = bass.Bass("TRN2", target_bir_lowering=False, debug=False)
    cds = declare_consts(nc)
    x_in = nc.dram_tensor("x_in", [B_loc * L, D], F32, kind="ExternalInput").ap()
    w = {}
    for nm, shape in (("Wq", [D, D]), ("Wk", [D, D]), ("Wv", [D, D]), ("Wo", [D, D]),
                      ("Wcat", [D, 68]), ("convcat", [D, 12]), ("gbias", [128, 1]),
                      ("normw", [1, 256])):
        w[nm] = nc.dram_tensor("w_" + nm, shape, F32, kind="ExternalInput").ap()
    x1_out = nc.dram_tensor("x1_out", [B_loc * L, D], F32, kind="ExternalOutput").ap()
    hln_out = nc.dram_tensor("hln_out", [B_loc * L, D], F32, kind="ExternalOutput").ap()
    with ExitStack() as stack:
        b = B(nc, stack)
        core = AttnACore(nc, b, B_loc)
        core.load_consts(cds)
        core.load_weights(w)
        core.normw_setup(w["normw"])
        for bb in range(B_loc):
            xsl = x_in[bb * L:(bb + 1) * L, :]
            for qt in range(NQUART):
                core.ln_pass(xsl, qt)
                core.projections(w, qt)
                core.gates_mm(qt)
                core.norm_pass(qt)
                core.gates_rows(qt)
                core.delta_pass(qt)
                core.wo_pass(w["Wo"], xsl, x1_out[bb * L:(bb + 1) * L, :],
                             hln_out[bb * L:(bb + 1) * L, :], qt)
        b.wait("sync", core.st_out)
        b.wait("sync", core.st_out2)
        b.finalize()
    return nc


def p1_weight_arrays(I):
    Wcat = np.zeros((D, 68), np.float32)
    Wcat[:, 0:8] = I["L0_attn_a_Wb"]
    Wcat[:, 32:36] = I["L0_attn_a_Wg"]
    Wcat[:, 64:68] = I["L0_attn_a_Wh"]
    gbias = np.zeros((128, 1), np.float32)
    gbias[32:36, 0] = I["L0_attn_a_bg"]
    gbias[64:68, 0] = I["L0_attn_a_bh"]
    convcat = np.concatenate([I["L0_attn_a_conv_q"], I["L0_attn_a_conv_k"],
                              I["L0_attn_a_conv_v"]], axis=1).astype(np.float32)
    return {
        "w_Wq": np.ascontiguousarray(I["L0_attn_a_Wq"]),
        "w_Wk": np.ascontiguousarray(I["L0_attn_a_Wk"]),
        "w_Wv": np.ascontiguousarray(I["L0_attn_a_Wv"]),
        "w_Wo": np.ascontiguousarray(I["L0_attn_a_Wo"]),
        "w_Wcat": Wcat, "w_convcat": convcat, "w_gbias": gbias,
        "w_normw": np.ascontiguousarray(I["L0_attn_a_norm_w"]).reshape(1, 256),
    }


def const_arrays():
    return {"c_" + k: v for k, v in host_consts().items()}


class AttnBCore:

    def __init__(self, nc, b):
        self.nc = nc
        self.b = b
        s = self.s = {}
        e = b.stack.enter_context
        s["xlnT"] = e(nc.sbuf_tensor("b_xlnT", [128, NDC * L], F32))
        s["xrow"] = e(nc.sbuf_tensor("b_xrow", [128, 2 * 1024], F32))
        s["lnout"] = e(nc.sbuf_tensor("b_lnout", [128, 1024], F32))
        s["c0"] = e(nc.sbuf_tensor("b_c0", [128, L], F32))
        s["c1"] = e(nc.sbuf_tensor("b_c1", [128, L], F32))
        s["c2"] = e(nc.sbuf_tensor("b_c2", [128, L], F32))
        s["c0h"] = e(nc.sbuf_tensor("b_c0h", [128, L], BF16))
        s["c0l"] = e(nc.sbuf_tensor("b_c0l", [128, L], BF16))
        s["c1h"] = e(nc.sbuf_tensor("b_c1h", [128, L], BF16))
        s["c1l"] = e(nc.sbuf_tensor("b_c1l", [128, L], BF16))
        s["vposT"] = e(nc.sbuf_tensor("b_vpT", [128, NLT * 130], F32))
        s["wring"] = e(nc.sbuf_tensor("b_wring", [128, NDC * 384], F32))
        s["woring"] = e(nc.sbuf_tensor("b_woring", [128, NDC * D], F32))
        s["bqkv"] = e(nc.sbuf_tensor("b_bqkv", [128, 24], F32))
        s["bo_row"] = e(nc.sbuf_tensor("b_bo", [1, D], F32))
        s["pT"] = e(nc.sbuf_tensor("b_pT", [128, 2 * 1024], F32))
        s["pv_sb"] = e(nc.sbuf_tensor("b_pv", [64, 1024], F32))
        s["rsum"] = e(nc.sbuf_tensor("b_rsum", [1, 1024], F32))
        s["obuf"] = e(nc.sbuf_tensor("b_obuf", [128, L], F32))
        s["stat"] = e(nc.sbuf_tensor("b_stat", [128, 32], F32))
        s["xsq"] = e(nc.sbuf_tensor("b_xsq", [128, D], F32))
        s["x1_sb"] = e(nc.sbuf_tensor("b_x1", [128, D], F32))
        s["ones_row"] = e(nc.sbuf_tensor("b_onesr", [1, 128], F32))
        s["ones_col"] = e(nc.sbuf_tensor("b_onesc", [128, 1], F32))
        s["ident_f"] = e(nc.sbuf_tensor("b_identf", [128, 128], F32))
        s["o_back"] = s["c0"]
        p = self.p = {}
        p["sc0"] = e(nc.psum_tensor("bp_sc0", [128, 512], F32))
        p["sc1"] = e(nc.psum_tensor("bp_sc1", [128, 512], F32))
        p["pv"] = e(nc.psum_tensor("bp_pv", [128, 1024], F32))
        p["r"] = e(nc.psum_tensor("bp_r", [128, 1024], F32))
        p["o0"] = e(nc.psum_tensor("bp_o0", [128, 512], F32))
        p["o1"] = e(nc.psum_tensor("bp_o1", [128, 512], F32))
        p["tr"] = p["o0"]
        p["tr2"] = p["o1"]
        self.ldw = b.sem("b_ldw")
        self.ldwh = b.sem("b_ldwh")
        self.ldx = b.sem("b_ldx")
        self.ldx2 = b.sem("b_ldx2")
        self.dv_l = {0: b.sem("b_dvl0"), 1: b.sem("b_dvl1")}
        self.ac_l = {0: b.sem("b_acl0"), 1: b.sem("b_acl1")}
        self.st_o = b.sem("b_sto")
        self.st_out = b.sem("b_stout")
        self.pe = b.sem("b_pe")
        self.dv = b.sem("b_dv")
        self.dv2 = b.sem("b_dv2")
        self.ac = b.sem("b_ac")

    def load_consts(self, cds, w):
        b, s = self.b, self.s
        b.dma("sync", s["ones_row"][:, :], cds["ones_row"][:, :], self.ldwh)
        b.dma("sync", s["ones_col"][:, :], cds["ones_row"].rearrange("o p -> p o"), self.ldwh)
        b.dma("sync", s["ident_f"][:, :], cds["ident"][:, :], self.ldwh)
        b.dma("sync", s["bqkv"][:, :], w["bqkv_sb"][:, :], self.ldwh)
        b.dma("gpsimd", s["bo_row"][:, :], w["bo_row"][:, :], self.ldw)
        b.run("vector", "memset",
              s["vposT"][:, :].rearrange("p (t c) -> p t c", t=NLT)[:, :, 64:65], 1.0)
        b.run("vector", "memset",
              s["vposT"][:, :].rearrange("p (t c) -> p t c", t=NLT)[:, :, 129:130], 1.0,
              sem=self.dv)
        b.wait("tensor", self.ldwh)
        b.wait("vector", self.ldwh)

    def ln_pass(self, x_dram):
        b, s = self.b, self.s

        def ln_chain(lt):
            par = lt % 2
            hb = par * 1024
            st = s["stat"][:, par * 16:(par + 1) * 16]
            dv_, ac_ = self.dv_l[par], self.ac_l[par]
            ldxp = self.ldx if par == 0 else self.ldx2
            xf = s["xrow"][:, hb:hb + 1024]
            xsq = s["xsq"][:, 0:D] if par == 0 else s["obuf"][:, 0:1024]
            lnout = s["lnout"][:, 0:1024] if par == 0 else s["obuf"][:, 1024:2048]
            st_ = []

            def S(f):
                st_.append(f)

            def s_load():
                b.wait("sync", dv_)
                b.wait("sync", ac_)
                b.wait("sync", self.dv)
                b.dma("sync", xf, x_dram[lt * 128:(lt + 1) * 128, :], ldxp)
            S(s_load)
            S(lambda: (b.wait("vector", ldxp),
                       b.wait("vector", self.st_o),
                       b.run("vector", "tensor_scalar", sem=dv_,
                             out=xsq, in0=xf, scalar1=1.0, scalar2=0.0,
                             op0=ALU.mult, op1=ALU.add, accum_out=st[:, 0:1])))
            S(lambda: (b.wait("scalar", dv_),
                       b.wait("scalar", self.st_o),
                       b.run("scalar", "activation", sem=ac_,
                             out=xsq, in_=xf, func=AF.Square, accum_out=st[:, 1:2])))
            S(lambda: (b.wait("vector", ac_),
                       b.run("vector", "tensor_scalar", out=st[:, 2:3], in0=st[:, 0:1],
                             scalar1=1.0 / D, scalar2=None, op0=ALU.mult, sem=dv_),
                       b.run("vector", "tensor_scalar", out=st[:, 12:13], in0=st[:, 1:2],
                             scalar1=1.0 / D, scalar2=1e-5, op0=ALU.mult, op1=ALU.add,
                             sem=dv_)))
            S(lambda: (b.wait("vector", dv_),
                       b.run("vector", "tensor_tensor", out=st[:, 4:5], in0=st[:, 2:3],
                             in1=st[:, 2:3], op=ALU.mult, sem=dv_)))
            S(lambda: (b.wait("vector", dv_),
                       b.run("vector", "tensor_sub", out=st[:, 12:13], in0=st[:, 12:13],
                             in1=st[:, 4:5], sem=dv_)))
            S(lambda: (b.wait("scalar", dv_),
                       b.run("scalar", "activation", out=st[:, 6:7], in_=st[:, 12:13],
                             func=AF.Sqrt, sem=ac_)))
            S(lambda: (b.wait("vector", ac_),
                       b.run("vector", "reciprocal", out=st[:, 7:8], in_=st[:, 6:7],
                             sem=dv_)))
            S(lambda: (b.wait("vector", dv_),
                       emit_newton_rstd(b, dv_, st[:, 7:8], st[:, 12:13],
                                        st[:, 13:14], st[:, 14:15]),
                       b.run("vector", "tensor_scalar", out=st[:, 15:16], in0=st[:, 2:3],
                             scalar1=-1.0, scalar2=None, op0=ALU.mult, sem=dv_)))
            S(lambda: (b.wait("vector", dv_),
                       b.wait("vector", self.pe),
                       b.wait("vector", self.st_o),
                       b.run("vector", "tensor_scalar", sem=dv_,
                             out=lnout, in0=xf,
                             scalar1=st[:, 15:16], scalar2=st[:, 7:8],
                             op0=ALU.add, op1=ALU.mult)))

            def s_tg():
                b.wait("tensor", dv_)
                for g in range(2):
                    srcs = [lnout[:, (g * 4 + i) * 128:(g * 4 + i + 1) * 128]
                            for i in range(4)]
                    dst = s["xlnT"][:, :].rearrange("p (kc l) -> p kc l", kc=NDC)[
                        :, g * 4:(g + 1) * 4, lt * 128:(lt + 1) * 128]
                    emit_tgroup(self, srcs, dst)
            S(s_tg)
            return st_
        for pr in range(NLT // 2):
            cA = ln_chain(2 * pr)
            cB = ln_chain(2 * pr + 1)
            for i in range(len(cA)):
                cA[i]()
                cB[i]()

    def qkv_pass(self, wqkv_dram, o_dram):
        b, s, p = self.b, self.s, self.p
        b.wait("tensor", self.ldw)
        for hp in range(8):
            b.wait("gpsimd", self.pe)
            b.wait("gpsimd", self.ldw)
            b.dma("gpsimd", s["wring"][:, :].rearrange("p (kc c) -> p kc c", kc=NDC),
                  wqkv_dram[:, hp * 384:(hp + 1) * 384].rearrange("(kc p) c -> p kc c", p=128),
                  self.ldw)
            b.wait("tensor", self.ldw)
            evac_dv = {}
            pe0 = self.pe.n
            nblk = 0
            for lq in range(4):
                for ch, (ps, dstn) in enumerate((("sc0", "c0"), ("sc1", "c1"), ("pv", "c2"))):
                    pst = p[ps] if ps != "pv" else p["pv"][:, 0:512]
                    if nblk == 0:
                        b.wait("tensor", self.ac)
                        b.wait("tensor", self.dv)
                    elif nblk >= 3:
                        b.wait("tensor", self.dv, evac_dv[nblk - 3])
                    for kc in range(NDC):
                        b.run("tensor", "matmul", pst[:, 0:512] if ps != "pv" else pst,
                              s["wring"][:, kc * 384 + ch * 128: kc * 384 + (ch + 1) * 128],
                              s["xlnT"][:, kc * L + lq * 512: kc * L + (lq + 1) * 512],
                              start=(kc == 0), stop=(kc == NDC - 1),
                              sem=(self.pe if kc == NDC - 1 else None))
                    b.wait("vector", self.pe, pe0 + nblk + 1)
                    b.run("vector", "tensor_scalar", sem=self.dv,
                          out=s[dstn][:, lq * 512:(lq + 1) * 512],
                          in0=pst[:, 0:512] if ps != "pv" else pst,
                          scalar1=s["bqkv"][:, hp * 3 + ch: hp * 3 + ch + 1],
                          scalar2=None, op0=ALU.add)
                    evac_dv[nblk] = self.dv.n
                    nblk += 1
            b.wait("vector", self.dv)
            b.wait("vector", self.pe)
            for cn in ("c0", "c1"):
                b.run("vector", "tensor_copy", out=s[cn + "h"][:, :], in_=s[cn][:, :],
                      sem=self.dv)
            b.wait("vector", self.dv)
            for cn in ("c0", "c1"):
                b.run("vector", "scalar_tensor_tensor", out=s[cn + "l"][:, :],
                      in0=s[cn][:, :], scalar=1.0, in1=s[cn + "h"][:, :],
                      op0=ALU.mult, op1=ALU.subtract, sem=self.dv)
            b.wait("tensor", self.dv)
            self.attention(hp)
            b.wait("sync", self.dv)
            b.wait("sync", self.st_o)
            b.dma("sync", o_dram[hp * 128:(hp + 1) * 128, :], s["obuf"][:, :], self.st_o)

    def attention(self, hp):
        b, s, p = self.b, self.s, self.p
        for g in range(4):
            flip = getattr(self, "_trflip", 0)
            self._trflip = 1 - flip
            bank = p["tr"] if flip == 0 else p["tr2"]
            sem_done = self.dv if flip == 0 else self.dv2
            b.wait("tensor", sem_done)
            for i in range(4):
                b.run("tensor", "transpose", bank[:, i * 128:(i + 1) * 128],
                      s["c2"][:, (g * 4 + i) * 128:(g * 4 + i + 1) * 128],
                      s["ident_f"][:, :],
                      sem=(self.pe if i == 3 else None))
            b.wait("vector", self.pe)
            vt3 = s["vposT"][:, :].rearrange("p (t c) -> p t c", t=NLT)
            b.run("vector", "tensor_copy",
                  out=vt3[:, g * 4:(g + 1) * 4, 0:64],
                  in_=bank[:, 0:512].rearrange("p (g c) -> p g c", g=4)[:, :, 0:64])
            b.run("vector", "tensor_copy",
                  out=vt3[:, g * 4:(g + 1) * 4, 65:129],
                  in_=bank[:, 0:512].rearrange("p (g c) -> p g c", g=4)[:, :, 64:128],
                  sem=sem_done)
        for hb in (0, 64):
            hb65 = 0 if hb == 0 else 65
            b.wait("tensor", self.dv)
            b.wait("tensor", self.dv2)
            for qc in range(2):
                b.wait("tensor", self.dv)
                sc_base = self.pe.n
                ex_base = self.ac.n
                pv_base = self.dv2.n
                for kt in range(NLT + 1):
                    if kt < NLT:
                        sb0 = p["sc0"] if kt % 2 == 0 else p["o0"]
                        sb1 = p["sc1"] if kt % 2 == 0 else p["o1"]
                        if kt >= 2:
                            b.wait("tensor", self.ac, ex_base + kt - 1)
                        for ti, (lsfx, rsfx) in enumerate((("h", "h"), ("h", "l"), ("l", "h"))):
                            b.run("tensor", "matmul", sb0[:, :],
                                  s["c1" + lsfx][hb:hb + 64, kt * 128:(kt + 1) * 128],
                                  s["c0" + rsfx][hb:hb + 64, qc * 1024: qc * 1024 + 512],
                                  start=(ti == 0), stop=(ti == 2))
                        for ti, (lsfx, rsfx) in enumerate((("h", "h"), ("h", "l"), ("l", "h"))):
                            b.run("tensor", "matmul", sb1[:, :],
                                  s["c1" + lsfx][hb:hb + 64, kt * 128:(kt + 1) * 128],
                                  s["c0" + rsfx][hb:hb + 64, qc * 1024 + 512: (qc + 1) * 1024],
                                  start=(ti == 0), stop=(ti == 2),
                                  sem=(self.pe if ti == 2 else None))
                    if kt >= 1:
                        j = kt - 1
                        b.wait("tensor", self.ac, ex_base + j + 1)
                        pb = (j % 2) * 1024
                        b.run("tensor", "matmul", p["pv"][0:65, 0:512],
                              s["vposT"][:, j * 130 + hb65: j * 130 + hb65 + 65],
                              s["pT"][:, pb:pb + 512], start=(j == 0), stop=(j == NLT - 1))
                        b.run("tensor", "matmul", p["pv"][0:65, 512:1024],
                              s["vposT"][:, j * 130 + hb65: j * 130 + hb65 + 65],
                              s["pT"][:, pb + 512:pb + 1024], start=(j == 0),
                              stop=(j == NLT - 1), sem=self.dv2)
                for kt in range(NLT):
                    sb0 = p["sc0"] if kt % 2 == 0 else p["o0"]
                    sb1 = p["sc1"] if kt % 2 == 0 else p["o1"]
                    pb = (kt % 2) * 1024
                    b.wait("scalar", self.pe, sc_base + kt + 1)
                    if kt >= 2:
                        b.wait("scalar", self.dv2, pv_base + kt - 1)
                    b.run("scalar", "activation", out=s["pT"][:, pb:pb + 512],
                          in_=sb0[:, :], func=AF.Exp, scale=0.125)
                    b.run("scalar", "activation", out=s["pT"][:, pb + 512:pb + 1024],
                          in_=sb1[:, :], func=AF.Exp, scale=0.125, sem=self.ac)
                b.wait("vector", self.dv2, pv_base + NLT)
                b.run("vector", "reciprocal", out=s["rsum"][0:1, 0:512],
                      in_=p["pv"][64:65, 0:512], sem=self.dv)
                b.run("vector", "reciprocal", out=s["rsum"][0:1, 512:1024],
                      in_=p["pv"][64:65, 512:1024], sem=self.dv)
                b.wait("tensor", self.dv)
                b.run("tensor", "matmul", p["r"][0:64, 0:512], s["ones_row"][0:1, 0:64],
                      s["rsum"][0:1, 0:512], start=True, stop=True)
                b.run("tensor", "matmul", p["r"][0:64, 512:1024], s["ones_row"][0:1, 0:64],
                      s["rsum"][0:1, 512:1024], start=True, stop=True, sem=self.pe)
                b.wait("scalar", self.pe)
                b.run("scalar", "copy", out=s["pv_sb"][0:64, :], in_=p["pv"][0:64, :],
                      sem=self.ac)
                b.wait("vector", self.ac)
                b.wait("vector", self.pe)
                b.wait("vector", self.st_o)
                b.run("vector", "scalar_tensor_tensor", sem=self.dv,
                      out=s["obuf"][hb:hb + 64, qc * 1024:(qc + 1) * 1024],
                      in0=p["r"][0:64, :], scalar=1.0, in1=s["pv_sb"][0:64, :],
                      op0=ALU.mult, op1=ALU.mult)

    def wo_pass(self, wo_dram, o_dram, x_dram, x1_out, hln_out):
        b, s, p = self.b, self.s, self.p
        b.wait("gpsimd", self.pe)
        b.wait("gpsimd", self.ldw)
        b.dma("gpsimd", s["woring"][:, :].rearrange("p (kc c) -> p kc c", kc=NDC),
              wo_dram.rearrange("(kc p) c -> p kc c", p=128), self.ldw)
        b.wait("tensor", self.ldw)
        b.wait("tensor", self.st_o)
        b.wait("sync", self.st_o)
        b.wait("tensor", self.dv)
        b.wait("tensor", self.dv2)
        st = s["stat"]
        for lt in range(NLT):
            ob = (lt % 2) * 1024
            b.wait("sync", self.pe)
            b.dma("sync", s["o_back"][:, ob:ob + 1024].rearrange("p (hp c) -> p hp c", hp=8),
                  o_dram[:, lt * 128:(lt + 1) * 128].rearrange("(hp p) c -> p hp c", p=128),
                  self.ldx)
            xb2 = (lt % 2) * 1024
            b.wait("sync", self.dv)
            b.dma("sync", s["xrow"][:, xb2:xb2 + 1024],
                  x_dram[lt * 128:(lt + 1) * 128, :], self.ldx)
            b.wait("tensor", self.ldx)
            for half in range(2):
                ps = p["o0"] if half == 0 else p["o1"]
                b.wait("tensor", self.dv)
                for hp in range(NDC):
                    b.run("tensor", "matmul", ps[:, :],
                          s["o_back"][:, ob + hp * 128: ob + (hp + 1) * 128],
                          s["woring"][:, hp * D + half * 512: hp * D + half * 512 + 512],
                          start=(hp == 0), stop=False)
                b.run("tensor", "matmul", ps[:, :], s["ones_row"][0:1, 0:128],
                      s["bo_row"][0:1, half * 512:(half + 1) * 512],
                      start=False, stop=True, sem=self.pe)
            b.wait("vector", self.pe)
            b.wait("vector", self.ldx)
            b.wait("vector", self.st_out)
            b.run("vector", "scalar_tensor_tensor", out=s["x1_sb"][:, 0:512],
                  in0=p["o0"][:, :], scalar=1.0, in1=s["xrow"][:, xb2:xb2 + 512],
                  op0=ALU.mult, op1=ALU.add, sem=self.dv)
            b.run("vector", "scalar_tensor_tensor", out=s["x1_sb"][:, 512:1024],
                  in0=p["o1"][:, :], scalar=1.0, in1=s["xrow"][:, xb2 + 512:xb2 + 1024],
                  op0=ALU.mult, op1=ALU.add, sem=self.dv)
            b.wait("vector", self.dv)
            b.run("vector", "tensor_scalar", out=s["xsq"][:, 0:D], in0=s["x1_sb"][:, 0:D],
                  scalar1=1.0, scalar2=0.0, op0=ALU.mult, op1=ALU.add,
                  accum_out=st[:, 0:1], sem=self.dv)
            b.wait("scalar", self.dv)
            b.run("scalar", "activation", out=s["xsq"][:, 0:D], in_=s["x1_sb"][:, 0:D],
                  func=AF.Square, accum_out=st[:, 3:4], sem=self.ac)
            b.wait("vector", self.ac)
            b.run("vector", "tensor_scalar", out=st[:, 6:7], in0=st[:, 0:1],
                  scalar1=1.0 / D, scalar2=None, op0=ALU.mult, sem=self.dv)
            b.wait("vector", self.dv)
            b.run("vector", "tensor_scalar", out=st[:, 7:8], in0=st[:, 3:4],
                  scalar1=1.0 / D, scalar2=None, op0=ALU.mult, sem=self.dv)
            b.run("vector", "tensor_tensor", out=st[:, 8:9], in0=st[:, 6:7],
                  in1=st[:, 6:7], op=ALU.mult, sem=self.dv)
            b.wait("vector", self.dv)
            b.run("vector", "tensor_sub", out=st[:, 9:10], in0=st[:, 7:8], in1=st[:, 8:9],
                  sem=self.dv)
            b.wait("vector", self.dv)
            b.run("vector", "tensor_scalar", out=st[:, 12:13], in0=st[:, 9:10],
                  scalar1=1e-5, scalar2=None, op0=ALU.add, sem=self.dv)
            b.wait("scalar", self.dv)
            b.run("scalar", "activation", out=st[:, 10:11], in_=st[:, 12:13],
                  func=AF.Sqrt, sem=self.ac)
            b.wait("vector", self.ac)
            b.run("vector", "reciprocal", out=st[:, 11:12], in_=st[:, 10:11], sem=self.dv)
            b.wait("vector", self.dv)
            emit_newton_rstd(b, self.dv, st[:, 11:12], st[:, 12:13],
                             st[:, 13:14], st[:, 14:15])
            b.run("vector", "tensor_scalar", out=st[:, 15:16], in0=st[:, 6:7],
                  scalar1=-1.0, scalar2=None, op0=ALU.mult, sem=self.dv)
            b.wait("vector", self.dv)
            b.run("vector", "tensor_scalar", out=s["lnout"][:, :], in0=s["x1_sb"][:, 0:D],
                  scalar1=st[:, 15:16], scalar2=st[:, 11:12],
                  op0=ALU.add, op1=ALU.mult, sem=self.dv)
            b.wait("sync", self.dv)
            b.dma("sync", x1_out[lt * 128:(lt + 1) * 128, :], s["x1_sb"][:, 0:D], self.st_out)
            b.dma("sync", hln_out[lt * 128:(lt + 1) * 128, :], s["lnout"][:, :], self.st_out)


def build_p3(B_loc=4):
    nc = bass.Bass("TRN2", target_bir_lowering=False, debug=False)
    cds = declare_consts(nc)
    x_in = nc.dram_tensor("x_in", [B_loc * L, D], F32, kind="ExternalInput").ap()
    w = {}
    for nm, shape in (("wqkv", [D, 8 * 384]), ("Wo", [D, D]),
                      ("bqkv_sb", [128, 24]), ("bo_row", [1, D])):
        w[nm] = nc.dram_tensor("w_" + nm, shape, F32, kind="ExternalInput").ap()
    x3_out = nc.dram_tensor("x3_out", [B_loc * L, D], F32, kind="ExternalOutput").ap()
    hln_out = nc.dram_tensor("hln_out", [B_loc * L, D], F32, kind="ExternalOutput").ap()
    o_dram = nc.dram_tensor("o_dram", [8 * 128, L], F32, kind="Internal").ap()
    with ExitStack() as stack:
        b = B(nc, stack)
        core = AttnBCore(nc, b)
        core.load_consts(cds, w)
        for bb in range(B_loc):
            xsl = x_in[bb * L:(bb + 1) * L, :]
            core.ln_pass(xsl)
            core.qkv_pass(w["wqkv"], o_dram)
            core.wo_pass(w["Wo"], o_dram, xsl, x3_out[bb * L:(bb + 1) * L, :],
                         hln_out[bb * L:(bb + 1) * L, :])
        b.wait("sync", core.st_out)
        b.finalize()
    return nc


def p3_weight_arrays(I):
    Wq = I["L0_attn_b_W_qkv"]
    bq = I["L0_attn_b_b_qkv"]
    cols = []
    bcols = np.zeros((128, 24), np.float32)
    for hp in range(8):
        A, Bh = 2 * hp, 2 * hp + 1
        cols += [Wq[A][:, 0:64], Wq[Bh][:, 0:64],
                 Wq[A][:, 64:128], Wq[Bh][:, 64:128],
                 Wq[A][:, 128:192], Wq[Bh][:, 128:192]]
        bcols[0:64, hp * 3 + 0] = bq[A][0:64]
        bcols[64:128, hp * 3 + 0] = bq[Bh][0:64]
        bcols[0:64, hp * 3 + 1] = bq[A][64:128]
        bcols[64:128, hp * 3 + 1] = bq[Bh][64:128]
        bcols[0:64, hp * 3 + 2] = bq[A][128:192]
        bcols[64:128, hp * 3 + 2] = bq[Bh][128:192]
    wqkv = np.concatenate(cols, axis=1).astype(np.float32)
    return {
        "w_wqkv": np.ascontiguousarray(wqkv),
        "w_Wo": np.ascontiguousarray(I["L0_attn_b_W_o"]),
        "w_bqkv_sb": bcols,
        "w_bo_row": np.ascontiguousarray(I["L0_attn_b_b_o"]).reshape(1, D),
    }


NROW = 8192
NPG = 2

EXP_INV_LN2 = 1.4426950408889634
EXP_LN2_HI = 0.693359375
EXP_LN2_LO = -0.00021219444005469057
EXP_C = [1.0, 1.0, 0.5, 1.0 / 6, 1.0 / 24, 1.0 / 120, 1.0 / 720, 1.0 / 5040,
         1.0 / 40320, 1.0 / 362880]


class ExpKit:

    def __init__(self, nc, stack, prefix, W):
        e = stack.enter_context
        self.W = W
        self.r = e(nc.sbuf_tensor(prefix + "_exr", [128, W], F32))
        self.u = e(nc.sbuf_tensor(prefix + "_exu", [128, W], F32))
        self.p = e(nc.sbuf_tensor(prefix + "_exq", [128, W], F32))
        self.ki = e(nc.sbuf_tensor(prefix + "_exi", [128, W], I32))

    def emit(self, b, dv, dst, x, n, npart=128, base=0):
        r = self.r[base:base + npart, 0:n]
        u = self.u[base:base + npart, 0:n]
        p = self.p[base:base + npart, 0:n]
        ki = self.ki[base:base + npart, 0:n]

        def w():
            b.wait("vector", dv)
        b.run("vector", "tensor_scalar", out=r, in0=x, scalar1=EXP_INV_LN2,
              scalar2=8192.5, op0=ALU.mult, op1=ALU.add, sem=dv)
        w()
        b.run("vector", "tensor_scalar", out=r, in0=r, scalar1=8072.5,
              scalar2=8312.5, op0=ALU.max, op1=ALU.min, sem=dv)
        w()
        b.run("vector", "tensor_copy", out=ki, in_=r, sem=dv)
        w()
        b.run("vector", "tensor_copy", out=r, in_=ki, sem=dv)
        w()
        b.run("vector", "tensor_scalar", out=r, in0=r, scalar1=-8192.0,
              scalar2=None, op0=ALU.add, sem=dv)
        w()
        b.run("vector", "scalar_tensor_tensor", out=u, in0=r,
              scalar=-EXP_LN2_HI, in1=x, op0=ALU.mult, op1=ALU.add, sem=dv)
        w()
        b.run("vector", "scalar_tensor_tensor", out=u, in0=r,
              scalar=-EXP_LN2_LO, in1=u, op0=ALU.mult, op1=ALU.add, sem=dv)
        w()
        b.run("vector", "memset", p, EXP_C[9], sem=dv)
        w()
        for k in range(8, -1, -1):
            b.run("vector", "tensor_tensor", out=p, in0=p, in1=u,
                  op=ALU.mult, sem=dv)
            w()
            b.run("vector", "tensor_scalar", out=p, in0=p, scalar1=EXP_C[k],
                  scalar2=None, op0=ALU.add, sem=dv)
            w()
        b.run("vector", "tensor_scalar", out=ki, in0=ki, scalar1=127 - 8192,
              scalar2=8388608, op0=ALU.add, op1=ALU.mult, sem=dv)
        w()
        b.run("vector", "tensor_tensor", out=dst, in0=p,
              in1=ki.bitcast(F32), op=ALU.mult, sem=dv)
        w()


def emit_newton_rstd(b, dv, r_col, vpe_col, scr0, scr1):
    b.run("vector", "tensor_tensor", out=scr0, in0=r_col, in1=r_col,
          op=ALU.mult, sem=dv)
    b.wait("vector", dv)
    b.run("vector", "tensor_tensor", out=scr1, in0=scr0, in1=vpe_col,
          op=ALU.mult, sem=dv)
    b.wait("vector", dv)
    b.run("vector", "tensor_scalar", out=scr1, in0=scr1, scalar1=-0.5,
          scalar2=1.5, op0=ALU.mult, op1=ALU.add, sem=dv)
    b.wait("vector", dv)
    b.run("vector", "tensor_tensor", out=r_col, in0=r_col, in1=scr1,
          op=ALU.mult, sem=dv)
    b.wait("vector", dv)


def emit_tgroup(core, srcs, dst, win=128):
    b, s, p = core.b, core.s, core.p
    flip = getattr(core, "_trflip", 0)
    core._trflip = 1 - flip
    bank = p["tr"] if flip == 0 else p["tr2"]
    sem_done = core.dv if flip == 0 else core.dv2
    b.wait("tensor", sem_done)
    for i, src in enumerate(srcs):
        nin = src.shape[0]
        b.run("tensor", "transpose", bank[:, i * win:i * win + nin], src,
              core.s["ident_f"][0:nin, 0:nin],
              sem=(core.pe if i == len(srcs) - 1 else None))
    b.wait("vector", core.pe)
    n = len(srcs)
    src_ap = bank[:, 0:n * win]
    if dst.ndim == 3:
        src_ap = src_ap.rearrange("p (g c) -> p g c", g=n)
    b.run("vector", "tensor_copy", out=dst, in_=src_ap, sem=sem_done)


class FfnACore:

    def __init__(self, nc, b):
        self.nc = nc
        self.b = b
        s = self.s = {}
        e = b.stack.enter_context
        s["hlnT"] = e(nc.sbuf_tensor("f_hlnT", [128, NDC * 2048], F32))
        s["hrow"] = e(nc.sbuf_tensor("f_hrow", [128, 2 * 1024], F32))
        s["lgev"] = e(nc.sbuf_tensor("f_lgev", [16, 2 * 512], F32))
        s["L4"] = e(nc.sbuf_tensor("f_L4", [32, 16 * 256], F32))
        s["ltt"] = e(nc.sbuf_tensor("f_ltt", [128, 32 * 32], F32))
        s["expb"] = e(nc.sbuf_tensor("f_expb", [128, 512], F32))
        s["Ssum"] = e(nc.sbuf_tensor("f_Ssum", [128, 32], F32))
        s["lse"] = e(nc.sbuf_tensor("f_lse", [128, 2 * 32], F32))
        s["lscr"] = e(nc.sbuf_tensor("f_lscr", [128, 32], F32))
        s["rt"] = e(nc.sbuf_tensor("f_rt", [128, 32], F32))
        s["top8"] = e(nc.sbuf_tensor("f_top8", [128, 8], F32))
        s["idxu"] = e(nc.sbuf_tensor("f_idxu", [128, 8], U32))
        s["idxf"] = e(nc.sbuf_tensor("f_idxf", [128, 4], F32))
        s["posf"] = e(nc.sbuf_tensor("f_posf", [128, 2], F32))
        s["gofff"] = e(nc.sbuf_tensor("f_goff", [128, 2], F32))
        s["gidx"] = e(nc.sbuf_tensor("f_gidx", [128, 64], I32))
        s["ghg"] = e(nc.sbuf_tensor("f_ghg", [128, 4 * 1024], F32))
        s["hgT"] = e(nc.sbuf_tensor("f_hgT", [128, NDC * 512], F32))
        s["wg"] = e(nc.sbuf_tensor("f_wg", [128, NDC * 16], F32))
        s["wqkv"] = e(nc.sbuf_tensor("f_wqkv", [128, 2 * NDC * 192], F32))
        s["bqkv"] = e(nc.sbuf_tensor("f_bqkv", [128, 32], F32))
        s["qA"] = e(nc.sbuf_tensor("f_qA", [128, 512], F32))
        s["qB"] = e(nc.sbuf_tensor("f_qB", [64, 512], F32))
        s["kT0"] = e(nc.sbuf_tensor("f_kT0", [64, 512], F32))
        s["Z"] = e(nc.sbuf_tensor("f_Z", [64, 1024], F32))
        s["attall"] = e(nc.sbuf_tensor("f_attall", [16, 1024], F32))
        s["eattall"] = e(nc.sbuf_tensor("f_eattall", [16, 1024], F32))
        s["att_r"] = e(nc.sbuf_tensor("f_attr", [1, 1024], F32))
        s["sl"] = e(nc.sbuf_tensor("f_sl", [16, 4], F32))
        s["ones_row"] = e(nc.sbuf_tensor("f_onesr", [1, 128], F32))
        s["ones_col"] = e(nc.sbuf_tensor("f_onesc", [128, 1], F32))
        s["ident_f"] = e(nc.sbuf_tensor("f_identf", [128, 128], F32))
        self.ek = ExpKit(nc, b.stack, "f", 512)
        p = self.p = {}
        p["tr"] = e(nc.psum_tensor("fp_tr", [128, 512], F32))
        p["tr2"] = e(nc.psum_tensor("fp_tr2", [128, 512], F32))
        p["lg"] = e(nc.psum_tensor("fp_lg", [128, 512], F32))
        p["bc"] = e(nc.psum_tensor("fp_bc", [128, 512], F32))
        p["A"] = e(nc.psum_tensor("fp_A", [128, 512], F32))
        p["B"] = e(nc.psum_tensor("fp_B", [128, 512], F32))
        p["att"] = e(nc.psum_tensor("fp_att", [128, 1024], F32))
        self.ldw = b.sem("f_ldw")
        self.ldwh = b.sem("f_ldwh")
        self.ldx = b.sem("f_ldx")
        self.gat = b.sem("f_gat")
        self.st_lg = b.sem("f_stlg")
        self.st_q = b.sem("f_stq")
        self.st_out = b.sem("f_stout")
        self.pe = b.sem("f_pe")
        self.dv = b.sem("f_dv")
        self.dv2 = b.sem("f_dv2")
        self.ac = b.sem("f_ac")
        self.ldx2s = b.sem("f_ldx2")

    def emit(self, hln_in, w, cds, lg_dram, qa_out, qb_out,
             eatt_out, slocal_out, gidx_out):
        b, s, p = self.b, self.s, self.p
        b.dma("sync", s["ones_row"][:, :], cds["ones_row"][:, :], self.ldwh)
        b.dma("sync", s["ones_col"][:, :], cds["ones_row"].rearrange("o p -> p o"), self.ldwh)
        b.dma("sync", s["ident_f"][:, :], cds["ident"][:, :], self.ldwh)
        b.dma("sync", s["posf"][:, :], w["posf"][:, :], self.ldwh)
        b.dma("sync", s["wg"][:, :].rearrange("p (kc c) -> p kc c", kc=NDC),
              w["Wg"].rearrange("(kc p) c -> p kc c", p=128), self.ldwh)
        b.dma("sync", s["bqkv"][:, :], w["bqkv_sb"][:, :], self.ldwh)
        b.wait("tensor", self.ldwh)
        b.wait("vector", self.ldwh)
        for ch in range(4):
            b.wait("tensor", self.st_lg)
            for t in range(NLT):
                hb = (t % 2) * 1024
                b.wait("sync", self.pe)
                b.dma("sync", s["hrow"][:, hb:hb + 1024],
                      hln_in[ch * 2048 + t * 128: ch * 2048 + (t + 1) * 128, :],
                      self.ldx)
                b.wait("tensor", self.ldx)
                for g in range(2):
                    srcs = [s["hrow"][:, hb + (g * 4 + i) * 128: hb + (g * 4 + i + 1) * 128]
                            for i in range(4)]
                    dst = s["hlnT"][:, :].rearrange("p (kc l) -> p kc l", kc=NDC)[
                        :, g * 4:(g + 1) * 4, t * 128:(t + 1) * 128]
                    emit_tgroup(self, srcs, dst)
            for lq in range(4):
                b.wait("tensor", self.dv)
                b.wait("tensor", self.dv2)
                b.wait("tensor", self.ac)
                for kc in range(NDC):
                    b.run("tensor", "matmul", p["lg"][0:16, :],
                          s["wg"][:, kc * 16:(kc + 1) * 16],
                          s["hlnT"][:, kc * 2048 + lq * 512: kc * 2048 + (lq + 1) * 512],
                          start=(kc == 0), stop=(kc == NDC - 1),
                          sem=(self.pe if kc == NDC - 1 else None))
                lb = (lq % 2) * 512
                b.wait("scalar", self.pe)
                b.wait("scalar", self.st_lg)
                b.run("scalar", "copy", out=s["lgev"][0:16, lb:lb + 512],
                      in_=p["lg"][0:16, :], sem=self.ac)
                b.wait("sync", self.ac)
                b.dma("sync",
                      lg_dram[:, ch * 2048 + lq * 512: ch * 2048 + (lq + 1) * 512],
                      s["lgev"][0:16, lb:lb + 512], self.st_lg)
        b.wait("sync", self.st_lg)
        b.dma("sync", s["L4"][:, :].rearrange("p (h c) -> p h c", h=16),
              lg_dram.rearrange("h (t c) -> t h c", t=32), self.ldx2s)
        b.wait("tensor", self.ldx2s)
        for tg in range(8):
            srcs = [s["L4"][0:32, ((tg * 4 + i) // 2) * 256 + ((tg * 4 + i) % 2) * 128:
                            ((tg * 4 + i) // 2) * 256 + ((tg * 4 + i) % 2) * 128 + 128]
                    for i in range(4)]
            dst = s["ltt"][:, tg * 128:(tg + 1) * 128]
            emit_tgroup(self, srcs, dst, win=32)
        for pg in range(NPG):
            b.wait("vector", self.dv)
            b.wait("vector", self.dv2)
            b.run("vector", "tensor_copy",
                  out=s["expb"][:, :].rearrange("p (h c) -> p h c", h=16),
                  in_=s["ltt"][:, :].rearrange("p (h g) -> p h g", h=16)[
                      :, :, pg * 32:(pg + 1) * 32],
                  sem=self.dv)
            b.wait("vector", self.dv)
            self.ek.emit(b, self.dv, s["expb"][:, :], s["expb"][:, :], 512)
            stride = 8
            while stride >= 1:
                b.run("vector", "tensor_tensor",
                      out=s["expb"][:, :].rearrange("p (h c) -> p h c", h=16)[:, 0:stride, :],
                      in0=s["expb"][:, :].rearrange("p (h c) -> p h c", h=16)[:, 0:stride, :],
                      in1=s["expb"][:, :].rearrange("p (h c) -> p h c", h=16)[:, stride:2 * stride, :],
                      op=ALU.add, sem=self.dv)
                b.wait("vector", self.dv)
                stride //= 2
            b.run("vector", "tensor_copy", out=s["Ssum"][:, :],
                  in_=s["expb"][:, 0:32], sem=self.dv)
            b.wait("scalar", self.dv)
            b.run("scalar", "activation", out=s["lse"][:, pg * 32:(pg + 1) * 32],
                  in_=s["Ssum"][:, :], func=AF.Ln, sem=self.ac)
            b.wait("vector", self.ac)
            b.run("vector", "tensor_scalar", out=s["lscr"][:, :],
                  in0=s["lse"][:, pg * 32:(pg + 1) * 32], scalar1=-1.0,
                  scalar2=None, op0=ALU.mult, sem=self.dv)
            b.wait("vector", self.dv)
            self.ek.emit(b, self.dv, s["lscr"][:, :], s["lscr"][:, :], 32)
            b.run("vector", "tensor_tensor", out=s["lscr"][:, :],
                  in0=s["lscr"][:, :], in1=s["Ssum"][:, :], op=ALU.mult,
                  sem=self.dv)
            b.wait("vector", self.dv)
            b.run("vector", "tensor_scalar", out=s["lscr"][:, :],
                  in0=s["lscr"][:, :], scalar1=-1.0, scalar2=None,
                  op0=ALU.add, sem=self.dv)
            b.wait("vector", self.dv)
            b.run("vector", "tensor_tensor",
                  out=s["lse"][:, pg * 32:(pg + 1) * 32],
                  in0=s["lse"][:, pg * 32:(pg + 1) * 32],
                  in1=s["lscr"][:, :], op=ALU.add, sem=self.dv)
            b.wait("vector", self.dv)
        for h in range(16):
            for pg in range(NPG):
                t = h * NPG + pg
                b.wait("vector", self.dv)
                b.run("vector", "tensor_sub", out=s["rt"][:, :],
                      in0=s["ltt"][:, t * 32:(t + 1) * 32],
                      in1=s["lse"][:, pg * 32:(pg + 1) * 32], sem=self.dv)
                b.wait("vector", self.dv)
                b.run("vector", "max", s["top8"][:, :], s["rt"][:, :], sem=self.dv)
                b.wait("vector", self.dv)
                b.run("vector", "max_index", s["idxu"][:, :], s["top8"][:, :],
                      s["rt"][:, :], sem=self.dv)
                b.wait("vector", self.dv)
                b.run("vector", "tensor_copy", out=s["idxf"][:, 0:2], in_=s["idxu"][:, 0:2],
                      sem=self.dv)
                b.wait("vector", self.dv)
                b.run("vector", "tensor_tensor", out=s["idxf"][:, 2:3], in0=s["idxf"][:, 0:1],
                      in1=s["idxf"][:, 1:2], op=ALU.min)
                b.run("vector", "tensor_tensor", out=s["idxf"][:, 3:4], in0=s["idxf"][:, 0:1],
                      in1=s["idxf"][:, 1:2], op=ALU.max, sem=self.dv)
                b.wait("vector", self.dv)
                for i in range(2):
                    b.run("vector", "scalar_tensor_tensor", out=s["gofff"][:, i:i + 1],
                          in0=s["idxf"][:, 2 + i:3 + i], scalar=256.0,
                          in1=s["posf"][:, pg:pg + 1], op0=ALU.mult, op1=ALU.add,
                          sem=self.dv)
                b.wait("vector", self.dv)
                b.run("vector", "tensor_copy",
                      out=s["gidx"][:, t * 2: t * 2 + 2], in_=s["gofff"][:, 0:2],
                      sem=self.dv)
        b.wait("sync", self.dv)
        b.dma("sync", gidx_out[:, :], s["gidx"][:, :], self.st_out)
        b.wait("gpsimd", self.dv)
        for h in range(16):
            whalf = h % 2
            wr = s["wqkv"][:, whalf * NDC * 192:(whalf + 1) * NDC * 192]
            b.wait("gpsimd", self.pe)
            b.wait("gpsimd", self.ldw)
            b.dma("gpsimd", wr.rearrange("p (kc c) -> p kc c", kc=NDC),
                  w["Wqkvf"][:, h * 192:(h + 1) * 192].rearrange("(kc p) c -> p kc c", p=128),
                  self.ldw)
            b.wait("gpsimd", self.pe)
            for sl4 in range(4):
                gg = h * 4 + sl4
                b.op("gpsimd", lambda E, s=s, hln_in=hln_in, gg=gg, sl4=sl4, sem=self.gat:
                     E.indirect_dma_start(
                         out=s["ghg"][:, sl4 * 1024:(sl4 + 1) * 1024],
                         out_offset=None, in_=hln_in,
                         in_offset=bass.IndirectOffsetOnAxis(ap=s["gidx"][:, gg:gg + 1], axis=0),
                     ).then_inc(sem.h, 16))
                self.gat.bump(16)
            b.wait("tensor", self.gat)
            for sl4 in range(4):
                for g in range(2):
                    srcs = [s["ghg"][:, sl4 * 1024 + (g * 4 + i) * 128:
                                     sl4 * 1024 + (g * 4 + i + 1) * 128]
                            for i in range(4)]
                    dst = s["hgT"][:, :].rearrange("p (kc r) -> p kc r", kc=NDC)[
                        :, g * 4:(g + 1) * 4, sl4 * 128:(sl4 + 1) * 128]
                    emit_tgroup(self, srcs, dst)
            b.wait("tensor", self.ldw)
            b.wait("tensor", self.dv)
            b.wait("tensor", self.dv2)
            b.wait("tensor", self.ac)
            for kc in range(NDC):
                b.run("tensor", "matmul", p["A"][:, :],
                      wr[:, kc * 192: kc * 192 + 128],
                      s["hgT"][:, kc * 512:(kc + 1) * 512],
                      start=(kc == 0), stop=(kc == NDC - 1))
            for kc in range(NDC):
                b.run("tensor", "matmul", p["B"][0:64, :],
                      wr[:, kc * 192 + 128: kc * 192 + 192],
                      s["hgT"][:, kc * 512:(kc + 1) * 512],
                      start=(kc == 0), stop=(kc == NDC - 1),
                      sem=(self.pe if kc == NDC - 1 else None))
            b.wait("vector", self.pe)
            b.wait("vector", self.st_q)
            b.run("vector", "tensor_scalar", out=s["qA"][:, :], in0=p["A"][:, :],
                  scalar1=s["bqkv"][:, h * 2: h * 2 + 1], scalar2=None, op0=ALU.add,
                  sem=self.dv)
            b.run("vector", "tensor_scalar", out=s["qB"][0:64, :], in0=p["B"][0:64, :],
                  scalar1=s["bqkv"][0:64, h * 2 + 1: h * 2 + 2], scalar2=None, op0=ALU.add,
                  sem=self.dv)
            b.wait("sync", self.dv)
            b.dma("sync", qa_out[h * 128:(h + 1) * 128, :], s["qA"][:, :], self.st_q)
            b.dma("sync", qb_out[h * 64:(h + 1) * 64, :], s["qB"][0:64, :], self.st_q)
            b.wait("scalar", self.dv)
            b.run("scalar", "copy", out=s["kT0"][0:64, :], in_=s["qA"][64:128, :], sem=self.ac)
            b.wait("vector", self.ac)
            for i in range(2):
                for j in range(2):
                    ij = i * 2 + j
                    for pg in range(NPG):
                        b.run("vector", "tensor_tensor",
                              out=s["Z"][0:64, ij * 256 + pg * 128: ij * 256 + (pg + 1) * 128],
                              in0=s["qA"][0:64, pg * 256 + i * 128: pg * 256 + (i + 1) * 128],
                              in1=s["kT0"][0:64, pg * 256 + j * 128: pg * 256 + (j + 1) * 128],
                              op=ALU.mult,
                              sem=(self.dv if (ij == 3 and pg == 1) else None))
            b.wait("tensor", self.dv)
            b.run("tensor", "matmul", p["att"][0:1, 0:512], s["ones_col"][0:64, 0:1],
                  s["Z"][0:64, 0:512], start=True, stop=True)
            b.run("tensor", "matmul", p["att"][0:1, 512:1024], s["ones_col"][0:64, 0:1],
                  s["Z"][0:64, 512:1024], start=True, stop=True, sem=self.pe)
            b.wait("vector", self.pe)
            b.wait("vector", self.st_out)
            b.run("vector", "tensor_scalar", out=s["att_r"][0:1, :],
                  in0=p["att"][0:1, 0:1024], scalar1=0.125, scalar2=None,
                  op0=ALU.mult, sem=self.dv)
            b.wait("sync", self.dv)
            b.dma("sync", s["attall"][h:h + 1, :], s["att_r"][0:1, :], self.st_out)
        b.wait("vector", self.st_out)
        b.wait("vector", self.dv)
        self.ek.emit(b, self.dv, s["eattall"][0:16, 0:512], s["attall"][0:16, 0:512], 512, npart=16)
        self.ek.emit(b, self.dv, s["eattall"][0:16, 512:1024], s["attall"][0:16, 512:1024], 512, npart=16)
        b.wait("sync", self.dv)
        b.dma("sync", eatt_out[:, :], s["eattall"][0:16, :], self.st_out)
        b.run("vector", "reduce_sum", s["sl"][0:16, 0:4],
              s["eattall"][0:16, :].rearrange("h (ij c) -> h ij c", ij=4),
              axis=AX.X, sem=self.dv)
        b.wait("sync", self.dv)
        b.dma("sync", slocal_out[:, :], s["sl"][0:16, 0:4], self.st_out)
        b.wait("sync", self.st_out)
        b.wait("sync", self.st_q)


class FfnBCore:

    def __init__(self, nc, b):
        self.nc = nc
        self.b = b
        s = self.s = {}
        e = b.stack.enter_context
        s["sallT"] = e(nc.sbuf_tensor("g_sallT", [64, 8], F32))
        s["rS64"] = e(nc.sbuf_tensor("g_rS64", [64, 1], F32))
        s["rSrow"] = e(nc.sbuf_tensor("g_rSrow", [1, 64], F32))
        s["eatt_h"] = e(nc.sbuf_tensor("g_eatth", [1, 2 * 1024], F32))
        s["p_hf"] = e(nc.sbuf_tensor("g_ph", [1, 2 * 1024], F32))
        s["ones5"] = e(nc.sbuf_tensor("g_ones5", [1, 512], F32))
        s["vT"] = e(nc.sbuf_tensor("g_vT", [64, 2 * 512], F32))
        s["avT"] = e(nc.sbuf_tensor("g_avT", [64, 2 * 512], F32))
        s["t0"] = e(nc.sbuf_tensor("g_t0", [64, 2 * 128], F32))
        s["aoT"] = e(nc.sbuf_tensor("g_aoT", [128, 2 * NDC * 512], F32))
        s["aoS"] = e(nc.sbuf_tensor("g_aoS", [128, 2 * 4 * 1024], F32))
        s["gidx"] = e(nc.sbuf_tensor("g_gidx", [128, 64], I32))
        s["wof"] = e(nc.sbuf_tensor("g_wof", [64, 2 * 1024], F32))
        s["bo_row"] = e(nc.sbuf_tensor("g_bo", [1, D], F32))
        s["bo_col"] = e(nc.sbuf_tensor("g_bocol", [128, NDC], F32))
        s["lng"] = e(nc.sbuf_tensor("g_lng", [1, D], F32))
        s["lnb"] = e(nc.sbuf_tensor("g_lnb", [1, D], F32))
        s["Gt"] = e(nc.sbuf_tensor("g_Gt", [128, D], F32))
        s["Bt"] = e(nc.sbuf_tensor("g_Bt", [128, D], F32))
        s["y_f"] = e(nc.sbuf_tensor("g_yf", [128, 2 * D], F32))
        s["xr_f"] = e(nc.sbuf_tensor("g_xrf", [128, 2 * D], F32))
        s["z_f"] = e(nc.sbuf_tensor("g_zf", [128, 2 * D], F32))
        s["t2"] = e(nc.sbuf_tensor("g_t2", [128, 2 * D], F32))
        s["wpre"] = e(nc.sbuf_tensor("g_wpre", [128, 2 * D], F32))
        s["xn"] = e(nc.sbuf_tensor("g_xn", [128, 2 * D], F32))
        s["stat"] = e(nc.sbuf_tensor("g_stat", [128, 32], F32))
        s["xsq"] = e(nc.sbuf_tensor("g_xsq", [128, 2 * D], F32))
        s["ones_row"] = e(nc.sbuf_tensor("g_onesr", [1, 128], F32))
        s["ident_f"] = e(nc.sbuf_tensor("g_identf", [128, 128], F32))
        s["lscr"] = e(nc.sbuf_tensor("g_lscr", [128, 32], F32))
        p = self.p = {}
        p["tr"] = e(nc.psum_tensor("gp_tr", [128, 512], F32))
        p["tr2"] = e(nc.psum_tensor("gp_tr2", [128, 512], F32))
        p["bcB"] = e(nc.psum_tensor("gp_bcB", [128, 512], F32))
        p["bcB2"] = e(nc.psum_tensor("gp_bcB2", [128, 512], F32))
        p["ao"] = e(nc.psum_tensor("gp_ao", [128, 512], F32))
        p["ao2"] = e(nc.psum_tensor("gp_ao2", [128, 512], F32))
        p["ao3"] = e(nc.psum_tensor("gp_ao3", [128, 512], F32))
        p["bc"] = e(nc.psum_tensor("gp_bc", [128, 512], F32))
        self.ldw = b.sem("g_ldw")
        self.ldwh = b.sem("g_ldwh")
        self.ldx = b.sem("g_ldx")
        self.ldx2 = b.sem("g_ldx2")
        self.pe_h = {0: b.sem("g_peh0"), 1: b.sem("g_peh1")}
        self.dv_h = {0: b.sem("g_dvh0"), 1: b.sem("g_dvh1")}
        self.ac_h = {0: b.sem("g_ach0"), 1: b.sem("g_ach1")}
        self.xb = b.sem("g_xb")
        self.sct = b.sem("g_sct")
        self.yini2 = b.sem("g_yini2")
        self.yini3 = b.sem("g_yini3")
        self.yini4 = b.sem("g_yini4")
        self.st_out = b.sem("g_stout")
        self.st_out2 = b.sem("g_stout2")
        self.yini = b.sem("g_yini")
        self.pe = b.sem("g_pe")
        self.dv = b.sem("g_dv")
        self.dv2 = b.sem("g_dv2")
        self.ac = b.sem("g_ac")

    def emit(self, hln_in, xres, eatt_in, sall_in, gidx_in, qb_in, w, cds,
             y_dram, xnext_out):
        b, s, p = self.b, self.s, self.p
        self._sct_mark = {}
        b.dma("sync", s["ones_row"][:, :], cds["ones_row"][:, :], self.ldwh)
        b.dma("sync", s["ident_f"][:, :], cds["ident"][:, :], self.ldwh)
        b.run("vector", "memset", s["ones5"][0:1, :], 1.0, sem=self.dv)
        b.dma("sync", s["sallT"][:, :], sall_in[:, :], self.ldwh)
        b.dma("sync", s["gidx"][:, :], gidx_in[:, :], self.ldwh)
        b.dma("gpsimd", s["bo_row"][:, :], w["bo_row"][:, :], self.ldw)
        b.dma("gpsimd", s["bo_col"][:, :],
              w["bo_row"].rearrange("o (dc p) -> p (o dc)", p=128), self.ldw,
              allow_slow_non_contiguous=True)
        b.dma("gpsimd", s["lng"][:, :], w["lng_row"][:, :], self.ldw)
        b.dma("gpsimd", s["lnb"][:, :], w["lnb_row"][:, :], self.ldw)
        b.wait("vector", self.ldwh)
        b.run("vector", "reduce_sum", s["rS64"][:, 0:1], s["sallT"][:, :], axis=AX.X,
              sem=self.dv)
        b.wait("vector", self.dv)
        b.run("vector", "reciprocal", out=s["rS64"][:, 0:1], in_=s["rS64"][:, 0:1],
              sem=self.dv)
        b.wait("tensor", self.dv)
        b.run("tensor", "transpose", p["bc"][0:1, 0:64], s["rS64"][0:64, 0:1],
              s["ident_f"][0:64, 0:64], sem=self.pe)
        b.wait("vector", self.pe)
        b.run("vector", "tensor_copy", out=s["rSrow"][0:1, :], in_=p["bc"][0:1, 0:64],
              sem=self.dv)
        b.wait("tensor", self.ldw)
        for src, dst in (("lng", "Gt"), ("lnb", "Bt")):
            for hf in range(2):
                b.wait("tensor", self.dv)
                b.run("tensor", "matmul", p["bc"][:, 0:512], s["ones_row"][0:1, :],
                      s[src][0:1, hf * 512:(hf + 1) * 512], start=True, stop=True,
                      sem=self.pe)
                b.wait("vector", self.pe)
                b.run("vector", "tensor_copy", out=s[dst][:, hf * 512:(hf + 1) * 512],
                      in_=p["bc"][:, 0:512], sem=self.dv)
        _yq = (("sync", self.yini), ("scalar", self.yini2),
               ("gpsimd", self.yini4))
        for sl in range(8):
            eng, sem = _yq[sl % 3]
            b.dma(eng, y_dram[sl * 1024:(sl + 1) * 1024, :],
                  hln_in[sl * 1024:(sl + 1) * 1024, :], sem)
        def head_chain(h):
            par = h % 2
            po_ = par * 1024
            pe_, dv_ = self.pe_h[par], self.dv_h[par]
            ldxp = self.ldx if par == 0 else self.ldx2
            wof = s["wof"][:, par * 1024:(par + 1) * 1024]
            eatt_h = s["eatt_h"][0:1, po_:po_ + 1024]
            p_hf = s["p_hf"][0:1, po_:po_ + 1024]
            vT = s["vT"][0:64, po_ // 2:po_ // 2 + 512]
            avT = s["avT"][0:64, po_ // 2:po_ // 2 + 512]
            t0 = s["t0"][0:64, par * 128:(par + 1) * 128]
            aoT = s["aoT"][:, par * 4096:(par + 1) * 4096]
            bankA = p["ao"] if par == 0 else p["ao2"]
            bankB = p["bc"] if par == 0 else p["ao3"]
            bcb = p["bcB"] if par == 0 else p["bcB2"]
            hpar = par * 4096
            st_ = []

            def S(f):
                st_.append(f)

            def s_loads():
                b.wait("sync", dv_)
                b.dma("sync", eatt_h, eatt_in[h:h + 1, :], ldxp)
                b.wait("sync", pe_)
                b.dma("sync", vT, qb_in[h * 64:(h + 1) * 64, :], ldxp)
                b.wait("gpsimd", pe_)
                b.wait("gpsimd", self.ldw)
                b.dma("gpsimd", wof[0:64, :], w["Wof"][h * 64:(h + 1) * 64, :], self.ldw)
                b.wait("vector", ldxp)
                b.wait("vector", dv_)
                b.run("vector", "tensor_tensor",
                      out=p_hf.rearrange("o (ij c) -> o ij c", ij=4),
                      in0=eatt_h.rearrange("o (ij c) -> o ij c", ij=4),
                      in1=s["rSrow"][0:1, h * 4:(h + 1) * 4]
                          .rearrange("o (ij u) -> o ij u", u=1).broadcast_to([1, 4, 256]),
                      op=ALU.mult, sem=dv_)
            S(s_loads)
            for pg in range(NPG):
                for i in range(2):
                    def s_av(pg=pg, i=i):
                        b.wait("tensor", dv_)
                        b.run("tensor", "matmul", bcb[0:64, 0:128], s["ones_row"][0:1, 0:64],
                              p_hf[0:1, (i * 2 + 0) * 256 + pg * 128: (i * 2 + 0) * 256 + (pg + 1) * 128],
                              start=True, stop=True)
                        b.run("tensor", "matmul", bcb[0:64, 128:256], s["ones_row"][0:1, 0:64],
                              p_hf[0:1, (i * 2 + 1) * 256 + pg * 128: (i * 2 + 1) * 256 + (pg + 1) * 128],
                              start=True, stop=True, sem=pe_)
                        b.wait("vector", pe_)
                        b.run("vector", "scalar_tensor_tensor", out=t0,
                              in0=bcb[0:64, 0:128], scalar=1.0,
                              in1=vT[0:64, pg * 256: pg * 256 + 128],
                              op0=ALU.mult, op1=ALU.mult, sem=dv_)
                        b.wait("vector", dv_)
                        b.run("vector", "scalar_tensor_tensor",
                              out=avT[0:64, pg * 256 + i * 128: pg * 256 + (i + 1) * 128],
                              in0=bcb[0:64, 128:256], scalar=1.0,
                              in1=vT[0:64, pg * 256 + 128: pg * 256 + 256],
                              op0=ALU.mult, op1=ALU.mult, sem=dv_)
                        b.wait("vector", dv_)
                        b.run("vector", "tensor_tensor",
                              out=avT[0:64, pg * 256 + i * 128: pg * 256 + (i + 1) * 128],
                              in0=avT[0:64, pg * 256 + i * 128: pg * 256 + (i + 1) * 128],
                              in1=t0, op=ALU.add, sem=dv_)
                    S(s_av)

            def s_aowait():
                b.wait("tensor", self.ldw)
                self._pe0[par] = pe_.n
                self._evm[par] = {}
            S(s_aowait)
            for dc in range(NDC):
                def s_ao(dc=dc):
                    bank = bankA if dc % 2 == 0 else bankB
                    if dc == 0:
                        b.wait("tensor", dv_)
                    elif dc >= 2:
                        b.wait("tensor", dv_, self._evm[par][dc - 2])
                    b.run("tensor", "matmul", bank[:, :], wof[0:64, dc * 128:(dc + 1) * 128],
                          avT, start=True, stop=True, sem=pe_)
                    b.wait("vector", pe_, self._pe0[par] + dc + 1)
                    if dc == 0:
                        b.wait("vector", pe_)
                        b.wait("vector", self.pe)
                    b.run("vector", "tensor_scalar", out=aoT[:, dc * 512:(dc + 1) * 512],
                          in0=bank[:, :], scalar1=s["bo_col"][:, dc:dc + 1], scalar2=None,
                          op0=ALU.add, sem=dv_)
                    self._evm[par][dc] = dv_.n
                S(s_ao)
            for t in range(4):
                def s_tg(t=t):
                    if t == 0:
                        b.wait("tensor", dv_)
                        if h >= 2:
                            b.wait("tensor", self.sct, self._sct_mark[h - 2])
                    for g in range(2):
                        srcs = [aoT[:, (g * 4 + i) * 512 + t * 128:
                                    (g * 4 + i) * 512 + (t + 1) * 128]
                                for i in range(4)]
                        dst = s["aoS"][:, hpar + t * 1024 + g * 512:
                                       hpar + t * 1024 + (g + 1) * 512]
                        emit_tgroup(self, srcs, dst)
                S(s_tg)

            def s_scatter():
                b.wait("gpsimd", self.dv)
                b.wait("gpsimd", self.dv2)
                b.wait("gpsimd", self.yini)
                b.wait("gpsimd", self.yini2)
                b.wait("gpsimd", self.yini3)
                b.wait("gpsimd", self.yini4)
                b.wait("gpsimd", self.sct)
                for t in range(4):
                    gg = h * 4 + t
                    b.op("gpsimd", lambda E, s=s, y_dram=y_dram, t=t, gg=gg, hpar=hpar,
                         sem=self.sct:
                         E.indirect_dma_start(
                             out=y_dram,
                             out_offset=bass.IndirectOffsetOnAxis(ap=s["gidx"][:, gg:gg + 1], axis=0),
                             in_=s["aoS"][:, hpar + t * 1024: hpar + (t + 1) * 1024],
                             in_offset=None,
                             compute_op=ALU.add,
                         ).then_inc(sem.h, 16))
                    self.sct.bump(16)
                b.wait("gpsimd", self.sct)
                self._sct_mark[h] = self.sct.n
            S(s_scatter)
            return st_
        self._pe0 = {}
        self._evm = {}
        for hp in range(8):
            cA = head_chain(2 * hp)
            cB = head_chain(2 * hp + 1)
            for i in range(len(cA)):
                cA[i]()
                cB[i]()
        b.wait("sync", self.sct)
        b.wait("sync", self.yini)
        b.wait("sync", self.yini2)
        b.wait("sync", self.yini4)

        def ln_chain(lt):
            par = lt % 2
            o = par * D
            st = s["stat"][:, par * 16:(par + 1) * 16]
            stp = self.st_out if par == 0 else self.st_out2
            ldxp = self.ldx if par == 0 else self.ldx2
            dv_, ac_ = self.dv_h[par], self.ac_h[par]
            y_f = s["y_f"][:, o:o + D]
            xr_f = s["xr_f"][:, o:o + D]
            xsq = s["xsq"][:, o:o + D]
            z_f = s["z_f"][:, o:o + D]
            t2 = s["t2"][:, o:o + D]
            wpre = s["wpre"][:, o:o + D]
            xn = s["xn"][:, o:o + D]
            st_ = []

            def S(f):
                st_.append(f)

            def s_load():
                b.wait("sync", dv_)
                b.wait("sync", ac_)
                b.wait("sync", stp)
                b.dma("sync", y_f, y_dram[lt * 128:(lt + 1) * 128, :], ldxp)
                b.dma("sync", xr_f, xres[lt * 128:(lt + 1) * 128, :], ldxp)
            S(s_load)
            S(lambda: (b.wait("vector", ldxp),
                       b.run("vector", "tensor_scalar", out=xsq, in0=y_f,
                             scalar1=1.0, scalar2=0.0, op0=ALU.mult, op1=ALU.add,
                             accum_out=st[:, 0:1], sem=dv_)))
            S(lambda: (b.wait("scalar", dv_),
                       b.run("scalar", "activation", out=xsq, in_=y_f,
                             func=AF.Square, accum_out=st[:, 1:2], sem=ac_)))
            S(lambda: (b.wait("vector", ac_),
                       b.run("vector", "tensor_scalar", out=st[:, 6:7], in0=st[:, 0:1],
                             scalar1=1.0 / D, scalar2=None, op0=ALU.mult, sem=dv_)))
            S(lambda: (b.wait("vector", dv_),
                       b.run("vector", "tensor_scalar", out=st[:, 13:14], in0=st[:, 1:2],
                             scalar1=1.0 / D, scalar2=1e-5, op0=ALU.mult, op1=ALU.add,
                             sem=dv_),
                       b.run("vector", "tensor_tensor", out=st[:, 8:9], in0=st[:, 6:7],
                             in1=st[:, 6:7], op=ALU.mult, sem=dv_)))
            S(lambda: (b.wait("vector", dv_),
                       b.run("vector", "tensor_sub", out=st[:, 13:14], in0=st[:, 13:14],
                             in1=st[:, 8:9], sem=dv_)))
            S(lambda: (b.wait("scalar", dv_),
                       b.run("scalar", "activation", out=st[:, 10:11], in_=st[:, 13:14],
                             func=AF.Sqrt, sem=ac_)))
            S(lambda: (b.wait("vector", ac_),
                       b.run("vector", "reciprocal", out=st[:, 11:12], in_=st[:, 10:11],
                             sem=dv_)))
            S(lambda: (b.wait("vector", dv_),
                       emit_newton_rstd(b, dv_, st[:, 11:12], st[:, 13:14],
                                        st[:, 14:15], st[:, 15:16]),
                       b.run("vector", "tensor_scalar", out=st[:, 12:13], in0=st[:, 6:7],
                             scalar1=-1.0, scalar2=None, op0=ALU.mult, sem=dv_)))
            S(lambda: (b.wait("vector", dv_),
                       b.run("vector", "tensor_scalar", out=z_f, in0=y_f,
                             scalar1=st[:, 12:13], scalar2=st[:, 11:12],
                             op0=ALU.add, op1=ALU.mult, sem=dv_),
                       b.run("vector", "tensor_add", out=wpre, in0=s["Bt"][:, :],
                             in1=xr_f, sem=dv_)))
            S(lambda: (b.wait("vector", dv_),
                       b.run("vector", "tensor_tensor", out=t2, in0=z_f,
                             in1=s["Gt"][:, :], op=ALU.mult, sem=dv_)))
            S(lambda: (b.wait("vector", dv_),
                       b.run("vector", "tensor_add", out=xn, in0=t2, in1=wpre,
                             sem=dv_)))
            def s_store():
                b.wait("sync", dv_)
                b.dma("sync", xnext_out[lt * 128:(lt + 1) * 128, :], xn, stp)
            S(s_store)
            return st_
        for pr in range(NROW // 256):
            cA = ln_chain(2 * pr)
            cB = ln_chain(2 * pr + 1)
            for i in range(len(cA)):
                cA[i]()
                cB[i]()
        b.wait("sync", self.st_out)
        b.wait("sync", self.st_out2)


def build_p2a():
    nc = bass.Bass("TRN2", target_bir_lowering=False, debug=False)
    cds = declare_consts(nc)
    hln_in = nc.dram_tensor("hln_in", [NROW, D], F32, kind="ExternalInput").ap()
    w = {}
    for nm, shape in (("Wg", [D, 16]), ("Wqkvf", [D, 16 * 192]),
                      ("bqkv_sb", [128, 32]), ("posf", [128, 2])):
        w[nm] = nc.dram_tensor("w_" + nm, shape, F32, kind="ExternalInput").ap()
    lg_dram = nc.dram_tensor("lg_dram", [16, NROW], F32, kind="Internal").ap()
    qa_out = nc.dram_tensor("qa_out", [16 * 128, 512], F32, kind="ExternalOutput").ap()
    qb_out = nc.dram_tensor("qb_out", [16 * 64, 512], F32, kind="ExternalOutput").ap()
    eatt_out = nc.dram_tensor("eatt_out", [16, 1024], F32, kind="ExternalOutput").ap()
    slocal_out = nc.dram_tensor("slocal_out", [16, 4], F32, kind="ExternalOutput").ap()
    gidx_out = nc.dram_tensor("gidx_out", [128, 64], I32, kind="ExternalOutput").ap()
    with ExitStack() as stack:
        b = B(nc, stack)
        core = FfnACore(nc, b)
        core.emit(hln_in, w, cds, lg_dram, qa_out, qb_out,
                  eatt_out, slocal_out, gidx_out)
        b.finalize()
    return nc


def build_p2b():
    nc = bass.Bass("TRN2", target_bir_lowering=False, debug=False)
    cds = declare_consts(nc)
    hln_in = nc.dram_tensor("hln_in", [NROW, D], F32, kind="ExternalInput").ap()
    xres = nc.dram_tensor("xres", [NROW, D], F32, kind="ExternalInput").ap()
    eatt_in = nc.dram_tensor("eatt_in", [16, 1024], F32, kind="ExternalInput").ap()
    sall_in = nc.dram_tensor("sall_in", [64, 8], F32, kind="ExternalInput").ap()
    gidx_in = nc.dram_tensor("gidx_in", [128, 64], I32, kind="ExternalInput").ap()
    qb_in = nc.dram_tensor("qb_in", [16 * 64, 512], F32, kind="ExternalInput").ap()
    w = {}
    for nm, shape in (("Wof", [16 * 64, D]), ("bo_row", [1, D]),
                      ("lng_row", [1, D]), ("lnb_row", [1, D])):
        w[nm] = nc.dram_tensor("w_" + nm, shape, F32, kind="ExternalInput").ap()
    y_dram = nc.dram_tensor("y_dram", [NROW, D], F32, kind="Internal").ap()
    xnext_out = nc.dram_tensor("xnext_out", [NROW, D], F32, kind="ExternalOutput").ap()
    with ExitStack() as stack:
        b = B(nc, stack)
        core = FfnBCore(nc, b)
        core.emit(hln_in, xres, eatt_in, sall_in, gidx_in, qb_in, w, cds,
                  y_dram, xnext_out)
        b.finalize()
    return nc


def p2_weight_arrays(I):
    Wq = I["L0_ffn_Wq"]
    Wkv = I["L0_ffn_Wkv"]
    bq = I["L0_ffn_bq"]
    bkv = I["L0_ffn_bkv"]
    cols = []
    bcols = np.zeros((128, 32), np.float32)
    for h in range(16):
        cols.append(Wq[h])
        cols.append(Wkv[h])
        bcols[0:64, h * 2] = bq[h]
        bcols[64:128, h * 2] = bkv[h][0:64]
        bcols[0:64, h * 2 + 1] = bkv[h][64:128]
    posf = np.zeros((128, 2), np.float32)
    posf[:, 0] = np.arange(128)
    posf[:, 1] = np.arange(128) + 128
    return {
        "w_Wg": np.ascontiguousarray(I["L0_ffn_Wg"]).astype(np.float32),
        "w_Wqkvf": np.ascontiguousarray(np.concatenate(cols, axis=1)).astype(np.float32),
        "w_bqkv_sb": bcols, "w_posf": posf,
    }


def p2b_weight_arrays(I):
    return {
        "w_Wof": np.ascontiguousarray(I["L0_ffn_Wo"].reshape(16 * 64, D)).astype(np.float32),
        "w_bo_row": np.ascontiguousarray(I["L0_ffn_bo"]).reshape(1, D).astype(np.float32),
        "w_lng_row": np.ascontiguousarray(I["L0_ffn_ln_g"]).reshape(1, D).astype(np.float32),
        "w_lnb_row": np.ascontiguousarray(I["L0_ffn_ln_b"]).reshape(1, D).astype(np.float32),
    }


B_LOC = 4
N_CORES = 8


def _batch_shards(arr):
    return [np.ascontiguousarray(arr[c * B_LOC:(c + 1) * B_LOC].reshape(B_LOC * L, D))
            for c in range(N_CORES)]


def _unbatch(shards):
    return np.concatenate([np.asarray(s).reshape(B_LOC, L, D) for s in shards], axis=0)


def _pos_shards(full):
    return [np.ascontiguousarray(full[:, c * 256:(c + 1) * 256, :].reshape(NROW, D))
            for c in range(N_CORES)]


def _unpos(shards):
    return np.concatenate([np.asarray(s).reshape(32, 256, D) for s in shards], axis=1)


def _ffn_layer(run, p2a, p2b, w2, w2b, carr, x_res, hln):
    hln_p = _pos_shards(hln)
    xres_p = _pos_shards(x_res)
    ins = [{"hln_in": hln_p[c], **w2, **carr}
           for c in range(N_CORES)]
    ra = run(p2a, ins)
    sall = np.stack([np.asarray(ra[c]["slocal_out"]).reshape(64) for c in range(N_CORES)])
    sallT = np.ascontiguousarray(sall.T)
    ins = [{"hln_in": hln_p[c], "xres": xres_p[c],
            "eatt_in": np.ascontiguousarray(np.asarray(ra[c]["eatt_out"])),
            "sall_in": sallT,
            "gidx_in": np.ascontiguousarray(np.asarray(ra[c]["gidx_out"])),
            "qb_in": np.ascontiguousarray(np.asarray(ra[c]["qb_out"])),
            **w2b, **carr} for c in range(N_CORES)]
    rb = run(p2b, ins)
    return _unpos([np.asarray(rb[c]["xnext_out"], np.float32) for c in range(N_CORES)])


def kernel(**inputs):
    from concourse.bass_utils import run_bass_kernel_spmd
    I = inputs
    I.pop("L0_ffn_attn_mask", None)
    x = np.ascontiguousarray(np.asarray(I["L0_attn_a_hidden_states"], np.float32))
    carr = const_arrays()
    w1 = p1_weight_arrays(I)
    w2 = p2_weight_arrays(I)
    w2b = p2b_weight_arrays(I)
    w3 = p3_weight_arrays(I)
    p1 = build_p1(B_LOC)
    p2a = build_p2a()
    p2b = build_p2b()
    p3 = build_p3(B_LOC)

    def run(nc, ins):
        res = run_bass_kernel_spmd(nc, ins, core_ids=list(range(N_CORES)))
        return res.results

    ins = [{"x_in": xs, **w1, **carr} for xs in _batch_shards(x)]
    r1 = run(p1, ins)
    x1 = _unbatch([np.asarray(r1[c]["x1_out"], np.float32) for c in range(N_CORES)])
    hln = _unbatch([np.asarray(r1[c]["hln_out"], np.float32) for c in range(N_CORES)])
    x2 = _ffn_layer(run, p2a, p2b, w2, w2b, carr, x1, hln)
    ins = [{"x_in": xs, **w3, **carr} for xs in _batch_shards(x2)]
    r3 = run(p3, ins)
    x3 = _unbatch([np.asarray(r3[c]["x3_out"], np.float32) for c in range(N_CORES)])
    hln3 = _unbatch([np.asarray(r3[c]["hln_out"], np.float32) for c in range(N_CORES)])
    x4 = _ffn_layer(run, p2a, p2b, w2, w2b, carr, x3, hln3)
    return np.ascontiguousarray(x4, dtype=np.float32)
